# version 02
# baseline (speedup 1.0000x reference)
import numpy as np
import ml_dtypes
import concourse.bass as bass
import concourse.mybir as mybir
from concourse.bass_utils import run_bass_kernel_spmd

F32 = mybir.dt.float32
BF16 = mybir.dt.bfloat16
I32 = mybir.dt.int32
AF = mybir.ActivationFunctionType
ALU = mybir.AluOpType
AX = mybir.AxisListType

P = 128
T = 2048
CC = 8
C = 1024
NHEAD = 16
DHEAD = 64
WIN = 256
NEXP = 8
FF = 2048
FC = FF // P
I2 = 4096
HID = 3072
HC = HID // P

ENGINES = ("sync", "gpsimd", "tensor", "vector", "scalar")


class CB:

    def __init__(self, nc, name):
        self.h = nc.alloc_semaphore(name)
        self.n = 0

    def inc(self, k=1):
        self.n += k
        return self.n

    @property
    def now(self):
        return self.n


class Prog:

    def __init__(self, nc):
        self.nc = nc
        self.steps = {e: [] for e in ENGINES}

    def sem(self, name):
        return CB(self.nc, name)

    def emit(self, eng, fn, wait=(), inc=(), drain=False):
        incs = []
        for cb, k in inc:
            cb.inc(k)
            incs.append((cb, k))
        self.steps[eng].append((fn, tuple(wait), tuple(incs), drain))

    def finalize(self):
        nc = self.nc
        with nc.Block() as block:
            for name in ENGINES:
                steps = self.steps[name]
                section = getattr(block, name)

                def body(eng, steps=steps):
                    for fn, waits, incs, drain in steps:
                        for cb, v in waits:
                            eng.wait_ge(cb.h, v)
                        if drain:
                            eng.drain()
                        inst = fn(eng)
                        if incs:
                            assert inst is not None, "inc on step with no instruction"
                            for cb, k in incs:
                                inst.then_inc(cb.h, k)

                section(body)


class Ctx:

    def __init__(self, nc):
        self.nc = nc
        self.pg = Prog(nc)
        self.res = {}
        self.arena = None
        self.arena_off = 0
        self.arena_marks = []

    def arena_alloc(self, name, shape, dtype):
        nc = self.nc
        per_part = int(np.prod(shape[1:])) * mybir.dt.size(dtype)
        per_part = (per_part + 31) & ~31
        assert self.arena_off + per_part <= self.arena_bytes, (
            f"arena overflow: {name} needs {per_part} at {self.arena_off}, cap {self.arena_bytes}")
        h = nc.virtual_sbuf_tensor(name, list(shape), dtype)
        nc._tensor(h.name, list(h.shape), h.dtype, type="SB")
        mloc = nc.lookup_mloc(h)
        mloc.allocated = True
        mloc.addr = self.arena_addr + self.arena_off
        self.arena_off += per_part
        return h

    def arena_alloc_at(self, name, shape, dtype, offset):
        nc = self.nc
        h = nc.virtual_sbuf_tensor(name, list(shape), dtype)
        nc._tensor(h.name, list(h.shape), h.dtype, type="SB")
        mloc = nc.lookup_mloc(h)
        mloc.allocated = True
        mloc.addr = self.arena_addr + offset
        return h

    def arena_reset(self):
        self.arena_off = 0
        self.arena_marks = []

    def arena_mark(self):
        self.arena_marks.append(self.arena_off)

    def arena_release(self):
        self.arena_off = self.arena_marks.pop()


def dt_np(dtype):
    return {F32: np.float32, BF16: ml_dtypes.bfloat16}[dtype]



VEC_NAMES = [
    "L0_attn_bq", "L0_attn_bk", "L0_attn_bv", "L0_attn_bd",
    "L1_attn_ln1_w", "L1_attn_ln2_w", "L1_attn_time_decay", "L1_attn_time_first",
    "L1_attn_time_mix_k", "L1_attn_time_mix_v", "L1_attn_time_mix_r",
    "L1_attn_time_mix_k2", "L1_attn_time_mix_r2",
]
VI = {n.split("_", 2)[-1]: i for i, n in enumerate(VEC_NAMES)}
DV = {"bq8": 0, "ew": 1}
ND = 2
CVAL = {"eps_ln": 0, "eps_rms": 1, "inv127": 2, "eps_wkv": 3}
CVAL_VALUES = {"eps_ln": 1e-5, "eps_rms": 1e-6, "inv127": 1.0 / 127.0, "eps_wkv": 1e-8}

ALL_PHASES = ("A1", "A2", "A3", "A4", "A5", "B1", "B2", "B3", "B4")


def build_program(phases=ALL_PHASES, dumps=(), dbg_loads=()):
    nc = bass.Bass("TRN2", target_bir_lowering=False, debug=False)
    c = Ctx(nc)
    pg = c.pg

    inp = {}
    inp["x"] = nc.dram_tensor("x", [T, C], F32, kind="ExternalInput")
    for nm, shape, dt in [
        ("L0_attn_Wq", [C, C], F32), ("L0_attn_Wk", [C, C], F32),
        ("L0_attn_Wv", [C, C], F32), ("L0_attn_Wd", [C, C], F32),
        ("L0_attn_bq", [C], F32), ("L0_attn_bk", [C], F32),
        ("L0_attn_bv", [C], F32), ("L0_attn_bd", [C], F32),
        ("L0_ffn_Wgate", [C, NEXP], F32), ("L0_ffn_bgate", [NEXP], F32),
        ("L0_ffn_W1", [NEXP, C, FF], F32), ("L0_ffn_b1", [NEXP, FF], F32),
        ("L0_ffn_W2", [NEXP, C, FF], F32), ("L0_ffn_b2", [NEXP, FF], F32),
        ("L0_ffn_Wg", [NEXP, FF, FF], F32), ("L0_ffn_bg", [NEXP, FF], F32),
        ("L0_ffn_Wl", [NEXP, FF, FF], F32), ("L0_ffn_bl", [NEXP, FF], F32),
        ("L0_ffn_W3", [NEXP, FF, C], F32), ("L0_ffn_b3", [NEXP, C], F32),
        ("L1_attn_ln1_w", [C], F32), ("L1_attn_ln2_w", [C], F32),
        ("L1_attn_time_decay", [C], F32), ("L1_attn_time_first", [C], F32),
        ("L1_attn_time_mix_k", [C], F32), ("L1_attn_time_mix_v", [C], F32),
        ("L1_attn_time_mix_r", [C], F32), ("L1_attn_W_key", [C, C], F32),
        ("L1_attn_W_value", [C, C], F32), ("L1_attn_W_recept", [C, C], F32),
        ("L1_attn_W_out", [C, C], F32), ("L1_attn_time_mix_k2", [C], F32),
        ("L1_attn_time_mix_r2", [C], F32), ("L1_attn_W_key2", [C, I2], F32),
        ("L1_attn_W_value2", [I2, C], F32), ("L1_attn_W_recept2", [C, C], F32),
        ("L1_ffn_w_gate", [HID, C], I32), ("L1_ffn_w_up", [HID, C], I32),
        ("L1_ffn_w_down", [C, HID], I32),
    ]:
        inp[nm] = nc.dram_tensor(nm, shape, dt, kind="ExternalInput")
    c.inp = inp

    y_out = nc.dram_tensor("y", [T, C], F32, kind="ExternalOutput")
    c.res["y"] = y_out

    c.res["F_hbm"] = nc.dram_tensor("F_hbm", [CC, P, T], F32, kind="Internal")
    c.res["x1_hbm"] = nc.dram_tensor("x1_hbm", [CC, P, T], F32, kind="Internal")
    c.res["xa_hbm"] = nc.dram_tensor("xa_hbm", [CC, P, T], F32, kind="Internal")
    c.res["wgT_hbm"] = nc.dram_tensor("wgT_hbm", [C, HID], BF16, kind="Internal")
    c.res["wuT_hbm"] = nc.dram_tensor("wuT_hbm", [C, HID], BF16, kind="Internal")
    c.res["wdT_hbm"] = nc.dram_tensor("wdT_hbm", [HID, C], BF16, kind="Internal")

    ident_bf_h = nc.inline_tensor(np.eye(P, dtype=ml_dtypes.bfloat16), name="ident_bf")
    ident_f_h = nc.inline_tensor(np.eye(P, dtype=np.float32), name="ident_f")
    ones_row_f_h = nc.inline_tensor(np.ones((1, P), np.float32), name="ones_row_f")
    ones_col_bf_h = nc.inline_tensor(np.ones((P, 1), ml_dtypes.bfloat16), name="ones_col_bf")
    ones_col_f_h = nc.inline_tensor(np.ones((P, 1), np.float32), name="ones_col_f")
    sel8_np = np.zeros((NEXP, NEXP * P), np.float32)
    for e in range(NEXP):
        sel8_np[e, e * P:(e + 1) * P] = 1.0
    sel8_h = nc.inline_tensor(sel8_np, name="sel8")

    sb = nc.alloc_sbuf_tensor
    c.res["ident_bf"] = sb("ident_bf_s", [P, P], BF16)
    c.res["ident_f"] = sb("ident_f_s", [P, P], F32)
    c.res["ones_row_f"] = sb("ones_row_f_s", [1, P], F32)
    c.res["ones_col_bf"] = sb("ones_col_bf_s", [P, 1], BF16)
    c.res["ones_col_f"] = sb("ones_col_f_s", [P, 1], F32)
    c.res["sel8"] = sb("sel8_s", [NEXP, NEXP * P], F32)
    c.res["masks"] = sb("masks_s", [P, 5, 640], F32)
    c.res["vecs"] = sb("vecs_s", [P, len(VEC_NAMES), CC], F32)
    c.res["dvec"] = sb("dvec_s", [P, ND, CC], F32)
    c.res["b1_fm"] = sb("b1_fm", [P, FC, NEXP], F32)
    c.res["b2_fm"] = sb("b2_fm", [P, FC, NEXP], F32)
    c.res["bg_fm"] = sb("bg_fm", [P, FC, NEXP], F32)
    c.res["bl_fm"] = sb("bl_fm", [P, FC, NEXP], F32)
    c.res["b3_f"] = sb("b3_f", [NEXP, C], F32)
    c.res["wgate"] = sb("wgate_s", [P, CC, NEXP], BF16)
    c.res["bgate"] = sb("bgate_s", [NEXP, 1], F32)
    c.res["cb_pf"] = sb("cb_pf", [NEXP, T], F32)
    c.res["xn_bf"] = sb("xn_bf", [P, CC, T], BF16)
    c.res["cval"] = sb("cval_s", [P, 8], F32)
    c.res["wgate_lo"] = sb("wgate_lo", [P, CC, NEXP], BF16)
    c.res["wgate_f"] = sb("wgate_f", [P, CC, NEXP], F32)

    rem = nc.sbuf_bytes_remaining - 512
    c.arena = sb("arena", [P, rem], mybir.dt.uint8)
    c.arena_addr = nc.lookup_mloc(c.arena).addr
    c.arena_bytes = rem

    c.pb01 = nc.alloc_psum_tensor("pb01", [P, 1024], F32)
    c.pb2 = nc.alloc_psum_tensor("pb2", [P, 512], F32)
    c.pb3 = nc.alloc_psum_tensor("pb3", [P, 512], F32)
    c.pbu = nc.alloc_psum_tensor("pbu", [P, 512], F32)
    c.pbs = nc.alloc_psum_tensor("pbs", [P, 512], F32)
    c.psum_tp = nc.alloc_psum_tensor("pb_tp", [P, 128], BF16)
    c.psum_tp2 = nc.alloc_psum_tensor("pb_tp2", [P, 128], BF16)

    c.consts_dram = dict(ident_bf=ident_bf_h, ident_f=ident_f_h,
                         ones_row_f=ones_row_f_h, ones_col_bf=ones_col_bf_h,
                         ones_col_f=ones_col_f_h, sel8=sel8_h)

    emit_setup(c)
    ph_fns = {"A1": emit_A1, "A2": emit_A2, "A3": emit_A3, "A4": emit_A4,
              "A5": emit_A5, "B1": emit_B1, "B2": emit_B2, "B3": emit_B3,
              "B4": emit_B4}
    c.last_done = list(c.setup_done_waits)
    for name in dbg_loads:
        emit_dbg_load(c, name)
    for ph in phases:
        ph_fns[ph](c)
    for name in dumps:
        emit_dump(c, name)
    emit_tail(c)

    pg.finalize()
    return nc, c



DBG_SPECS = {
    "xn_bf": ("xn_bf", [P, CC, T], BF16),
    "cb_pf": ("cb_pf", [NEXP, T], F32),
}
DBG_HBM = {"F_hbm": [CC, P, T], "x1_hbm": [CC, P, T], "xa_hbm": [CC, P, T]}


def emit_dbg_load(c, name):
    nc, pg = c.nc, c.pg
    if name in DBG_SPECS:
        key, shape, dt = DBG_SPECS[name]
        h = nc.dram_tensor("dbg_" + name, shape, dt, kind="ExternalInput")
        dst = c.res[key]
        s = pg.sem("dbgl_" + name)
        pg.emit("sync", lambda e, dst=dst, h=h: e.dma_start(
            out=dst[tuple(slice(None) for _ in dst.shape)], in_=h.ap()), inc=[(s, 16)])
        c.setup_done_waits.append((s, s.now))
        c.last_done.append((s, s.now))
    elif name in DBG_HBM:
        h = nc.dram_tensor("dbg_" + name, DBG_HBM[name], F32, kind="ExternalInput")
        dst = c.res[name]
        s = pg.sem("dbgl_" + name)
        pg.emit("sync", lambda e, dst=dst, h=h: e.dma_start(
            out=dst.ap(), in_=h.ap()), inc=[(s, 16)])
        c.setup_done_waits.append((s, s.now))
        c.last_done.append((s, s.now))
    else:
        raise KeyError(name)


def emit_dump(c, name):
    nc, pg = c.nc, c.pg
    w = list(c.last_done)
    if name not in DBG_SPECS and name not in DBG_HBM and name in c.res:
        src_t = c.res[name]
        shape = list(src_t.shape)
        h = nc.dram_tensor("dump_" + name, shape, src_t.dtype, kind="ExternalOutput")
        s = pg.sem("dmp_" + name)
        pg.emit("sync", lambda e, src_t=src_t, h=h: e.dma_start(
            out=h.ap(), in_=src_t[tuple(slice(None) for _ in src_t.shape)]),
            wait=w, inc=[(s, 16)])
        c.final_waits.append((s, s.now))
        return
    if name in DBG_SPECS:
        key, shape, dt = DBG_SPECS[name]
        h = nc.dram_tensor("dump_" + name, shape, dt, kind="ExternalOutput")
        src = c.res[key]
        s = pg.sem("dmp_" + name)
        pg.emit("sync", lambda e, src=src, h=h: e.dma_start(
            out=h.ap(), in_=src[tuple(slice(None) for _ in src.shape)]),
            wait=w, inc=[(s, 16)])
        c.final_waits.append((s, s.now))
    elif name in DBG_HBM:
        h = nc.dram_tensor("dump_" + name, DBG_HBM[name], F32, kind="ExternalOutput")
        src = c.res[name]
        s = pg.sem("dmp_" + name)
        pg.emit("sync", lambda e, src=src, h=h: e.dma_start(out=h.ap(), in_=src.ap()),
                wait=w, inc=[(s, 16)])
        c.final_waits.append((s, s.now))
    else:
        raise KeyError(name)


def emit_tail(c):
    pg = c.pg
    def tail(e):
        return None
    waits = list(c.final_waits)
    pg.emit("sync", lambda e: e.nop(), wait=waits)



def emit_setup(c):
    nc, pg = c.nc, c.pg
    ld = pg.sem("s_ld")
    c.final_waits = []
    c.last_done = []

    for key in ("ident_bf", "ident_f", "ones_row_f", "ones_col_bf", "ones_col_f", "sel8"):
        h = c.consts_dram[key]
        dst = c.res[key]
        pg.emit("sync", lambda e, dst=dst, h=h: e.dma_start(
            out=dst[tuple(slice(None) for _ in dst.shape)], in_=h.ap()), inc=[(ld, 16)])

    c.arena_mark()
    stage = c.arena_alloc("stage_vec", [NEXP, len(VEC_NAMES), P], F32)
    for j, nm in enumerate(VEC_NAMES):
        src = c.inp[nm].ap().rearrange("(a b) -> a b", b=P)
        pg.emit("sync", lambda e, j=j, src=src: e.dma_start(
            out=stage[:, j, :], in_=src), inc=[(ld, 16)])

    stage_b = c.arena_alloc("stage_b", [NEXP, 4, FF], F32)
    for j, nm in enumerate(("L0_ffn_b1", "L0_ffn_b2", "L0_ffn_bg", "L0_ffn_bl")):
        pg.emit("sync", lambda e, j=j, nm=nm: e.dma_start(
            out=stage_b[:, j, :], in_=c.inp[nm].ap()), inc=[(ld, 16)])
    pg.emit("sync", lambda e: e.dma_start(out=c.res["b3_f"][:, :],
                                          in_=c.inp["L0_ffn_b3"].ap()), inc=[(ld, 16)])
    pg.emit("sync", lambda e: e.dma_start(out=c.res["bgate"][:, 0],
                                          in_=c.inp["L0_ffn_bgate"].ap()), inc=[(ld, 16)])
    pg.emit("gpsimd", lambda e: e.dma_start(
        out=c.res["wgate"][:, :, :],
        in_=c.inp["L0_ffn_Wgate"].ap().rearrange("(kc p) f -> p kc f", p=P)),
        inc=[(ld, 16)])
    pg.emit("sync", lambda e: e.dma_start(
        out=c.res["wgate_f"][:, :, :],
        in_=c.inp["L0_ffn_Wgate"].ap().rearrange("(kc p) f -> p kc f", p=P)),
        inc=[(ld, 16)])
    n_ld = ld.now

    cvt = pg.sem("s_cval")
    cval = c.res["cval"]
    for k, j in CVAL.items():
        pg.emit("gpsimd", lambda e, j=j, v=CVAL_VALUES[k]: e.memset(cval[:, j:j + 1], v),
                drain=True, inc=[(cvt, 1)])

    mk = pg.sem("s_mask")
    masks = c.res["masks"]
    for mi, off in enumerate((0, 128, 256, 384, 512)):
        pg.emit("gpsimd", lambda e, mi=mi: e.memset(masks[:, mi, :], 0.0), drain=True)
        pg.emit("gpsimd", lambda e, mi=mi, off=off: e.affine_select(
            masks[:, mi, :], masks[:, mi, :], [[-1, 640]], ALU.is_ge, -1e9,
            base=WIN + off, channel_multiplier=1), drain=True)
        pg.emit("gpsimd", lambda e, mi=mi, off=off: e.affine_select(
            masks[:, mi, :], masks[:, mi, :], [[1, 640]], ALU.is_ge, -1e9,
            base=WIN - off, channel_multiplier=-1), drain=True,
            inc=[(mk, 1)])

    tp = pg.sem("s_tp")
    ev = pg.sem("s_ev")
    vecs = c.res["vecs"]
    idf = c.res["ident_f"]
    pu = c.pbu
    def tpev(src_ap_fn, dst_ap_fn, first_wait):
        j = tp.now
        pg.emit("tensor", lambda e, s=src_ap_fn: e.transpose(
            pu[:, 0:8], s(), idf[0:NEXP, 0:NEXP]),
            wait=first_wait + [(ev, j)], inc=[(tp, 1)])
        pg.emit("vector", lambda e, d=dst_ap_fn: e.tensor_copy(d(), pu[:, 0:8]),
            wait=[(tp, tp.now)], inc=[(ev, 1)], drain=True)

    for j in range(len(VEC_NAMES)):
        tpev(lambda j=j: stage[:, j, :], lambda j=j: vecs[:, j, :],
             [(ld, n_ld)] if j == 0 else [])
    for j, key in enumerate(("b1_fm", "b2_fm", "bg_fm", "bl_fm")):
        dstt = c.res[key]
        for fc in range(FC):
            tpev(lambda j=j, fc=fc: stage_b[:, j, fc * P:(fc + 1) * P],
                 lambda dstt=dstt, fc=fc: dstt[:, fc, :], [])

    dvec = c.res["dvec"]
    dv = pg.sem("s_dv")
    pg.emit("vector", lambda e: e.tensor_scalar_mul(
        dvec[:, DV["bq8"], :], vecs[:, VI["bq"], :], 0.125),
        wait=[(ev, ev.now)], drain=True, inc=[(dv, 1)])
    pg.emit("scalar", lambda e: e.activation(
        dvec[:, DV["ew"], :], vecs[:, VI["time_decay"], :], AF.Exp),
        wait=[(ev, ev.now)], inc=[(dv, 1)])
    pg.emit("vector", lambda e: e.tensor_scalar(
        dvec[:, DV["ew"], :], dvec[:, DV["ew"], :], -1.0, 20.0, ALU.mult, ALU.min),
        wait=[(dv, dv.now)], drain=True, inc=[(dv, 1)])
    pg.emit("vector", lambda e: e.tensor_scalar_max(
        dvec[:, DV["ew"], :], dvec[:, DV["ew"], :], -20.0), drain=True, inc=[(dv, 1)])
    pg.emit("scalar", lambda e: e.activation(
        dvec[:, DV["ew"], :], dvec[:, DV["ew"], :], AF.Exp),
        wait=[(dv, dv.now)], drain=True, inc=[(dv, 1)])

    pg.emit("vector", lambda e: e.tensor_tensor(
        c.res["wgate_lo"][:, :, :], c.res["wgate_f"][:, :, :],
        c.res["wgate"][:, :, :], ALU.subtract),
        wait=[(ld, n_ld)], drain=True, inc=[(dv, 1)])

    c.arena_release()
    c.setup_done_waits = [(ld, n_ld), (mk, mk.now), (ev, ev.now), (dv, dv.now),
                          (cvt, cvt.now)]



def emit_A1(c):
    nc, pg = c.nc, c.pg
    c.res["xn_lo"] = c.arena_alloc("xn_lo", [P, CC, T], BF16)
    c.arena_mark()
    x_ring = [c.arena_alloc(f"a1_x{i}", [P, C], F32) for i in range(2)]
    sq = c.arena_alloc("a1_sq", [P, C], F32)
    st_ring = [c.arena_alloc(f"a1_st{i}", [P, 8], F32) for i in range(2)]
    xn_tm = [c.arena_alloc(f"a1_xn{i}", [P, C], BF16) for i in range(2)]
    xn_lo_tm = [c.arena_alloc(f"a1_xnl{i}", [P, C], BF16) for i in range(2)]
    xnf = [c.arena_alloc(f"a1_xnf{i}", [P, C], F32) for i in range(2)]
    xn_lo = c.res["xn_lo"]

    ldx = pg.sem("a1_ldx")
    stat = pg.sem("a1_stat")
    xnr = pg.sem("a1_xnr")
    tpd = pg.sem("a1_tpd")
    evd = pg.sem("a1_evd")
    setup_w = c.setup_done_waits

    x_h = c.inp["x"].ap()
    xn_bf = c.res["xn_bf"]
    idb = c.res["ident_bf"]

    for i in range(16):
        ring = x_ring[i % 2]
        xnt = xn_tm[i % 2]
        st = st_ring[i % 2]
        w = list(setup_w) if i < 2 else []
        if i >= 2:
            w.append((xnr, i - 1))
        pg.emit("sync", lambda e, ring=ring, i=i: e.dma_start(
            out=ring[:, :], in_=x_h[i * P:(i + 1) * P, :]), wait=w, inc=[(ldx, 16)])
        wld = [(ldx, 16 * (i + 1))]
        if i >= 2:
            wld = wld + [(xnr, i - 1)]
        pg.emit("scalar", lambda e, ring=ring, st=st: e.activation(
            sq[:, :], ring[:, :], AF.Square, accum_out=st[:, 1:2]),
            wait=wld, drain=True, inc=[(stat, 1)])
        pg.emit("vector", lambda e, ring=ring, st=st: e.reduce_sum(
            st[:, 0:1], ring[:, :], axis=AX.X), wait=wld, drain=True, inc=[(stat, 1)])
        pg.emit("vector", lambda e, st=st: e.tensor_scalar_mul(st[:, 2:3], st[:, 0:1], 1.0 / C),
                wait=[(stat, stat.now)], drain=True)
        pg.emit("vector", lambda e, st=st: e.tensor_scalar_mul(st[:, 3:4], st[:, 2:3], -1.0),
                drain=True)
        pg.emit("vector", lambda e, st=st: e.tensor_tensor(st[:, 4:5], st[:, 2:3], st[:, 2:3],
                                                    ALU.mult), drain=True)
        pg.emit("vector", lambda e, st=st: e.scalar_tensor_tensor(
            st[:, 5:6], st[:, 1:2], 1.0 / C, st[:, 4:5], ALU.mult, ALU.subtract),
            drain=True, inc=[(stat, 1)])
        pg.emit("scalar", lambda e, st=st: e.activation(st[:, 6:7], st[:, 5:6], AF.Sqrt,
                                                 bias=c.res["cval"][:, CVAL["eps_ln"]:CVAL["eps_ln"] + 1]),
                wait=[(stat, stat.now)], drain=True, inc=[(stat, 1)])
        pg.emit("vector", lambda e, st=st: e.reciprocal(st[:, 7:8], st[:, 6:7]),
                wait=[(stat, stat.now)], drain=True, inc=[(stat, 1)])
        xf = xnf[i % 2]
        xlt = xn_lo_tm[i % 2]
        pg.emit("vector", lambda e, ring=ring, xf=xf, st=st: e.tensor_scalar(
            xf[:, :], ring[:, :], st[:, 3:4], st[:, 7:8], ALU.add, ALU.mult),
            wait=[(stat, stat.now)] + ([(evd, 16 * (i - 1))] if i >= 2 else []),
            drain=True)
        pg.emit("vector", lambda e, xf=xf, xnt=xnt: e.tensor_copy(xnt[:, :], xf[:, :]),
                drain=True)
        pg.emit("vector", lambda e, xf=xf, xnt=xnt, xlt=xlt: e.tensor_tensor(
            xlt[:, :], xf[:, :], xnt[:, :], ALU.subtract),
            drain=True, inc=[(xnr, 1)])
        for cc in range(CC):
            for half, (srcm, dstm) in enumerate(((xnt, xn_bf), (xlt, xn_lo))):
                g = i * 16 + cc * 2 + half
                ps = c.psum_tp if g % 2 == 0 else c.psum_tp2
                w2 = [(xnr, xnr.now)]
                if g >= 2:
                    w2.append((evd, g - 1))
                pg.emit("tensor", lambda e, ps=ps, srcm=srcm, cc=cc: e.transpose(
                    ps[:, :], srcm[:, cc * P:(cc + 1) * P], idb[:, :]),
                    wait=w2, inc=[(tpd, 1)])
                pg.emit("vector", lambda e, ps=ps, dstm=dstm, cc=cc, i=i: e.tensor_copy(
                    dstm[:, cc, i * P:(i + 1) * P], ps[:, :]),
                    wait=[(tpd, tpd.now)], drain=True, inc=[(evd, 1)])

    c.arena_release()
    c.last_done = [(evd, evd.now)]



def emit_A2(c):
    nc, pg = c.nc, c.pg
    c.arena_mark()
    gf = c.arena_alloc("a2_gf", [NEXP, 512], F32)
    gtm = c.arena_alloc("a2_gtm", [P, NEXP], F32)
    st = c.arena_alloc("a2_st", [P, 8], F32)
    eq1 = c.arena_alloc("a2_eq1", [P, NEXP], F32)
    eq2 = c.arena_alloc("a2_eq2", [P, NEXP], F32)
    gmask = c.arena_alloc("a2_gm", [P, NEXP], F32)
    t1 = c.arena_alloc("a2_t1", [P, NEXP], F32)
    cbtm = c.arena_alloc("a2_cb", [P, NEXP], F32)

    mm = pg.sem("a2_mm")
    ev = pg.sem("a2_ev")
    dn = pg.sem("a2_dn")

    xn_bf = c.res["xn_bf"]
    wg = c.res["wgate"]
    bg = c.res["bgate"]
    cb_pf = c.res["cb_pf"]
    pbs = c.pbs
    pbu = c.pbu
    idf = c.res["ident_f"]
    prev = list(c.last_done)

    for tt in range(4):
        def gmm(e, tt=tt):
            r = None
            xl = c.res["xn_lo"]
            wgl = c.res["wgate_lo"]
            terms = [(wg, xn_bf), (wg, xl), (wgl, xn_bf)]
            n = len(terms) * CC
            i = 0
            for wgt, act in terms:
                for kc in range(CC):
                    r = e.matmul(pbs[0:NEXP, :], wgt[:, kc, :],
                                 act[:, kc, tt * 512:(tt + 1) * 512],
                                 start=(i == 0), stop=(i == n - 1))
                    i += 1
            return r
        pg.emit("tensor", gmm, wait=prev + [(ev, ev.now)], inc=[(mm, 1)])
        pg.emit("vector", lambda e: e.tensor_scalar_add(gf[:, :], pbs[0:NEXP, :],
                                                        bg[:, 0:1]),
                wait=[(mm, mm.now)], drain=True, inc=[(ev, 1)])
        for q in range(4):
            t0 = tt * 512 + q * P
            pg.emit("tensor", lambda e, q=q: e.transpose(
                pbu[:, 0:NEXP], gf[:, q * P:(q + 1) * P], idf[0:NEXP, 0:NEXP]),
                wait=[(ev, ev.now)], inc=[(mm, 1)])
            pg.emit("vector", lambda e: e.tensor_copy(gtm[:, :], pbu[:, 0:NEXP]),
                    wait=[(mm, mm.now)], drain=True)
            pg.emit("vector", lambda e: e.reduce_max(st[:, 0:1], gtm[:, :], axis=AX.X),
                    drain=True)
            pg.emit("vector", lambda e: e.tensor_single_scalar(
                eq1[:, :], gtm[:, :], st[:, 0:1], ALU.is_equal), drain=True)
            pg.emit("vector", lambda e: e.scalar_tensor_tensor(
                gmask[:, :], eq1[:, :], -1e30, gtm[:, :], ALU.mult, ALU.add), drain=True)
            pg.emit("vector", lambda e: e.reduce_max(st[:, 1:2], gmask[:, :], axis=AX.X),
                    drain=True)
            pg.emit("vector", lambda e: e.tensor_single_scalar(
                eq2[:, :], gmask[:, :], st[:, 1:2], ALU.is_equal), drain=True)
            pg.emit("vector", lambda e: e.tensor_scalar_mul(st[:, 2:3], st[:, 0:1], -1.0),
                    drain=True)
            pg.emit("vector", lambda e: e.tensor_scalar_mul(st[:, 3:4], st[:, 1:2], -1.0),
                    drain=True, inc=[(dn, 1)])
            pg.emit("scalar", lambda e: e.activation(st[:, 4:5], st[:, 0:1], AF.Sigmoid,
                                                     bias=st[:, 3:4]),
                    wait=[(dn, dn.now)], drain=True)
            pg.emit("scalar", lambda e: e.activation(st[:, 5:6], st[:, 1:2], AF.Sigmoid,
                                                     bias=st[:, 2:3]),
                    drain=True, inc=[(dn, 1)])
            pg.emit("vector", lambda e: e.tensor_scalar_mul(t1[:, :], eq1[:, :], st[:, 4:5]),
                    wait=[(dn, dn.now)], drain=True)
            pg.emit("vector", lambda e: e.scalar_tensor_tensor(
                cbtm[:, :], eq2[:, :], st[:, 5:6], t1[:, :], ALU.mult, ALU.add),
                drain=True, inc=[(dn, 1)])
            pg.emit("tensor", lambda e: e.transpose(
                pbu[0:NEXP, 0:P], cbtm[:, :], idf[:, :]),
                wait=[(dn, dn.now)], inc=[(mm, 1)])
            pg.emit("vector", lambda e, t0=t0: e.tensor_copy(
                cb_pf[0:NEXP, t0:t0 + P], pbu[0:NEXP, 0:P]), wait=[(mm, mm.now)],
                drain=True, inc=[(ev, 1)])

    c.arena_release()
    c.last_done = [(ev, ev.now), (dn, dn.now)]


class Banks:

    def __init__(self, slots):
        self.slots = slots
        self.free = [None] * len(slots)
        self.i = 0

    def acquire(self):
        j = self.i % len(self.slots)
        self.i += 1
        ap = self.slots[j]
        w = self.free[j]
        return j, ap, ([w] if w else [])

    def release(self, j, cb):
        self.free[j] = (cb, cb.now)



def emit_A3(c):
    nc, pg = c.nc, c.pg
    c.arena_reset()
    z = c.arena_alloc("a3_z", [P, FC, 1024], BF16)
    o = c.arena_alloc("a3_o", [P, FC, 1024], BF16)
    w12a = c.arena_alloc("a3_wa", [P, FC, 1024], BF16)
    w12b = c.arena_alloc("a3_wb", [P, FC, 1024], BF16)
    cbE = c.arena_alloc("a3_cbE", [P, 1024], F32)
    x1s = c.arena_alloc("a3_x1s", [P, 512], F32)
    ys = c.arena_alloc("a3_ys", [P, 512], F32)
    sgs = c.arena_alloc("a3_sgs", [P, 512], BF16)
    ls = c.arena_alloc("a3_ls", [P, 512], F32)
    blk = (FC * 1024 * 2 + 31) & ~31
    w3 = c.arena_alloc_at("a3_w3", [P, FC, C], BF16, 2 * blk)

    wdma = pg.sem("a3_wdma")
    fdma = pg.sem("a3_fdma")
    mm = pg.sem("a3_mm")
    evv = pg.sem("a3_evv")
    evs = pg.sem("a3_evs")

    xn_bf = c.res["xn_bf"]
    cb_pf = c.res["cb_pf"]
    sel8 = c.res["sel8"]
    b3_f = c.res["b3_f"]
    F_hbm = c.res["F_hbm"].ap()
    b1_fm, b2_fm = c.res["b1_fm"], c.res["b2_fm"]
    bg_fm, bl_fm = c.res["bg_fm"], c.res["bl_fm"]

    banks = Banks([c.pb01[:, 0:512], c.pb01[:, 512:1024], c.pb2[:, :], c.pb3[:, :]])
    ubank_free = [None]
    prev = list(c.last_done)

    w1v = w12a.rearrange("p a b -> p (a b)").rearrange("p (kc f) -> p kc f", kc=CC)
    w2v = w12b.rearrange("p a b -> p (a b)").rearrange("p (kc f) -> p kc f", kc=CC)

    for tt in range(4):
        for ct in range(CC):
            t0 = tt * 512
            w0 = list(prev) if (tt == 0 and ct == 0) else []
            pg.emit("tensor",
                    lambda e, ct=ct, t0=t0: e.matmul(
                        c.pbu[:, :], b3_f[0:NEXP, ct * P:(ct + 1) * P],
                        cb_pf[0:NEXP, t0:t0 + 512], start=True, stop=True),
                    wait=w0 + ([ubank_free[0]] if ubank_free[0] else []), inc=[(mm, 1)])
            pg.emit("vector", lambda e: e.tensor_copy(ys[:, :], c.pbu[:, :]),
                    wait=[(mm, mm.now), (fdma, fdma.now)], drain=True, inc=[(evv, 1)])
            ubank_free[0] = (evv, evv.now)
            pg.emit("gpsimd", lambda e, ct=ct, t0=t0: e.dma_start(
                out=F_hbm[ct, :, t0:t0 + 512], in_=ys[:, :]),
                wait=[(evv, evv.now)], inc=[(fdma, 16)])

    wbuf_free = None
    for tt2 in range(2):
        tb = tt2 * 1024
        for ex in range(NEXP):
            for s in range(2):
                pg.emit("tensor",
                        lambda e, ex=ex, tb=tb, s=s: e.matmul(
                            c.pbu[:, :], sel8[0:NEXP, ex * P:(ex + 1) * P],
                            cb_pf[0:NEXP, tb + s * 512: tb + s * 512 + 512],
                            start=True, stop=True),
                        wait=([ubank_free[0]] if ubank_free[0] else []), inc=[(mm, 1)])
                pg.emit("vector", lambda e, s=s: e.tensor_copy(
                    cbE[:, s * 512:(s + 1) * 512], c.pbu[:, :]),
                    wait=[(mm, mm.now)], drain=True, inc=[(evv, 1)])
                ubank_free[0] = (evv, evv.now)

            for wname, dst in (("L0_ffn_W1", w1v), ("L0_ffn_W2", w2v)):
                src_ap = c.inp[wname].ap()[ex].rearrange("(kc p) f -> p kc f", p=P)
                pg.emit("gpsimd", lambda e, dst=dst, src_ap=src_ap: e.dma_start(
                    out=dst[:, :, :], in_=src_ap),
                    wait=([wbuf_free] if wbuf_free else []), inc=[(wdma, 16)])
            wd_12 = wdma.now

            z_wait_first = [(mm, mm.now)] if wbuf_free else []
            for fc in range(FC):
                for s in range(2):
                    j1, pA, wA = banks.acquire()
                    j2, pB, wB = banks.acquire()
                    pg.emit("tensor",
                            lambda e, fc=fc, s=s, pA=pA, tb=tb: _kmm(
                                e, pA, w1v, xn_bf, fc, CC, tb + s * 512),
                            wait=[(wdma, wd_12)] + wA + wB, inc=[(mm, 1)])
                    pg.emit("tensor",
                            lambda e, fc=fc, s=s, pB=pB, tb=tb: _kmm(
                                e, pB, w2v, xn_bf, fc, CC, tb + s * 512),
                            wait=[], inc=[(mm, 1)])
                    mmv = mm.now
                    pg.emit("vector", lambda e, fc=fc, ex=ex, pA=pA: e.tensor_scalar_add(
                        x1s[:, :], pA, b1_fm[:, fc, ex:ex + 1]),
                        wait=[(mm, mmv)], drain=True, inc=[(evv, 1)])
                    banks.release(j1, evv)
                    pg.emit("vector",
                            lambda e, fc=fc, s=s, ex=ex, pB=pB: e.scalar_tensor_tensor(
                                z[:, fc, s * 512:(s + 1) * 512], pB,
                                b2_fm[:, fc, ex:ex + 1], x1s[:, :], ALU.add, ALU.mult),
                            drain=True, inc=[(evv, 1)])
                    banks.release(j2, evv)
            z_done = evv.now

            for h in range(2):
                for wname, dst in (("L0_ffn_Wg", w12a), ("L0_ffn_Wl", w12b)):
                    src_ap = c.inp[wname].ap()[ex][:, h * 1024:(h + 1) * 1024]
                    src_ap = src_ap.rearrange("(kc p) f -> p kc f", p=P)
                    pg.emit("gpsimd", lambda e, dst=dst, src_ap=src_ap: e.dma_start(
                        out=dst[:, :, :], in_=src_ap), wait=[(mm, mm.now)],
                        inc=[(wdma, 16)])
                wd_gl = wdma.now
                for s in range(2):
                    for gf in range(NEXP):
                        gt = h * NEXP + gf
                        jg, pG, wG = banks.acquire()
                        jl, pL, wL = banks.acquire()
                        pg.emit("tensor",
                                lambda e, gf=gf, s=s, pG=pG: _kmm(
                                    e, pG, w12a, z, gf, FC, s * 512),
                                wait=[(wdma, wd_gl), (evv, z_done)] + wG + wL,
                                inc=[(mm, 1)])
                        pg.emit("tensor",
                                lambda e, gf=gf, s=s, pL=pL: _kmm(
                                    e, pL, w12b, z, gf, FC, s * 512),
                                wait=[], inc=[(mm, 1)])
                        mmv = mm.now
                        pg.emit("scalar", lambda e, gt=gt, ex=ex, pG=pG: e.activation(
                            sgs[:, :], pG, AF.Silu, bias=bg_fm[:, gt, ex:ex + 1]),
                            wait=[(mm, mmv), (evv, evv.now)], drain=True, inc=[(evs, 1)])
                        banks.release(jg, evs)
                        pg.emit("vector",
                                lambda e, gt=gt, s=s, ex=ex, pL=pL: e.scalar_tensor_tensor(
                                    ls[:, :], pL, bl_fm[:, gt, ex:ex + 1],
                                    cbE[:, s * 512:(s + 1) * 512], ALU.add, ALU.mult),
                                wait=[(evs, evs.now)], drain=True, inc=[(evv, 1)])
                        banks.release(jl, evv)
                        pg.emit("vector", lambda e, gt=gt, s=s: e.tensor_tensor(
                            o[:, gt, s * 512:(s + 1) * 512], ls[:, :], sgs[:, :],
                            ALU.mult), drain=True, inc=[(evv, 1)])
            o_done = evv.now

            src_ap = c.inp["L0_ffn_W3"].ap()[ex].rearrange("(kc p) f -> p kc f", p=P)
            pg.emit("gpsimd", lambda e, src_ap=src_ap: e.dma_start(
                out=w3[:, :, :], in_=src_ap), wait=[(mm, mm.now)], inc=[(wdma, 16)])
            wd_3 = wdma.now
            for s in range(2):
                for ct in range(CC):
                    jy, pY, wY = banks.acquire()
                    pg.emit("tensor",
                            lambda e, ct=ct, s=s, pY=pY: _kmm(
                                e, pY, w3, o, ct, FC, s * 512),
                            wait=[(wdma, wd_3), (evv, o_done)] + wY, inc=[(mm, 1)])
                    mmv = mm.now
                    pg.emit("vector", lambda e, pY=pY: e.tensor_copy(ys[:, :], pY),
                            wait=[(mm, mmv), (fdma, fdma.now)], drain=True,
                            inc=[(evv, 1)])
                    banks.release(jy, evv)
                    pg.emit("gpsimd", lambda e, ct=ct, s=s, tb=tb: e.dma_start(
                        out=F_hbm[ct, :, tb + s * 512: tb + s * 512 + 512],
                        in_=ys[:, :], accum_op=ALU.add),
                        wait=[(evv, evv.now)], inc=[(fdma, 16)])
            wbuf_free = (mm, mm.now)

    c.last_done = [(fdma, fdma.now), (evv, evv.now), (evs, evs.now), (mm, mm.now)]


def _kmm(e, psum, wbuf, act, ftile, nk, t0):
    r = None
    for kc in range(nk):
        r = e.matmul(psum, wbuf[:, kc, ftile * P:(ftile + 1) * P],
                     act[:, kc, t0:t0 + 512], start=(kc == 0), stop=(kc == nk - 1))
    return r


def emit_A4(c):
    nc, pg = c.nc, c.pg
    c.arena_reset()
    Q = c.arena_alloc("a4_Q", [P, CC, T], BF16)
    Kf = c.arena_alloc("a4_K", [P, CC, T], BF16)
    Vt = c.arena_alloc("a4_V", [P, 16, C], BF16)
    wpa = c.arena_alloc("a4_wpa", [P, CC, C], BF16)
    wpb = c.arena_alloc("a4_wpb", [P, CC, C], BF16)
    c.a5 = dict(Q=Q, Kf=Kf, Vt=Vt, wpa=wpa, wpb=wpb)

    wdma = pg.sem("a4_wdma")
    mm = pg.sem("a4_mm")
    evv = pg.sem("a4_evv")

    xn_bf = c.res["xn_bf"]
    vecs = c.res["vecs"]
    dvec = c.res["dvec"]
    banks = Banks([c.pb01[:, 0:512], c.pb01[:, 512:1024], c.pb2[:, :], c.pb3[:, :]])
    prev = list(c.last_done)

    for wname, dst, w0 in (("L0_attn_Wq", wpa, prev), ("L0_attn_Wk", wpb, prev)):
        src_ap = c.inp[wname].ap().rearrange("(kc p) f -> p kc f", p=P)
        pg.emit("gpsimd", lambda e, dst=dst, src_ap=src_ap: e.dma_start(
            out=dst[:, :, :], in_=src_ap), wait=w0, inc=[(wdma, 16)])
    wd_qk = wdma.now

    for co in range(CC):
        for s in range(4):
            jq, pQ, wQ = banks.acquire()
            jk, pK, wK = banks.acquire()
            pg.emit("tensor", lambda e, co=co, s=s, pQ=pQ: _kmm(
                e, pQ, wpa, xn_bf, co, CC, s * 512),
                wait=[(wdma, wd_qk)] + wQ + wK, inc=[(mm, 1)])
            pg.emit("tensor", lambda e, co=co, s=s, pK=pK: _kmm(
                e, pK, wpb, xn_bf, co, CC, s * 512),
                wait=[], inc=[(mm, 1)])
            mmv = mm.now
            pg.emit("vector", lambda e, co=co, s=s, pQ=pQ: e.tensor_scalar(
                Q[:, co, s * 512:(s + 1) * 512], pQ, 0.125,
                dvec[:, DV["bq8"], co:co + 1], ALU.mult, ALU.add),
                wait=[(mm, mmv)], drain=True, inc=[(evv, 1)])
            banks.release(jq, evv)
            pg.emit("vector", lambda e, co=co, s=s, pK=pK: e.tensor_scalar_add(
                Kf[:, co, s * 512:(s + 1) * 512], pK, vecs[:, VI["bk"], co:co + 1]),
                drain=True, inc=[(evv, 1)])
            banks.release(jk, evv)

    src_ap = c.inp["L0_attn_Wv"].ap().rearrange("(kc p) f -> p kc f", p=P)
    pg.emit("gpsimd", lambda e, src_ap=src_ap: e.dma_start(
        out=wpa[:, :, :], in_=src_ap), wait=[(mm, mm.now)], inc=[(wdma, 16)])
    wd_v = wdma.now

    for i in range(16):
        for half in range(2):
            jv, pV, wV = banks.acquire()
            pg.emit("tensor", lambda e, i=i, half=half, pV=pV: _vmm(
                e, pV, xn_bf, wpa, i, half),
                wait=[(wdma, wd_v)] + wV, inc=[(mm, 1)])
            mmv = mm.now
            pg.emit("vector", lambda e, i=i, half=half, pV=pV: e.tensor_copy(
                Vt[:, i, half * 512:(half + 1) * 512], pV),
                wait=[(mm, mmv)], drain=True, inc=[(evv, 1)])
            banks.release(jv, evv)

    c.last_done = [(mm, mm.now), (evv, evv.now)]


def _vmm(e, psum, xn_bf, wv, i, half):
    r = None
    for kc in range(CC):
        r = e.matmul(psum, xn_bf[:, kc, i * P:(i + 1) * P],
                     wv[:, kc, half * 512:(half + 1) * 512],
                     start=(kc == 0), stop=(kc == CC - 1))
    return r



def emit_A5(c):
    nc, pg = c.nc, c.pg
    Q, Kf, Vt = c.a5["Q"], c.a5["Kf"], c.a5["Vt"]
    wpb = c.a5["wpb"]
    sc = c.arena_alloc("a5_sc", [P, 640], F32)
    pbf = c.arena_alloc("a5_pbf", [P, 640], BF16)
    pbf2 = c.arena_alloc("a5_pbf2", [P, 640], BF16)
    pT = c.arena_alloc("a5_pT", [P, 5, P], BF16)
    ao = c.arena_alloc("a5_ao", [P, CC, P], BF16)
    stt_ = c.arena_alloc("a5_st", [P, 4], F32)
    x1b = c.arena_alloc("a5_x1b", [P, CC, P], F32)
    sf = c.arena_alloc("a5_sf", [P, P], F32)
    ss = c.arena_alloc("a5_ss", [P, P], F32)
    ss2 = c.arena_alloc("a5_ss2", [P, P], F32)
    wpa_off = 3 * ((CC * T * 2 + 31) & ~31)
    blk32 = (CC * T * 2 + 31) & ~31
    blk16 = (CC * C * 2 + 31) & ~31
    wpa_off = 3 * blk32
    xtm = [c.arena_alloc_at(f"a5_x{i}", [P, C], F32, wpa_off + i * 4096) for i in range(2)]
    xfm = c.arena_alloc_at("a5_xfm", [P, CC, P], F32, wpa_off + 8192)
    Ftile = c.arena_alloc_at("a5_Ft", [P, CC, P], F32, wpa_off + 12288)

    ld = pg.sem("a5_ld")
    fl = pg.sem("a5_fl")
    mm = pg.sem("a5_mm")
    evv = pg.sem("a5_evv")
    evs = pg.sem("a5_evs")
    odma = pg.sem("a5_odma")
    wdma = pg.sem("a5_wdma")

    vecs = c.res["vecs"]
    masks = c.res["masks"]
    idb = c.res["ident_bf"]
    idf = c.res["ident_f"]
    x_h = c.inp["x"].ap()
    F_hbm = c.res["F_hbm"].ap()
    x1_hbm = c.res["x1_hbm"].ap()
    prev = list(c.last_done)

    src_ap = c.inp["L0_attn_Wd"].ap().rearrange("(kc p) f -> p kc f", p=P)
    pg.emit("gpsimd", lambda e, src_ap=src_ap: e.dma_start(
        out=wpb[:, :, :], in_=src_ap), wait=prev, inc=[(wdma, 16)])
    wd_d = wdma.now

    for qt in range(16):
        q0 = qt * P
        ks = min(max(q0 - 2 * WIN // 2, 0), T - 640)
        ks = min(max(q0 - 256, 0), T - 640)
        mi = (q0 - ks) // 128
        xr = xtm[qt % 2]
        wx = [(evv, evv.now)] if qt >= 2 else list(prev)
        pg.emit("sync", lambda e, xr=xr, q0=q0: e.dma_start(
            out=xr[:, :], in_=x_h[q0:q0 + P, :]), wait=wx, inc=[(ld, 16)])
        pg.emit("sync", lambda e, q0=q0: e.dma_start(
            out=Ftile[:, :, :],
            in_=F_hbm[:, :, q0:q0 + P].rearrange("cc p t -> p cc t")),
            wait=[(evv, evv.now)] + (list(prev) if qt == 0 else []), inc=[(fl, 16)])
        ld_t = ld.now
        fl_t = fl.now

        for cc in range(CC):
            pg.emit("tensor", lambda e, xr=xr, cc=cc: e.transpose(
                c.pbu[:, 0:P], xr[:, cc * P:(cc + 1) * P], idf[:, :]),
                wait=[(ld, ld_t), (evv, evv.now)], inc=[(mm, 1)])
            pg.emit("vector", lambda e, cc=cc: e.tensor_copy(
                xfm[:, cc, :], c.pbu[:, 0:P]),
                wait=[(mm, mm.now)], drain=True, inc=[(evv, 1)])

        for h in range(NHEAD):
            pb = (h % 2) * 64
            hc = h // 2
            def smm(e, h=h, pb=pb, hc=hc, q0=q0, ks=ks):
                r = e.matmul(c.pb01[:, 0:512], Q[pb:pb + 64, hc, q0:q0 + P],
                             Kf[pb:pb + 64, hc, ks:ks + 512], start=True, stop=True)
                r = e.matmul(c.pb01[:, 512:640], Q[pb:pb + 64, hc, q0:q0 + P],
                             Kf[pb:pb + 64, hc, ks + 512:ks + 640],
                             start=True, stop=True)
                return r
            pg.emit("tensor", smm, wait=[(evv, evv.now), (wdma, wd_d)], inc=[(mm, 1)])
            pg.emit("vector", lambda e, mi=mi: e.tensor_tensor(
                sc[:, :], c.pb01[:, 0:640], masks[:, mi, :], ALU.add),
                wait=[(mm, mm.now)], drain=True, inc=[(evv, 1)])
            pg.emit("vector", lambda e: e.reduce_max(
                stt_[:, 0:1], sc[:, :], axis=AX.X, negate=True), drain=True,
                inc=[(evv, 1)])
            pg.emit("scalar", lambda e: e.activation(
                pbf[:, :], sc[:, :], AF.Exp, bias=stt_[:, 0:1],
                accum_out=stt_[:, 1:2]),
                wait=[(evv, evv.now)], drain=True, inc=[(evs, 1)])
            pg.emit("vector", lambda e: e.reciprocal(stt_[:, 2:3], stt_[:, 1:2]),
                    wait=[(evs, evs.now)], drain=True)
            pg.emit("vector", lambda e: e.tensor_scalar_mul(
                pbf2[:, :], pbf[:, :], stt_[:, 2:3]), drain=True, inc=[(evv, 1)])
            for ki in range(5):
                ps = c.psum_tp if ki % 2 == 0 else c.psum_tp2
                pg.emit("tensor", lambda e, ki=ki, ps=ps: e.transpose(
                    ps[:, :], pbf2[:, ki * P:(ki + 1) * P], idb[:, :]),
                    wait=[(evv, evv.now)] if ki == 0 else [(evv, evv.now)],
                    inc=[(mm, 1)])
                pg.emit("vector", lambda e, ki=ki, ps=ps: e.tensor_copy(
                    pT[:, ki, :], ps[:, :]),
                    wait=[(mm, mm.now)], drain=True, inc=[(evv, 1)])
            pvb = c.pb2 if h % 2 == 0 else c.pb3
            def pvmm(e, h=h, pb=pb, ks=ks, pvb=pvb):
                r = None
                for ki in range(5):
                    r = e.matmul(pvb[pb:pb + 64, 0:P], Vt[:, ks // P + ki, h * 64:(h + 1) * 64],
                                 pT[:, ki, :], start=(ki == 0), stop=(ki == 4))
                return r
            pg.emit("tensor", pvmm, wait=[(evv, evv.now)] + (
                [(evv, evv.now)] if h < 2 else [(evv, evv.now)]), inc=[(mm, 1)])
            mmv = mm.now
            pg.emit("vector", lambda e, h=h, pb=pb, pvb=pvb: e.tensor_scalar_add(
                ao[pb:pb + 64, h // 2, :], pvb[pb:pb + 64, 0:P],
                vecs[pb:pb + 64, VI["bv"], h // 2:h // 2 + 1]),
                wait=[(mm, mmv)], drain=True, inc=[(evv, 1)])

        for ct in range(CC):
            jy = None
            pg.emit("tensor", lambda e, ct=ct: _wdmm(e, c.pbs[:, 0:P], wpb, ao, ct),
                    wait=[(evv, evv.now), (fl, fl_t)], inc=[(mm, 1)])
            mmv = mm.now
            pg.emit("scalar", lambda e, ct=ct: e.activation(
                sf[:, :], Ftile[:, ct, :], AF.Silu),
                wait=[(fl, fl_t), (evv, evv.now)], drain=True, inc=[(evs, 1)])
            pg.emit("vector", lambda e, ct=ct: e.tensor_scalar_add(
                ss[:, :], c.pbs[:, 0:P], vecs[:, VI["bd"], ct:ct + 1]),
                wait=[(mm, mmv), (evs, evs.now)], drain=True)
            pg.emit("vector", lambda e: e.tensor_tensor(
                ss2[:, :], ss[:, :], sf[:, :], ALU.mult), drain=True)
            pg.emit("vector", lambda e, ct=ct: e.tensor_tensor(
                x1b[:, ct, :], ss2[:, :], xfm[:, ct, :], ALU.add),
                wait=([(odma, odma.now)] if ct == 0 else []),
                drain=True, inc=[(evv, 1)])
        pg.emit("sync", lambda e, q0=q0: e.dma_start(
            out=x1_hbm[:, :, q0:q0 + P].rearrange("cc p t -> p cc t"),
            in_=x1b[:, :, :]),
            wait=[(evv, evv.now), (odma, odma.now)], inc=[(odma, 16)])

    c.last_done = [(odma, odma.now), (evv, evv.now), (evs, evs.now), (mm, mm.now)]


def _wdmm(e, psum, wd, ao, ct):
    r = None
    for kc in range(CC):
        r = e.matmul(psum, wd[:, kc, ct * P:(ct + 1) * P], ao[:, kc, :],
                     start=(kc == 0), stop=(kc == CC - 1))
    return r


def emit_B1(c):
    nc, pg = c.nc, c.pg
    c.arena_reset()
    x1t = c.arena_alloc("b1_x1t", [P, CC, 512], F32)
    xat = c.arena_alloc("b1_xat", [P, CC, 512], F32)
    sq = c.arena_alloc("b1_sq", [P, 512], BF16)
    rows = c.arena_alloc("b1_rows", [1, 8, 512], F32)
    t1 = c.arena_alloc("b1_t1", [P, 512], F32)

    ld = pg.sem("b1_ld")
    mm = pg.sem("b1_mm")
    evv = pg.sem("b1_evv")
    evs = pg.sem("b1_evs")
    odma = pg.sem("b1_odma")

    x1_hbm = c.res["x1_hbm"].ap()
    xa_hbm = c.res["xa_hbm"].ap()
    xn_bf = c.res["xn_bf"]
    onef = c.res["ones_col_f"]
    oneb = c.res["ones_col_bf"]
    onerf = c.res["ones_row_f"]
    cval = c.res["cval"]
    prev = list(c.last_done)

    for tt in range(4):
        t0 = tt * 512
        w0 = list(prev) if tt == 0 else [(evv, evv.now)]
        pg.emit("sync", lambda e, t0=t0: e.dma_start(
            out=x1t[:, :, :],
            in_=x1_hbm[:, :, t0:t0 + 512].rearrange("cc p t -> p cc t")),
            wait=w0, inc=[(ld, 16)])
        ldv = ld.now
        def sxmm(e):
            r = None
            for cc in range(CC):
                r = e.matmul(c.pbs[0:1, :], onef[:, 0:1], x1t[:, cc, :],
                             start=(cc == 0), stop=(cc == CC - 1))
            return r
        pg.emit("tensor", sxmm, wait=[(ld, ldv), (evv, evv.now)], inc=[(mm, 1)])
        pg.emit("vector", lambda e: e.tensor_copy(rows[:, 0, :], c.pbs[0:1, :]),
                wait=[(mm, mm.now)], drain=True, inc=[(evv, 1)])
        for cc in range(CC):
            pg.emit("scalar", lambda e, cc=cc: e.activation(
                sq[:, :], x1t[:, cc, :], AF.Square),
                wait=[(ld, ldv), (mm, mm.now)],
                drain=True, inc=[(evs, 1)])
            pg.emit("tensor", lambda e, cc=cc: e.matmul(
                c.pbu[0:1, :], oneb[:, 0:1], sq[:, :],
                start=(cc == 0), stop=(cc == CC - 1)),
                wait=[(evs, evs.now)], inc=[(mm, 1)])
        pg.emit("vector", lambda e: e.tensor_copy(rows[:, 1, :], c.pbu[0:1, :]),
                wait=[(mm, mm.now)], drain=True, inc=[(evv, 1)])
        pg.emit("vector", lambda e: e.tensor_scalar_mul(rows[:, 2, :], rows[:, 0, :],
                                                        1.0 / C), drain=True)
        pg.emit("vector", lambda e: e.tensor_scalar_mul(rows[:, 3, :], rows[:, 2, :],
                                                        -1.0), drain=True)
        pg.emit("vector", lambda e: e.tensor_tensor(rows[:, 4, :], rows[:, 2, :],
                                                    rows[:, 2, :], ALU.mult), drain=True)
        pg.emit("vector", lambda e: e.scalar_tensor_tensor(
            rows[:, 5, :], rows[:, 1, :], 1.0 / C, rows[:, 4, :],
            ALU.mult, ALU.subtract), drain=True, inc=[(evv, 1)])
        pg.emit("scalar", lambda e: e.activation(rows[:, 6, :], rows[:, 5, :], AF.Sqrt,
                                                 bias=cval[0:1, CVAL["eps_ln"]:CVAL["eps_ln"] + 1]),
                wait=[(evv, evv.now)], drain=True, inc=[(evs, 1)])
        pg.emit("vector", lambda e: e.reciprocal(rows[:, 7, :], rows[:, 6, :]),
                wait=[(evs, evs.now)], drain=True, inc=[(evv, 1)])
        pg.emit("tensor", lambda e: e.matmul(c.pbu[:, :], onerf[0:1, :], rows[:, 3, :],
                                             start=True, stop=True),
                wait=[(evv, evv.now)], inc=[(mm, 1)])
        pg.emit("tensor", lambda e: e.matmul(c.pbs[:, :], onerf[0:1, :], rows[:, 7, :],
                                             start=True, stop=True),
                wait=[], inc=[(mm, 1)])
        mmv = mm.now
        for cc in range(CC):
            pg.emit("vector", lambda e, cc=cc: e.tensor_tensor(
                t1[:, :], x1t[:, cc, :], c.pbu[:, :], ALU.add),
                wait=[(mm, mmv)] + ([(odma, odma.now)] if cc == 0 else []),
                drain=True)
            pg.emit("vector", lambda e, cc=cc: e.tensor_tensor(
                xat[:, cc, :], t1[:, :], c.pbs[:, :], ALU.mult), drain=True)
            pg.emit("vector", lambda e, cc=cc, t0=t0: e.tensor_copy(
                xn_bf[:, cc, t0:t0 + 512], xat[:, cc, :]), drain=True, inc=[(evv, 1)])
        pg.emit("gpsimd", lambda e, t0=t0: e.dma_start(
            out=xa_hbm[:, :, t0:t0 + 512].rearrange("cc p t -> p cc t"),
            in_=xat[:, :, :]), wait=[(evv, evv.now)], inc=[(odma, 16)])

    c.last_done = [(odma, odma.now), (evv, evv.now)]


def _rms_rows(c, pg, src_tile, sq, rows, mm, evv, evs, ld_w, eps_key):
    oneb = c.res["ones_col_bf"]
    cval = c.res["cval"]
    for cc in range(CC):
        pg.emit("scalar", lambda e, cc=cc: e.activation(
            sq[:, :], src_tile[:, cc, :], AF.Square),
            wait=list(ld_w) + [(mm, mm.now)],
            drain=True, inc=[(evs, 1)])
        pg.emit("tensor", lambda e, cc=cc: e.matmul(
            c.pbu[0:1, :], oneb[:, 0:1], sq[:, :],
            start=(cc == 0), stop=(cc == CC - 1)),
            wait=[(evs, evs.now)], inc=[(mm, 1)])
    pg.emit("vector", lambda e: e.tensor_copy(rows[:, 0, :], c.pbu[0:1, :]),
            wait=[(mm, mm.now)], drain=True, inc=[(evv, 1)])
    pg.emit("vector", lambda e: e.tensor_scalar_mul(rows[:, 1, :], rows[:, 0, :],
                                                    1.0 / C), drain=True, inc=[(evv, 1)])
    pg.emit("scalar", lambda e: e.activation(
        rows[:, 2, :], rows[:, 1, :], AF.Sqrt,
        bias=cval[0:1, CVAL[eps_key]:CVAL[eps_key] + 1]),
        wait=[(evv, evv.now)], drain=True, inc=[(evs, 1)])
    pg.emit("vector", lambda e: e.reciprocal(rows[:, 3, :], rows[:, 2, :]),
            wait=[(evs, evs.now)], drain=True, inc=[(evv, 1)])


def emit_B2(c):
    nc, pg = c.nc, c.pg
    c.arena_reset()
    xat = c.arena_alloc("b2_xat", [P, CC, 512], F32)
    hb = c.arena_alloc("b2_h", [P, CC, 513], F32)
    hl = c.arena_alloc("b2_hl", [P, CC, 1], F32)
    sq = c.arena_alloc("b2_sq", [P, 512], BF16)
    rows = c.arena_alloc("b2_rows", [1, 4, 512], F32)
    d = c.arena_alloc("b2_d", [P, 512], F32)
    xk = c.arena_alloc("b2_xk", [P, CC, 512], BF16)
    xv = c.arena_alloc("b2_xv", [P, CC, 512], BF16)
    xr = c.arena_alloc("b2_xr", [P, CC, 512], BF16)
    wb = c.arena_alloc("b2_wb", [P, CC, C], BF16)
    kf = c.arena_alloc("b2_kf", [P, CC, 512], F32)
    vf = c.arena_alloc("b2_vf", [P, CC, 512], F32)
    rcc = c.arena_alloc("b2_rcc", [P, 512], F32)
    rwkv = c.arena_alloc("b2_rwkv", [P, CC, 512], BF16)
    abuf_off = c.arena_off
    abuf = c.arena_alloc("b2_ab", [P, 513], F32)
    bbuf = c.arena_alloc("b2_bb", [P, 513], F32)
    ekt = c.arena_alloc("b2_ekt", [P, 512], F32)
    ek = c.arena_alloc("b2_ek", [P, 512], F32)
    ekv = c.arena_alloc("b2_ekv", [P, 512], F32)
    num = ekv
    den = ek
    t5 = c.arena_alloc_at("b2_t5", [P, 512], F32, abuf_off)
    ast = c.arena_alloc("b2_ast", [P, CC], F32)
    bst = c.arena_alloc("b2_bst", [P, CC], F32)

    for nm, t in (("b2_kf", kf), ("b2_vf", vf), ("b2_rwkv", rwkv),
                  ("b2_xk", xk), ("b2_h", hb), ("b2_ast", ast), ("b2_bst", bst)):
        c.res[nm] = t

    ld = pg.sem("b2_ld")
    wdma = pg.sem("b2_wdma")
    mm = pg.sem("b2_mm")
    evv = pg.sem("b2_evv")
    evs = pg.sem("b2_evs")
    odma = pg.sem("b2_odma")

    xa_hbm = c.res["xa_hbm"].ap()
    vecs = c.res["vecs"]
    dvec = c.res["dvec"]
    cval = c.res["cval"]
    onerf = c.res["ones_row_f"]
    prev = list(c.last_done)

    pg.emit("gpsimd", lambda e: e.memset(ast[:, :], 0.0), wait=prev, drain=True,
            inc=[(odma, 1)])
    pg.emit("gpsimd", lambda e: e.memset(bst[:, :], 0.0), drain=True, inc=[(odma, 1)])
    pg.emit("gpsimd", lambda e: e.memset(hl[:, :, :], 0.0), drain=True, inc=[(odma, 1)])
    init_v = odma.now

    banks = Banks([c.pb01[:, 0:512], c.pb01[:, 512:1024], c.pb2[:, :], c.pb3[:, :]])

    for ti in range(4):
        t0 = ti * 512
        w0 = [(odma, init_v)] if ti == 0 else [(evv, evv.now), (odma, odma.now)]
        pg.emit("sync", lambda e, t0=t0: e.dma_start(
            out=xat[:, :, :],
            in_=xa_hbm[:, :, t0:t0 + 512].rearrange("cc p t -> p cc t")),
            wait=w0 + list(prev if ti == 0 else []), inc=[(ld, 16)])
        ldv = ld.now
        _rms_rows(c, pg, xat, sq, rows, mm, evv, evs, [(ld, ldv)], "eps_rms")
        pg.emit("tensor", lambda e: e.matmul(c.pbs[:, :], onerf[0:1, :], rows[:, 3, :],
                                             start=True, stop=True),
                wait=[(evv, evv.now)], inc=[(mm, 1)])
        mmv = mm.now
        for cc in range(CC):
            pg.emit("vector", lambda e, cc=cc: e.scalar_tensor_tensor(
                hb[:, cc, 1:513], xat[:, cc, :], vecs[:, VI["ln1_w"], cc:cc + 1],
                c.pbs[:, :], ALU.mult, ALU.mult),
                wait=[(mm, mmv)], drain=True, inc=[(evv, 1)])
        pg.emit("vector", lambda e: e.tensor_copy(hb[:, :, 0:1], hl[:, :, :]),
                drain=True, inc=[(evv, 1)])
        for cc in range(CC):
            pg.emit("vector", lambda e, cc=cc: e.tensor_tensor(
                d[:, :], hb[:, cc, 1:513], hb[:, cc, 0:512], ALU.subtract),
                drain=True)
            for mix, dst in (("time_mix_k", xk), ("time_mix_v", xv), ("time_mix_r", xr)):
                pg.emit("vector", lambda e, cc=cc, mix=mix, dst=dst: e.scalar_tensor_tensor(
                    dst[:, cc, :], d[:, :], vecs[:, VI[mix], cc:cc + 1],
                    hb[:, cc, 0:512], ALU.mult, ALU.add), drain=True, inc=[(evv, 1)])
        mixes_done = evv.now
        pg.emit("vector", lambda e: e.tensor_copy(hl[:, :, :], hb[:, :, 512:513]),
                drain=True, inc=[(evv, 1)])

        for wname, xsrc, dst in (("L1_attn_W_key", xk, kf),
                                 ("L1_attn_W_value", xv, vf)):
            src_ap = c.inp[wname].ap().rearrange("(kc p) f -> p kc f", p=P)
            pg.emit("gpsimd", lambda e, src_ap=src_ap: e.dma_start(
                out=wb[:, :, :], in_=src_ap), wait=[(mm, mm.now)], inc=[(wdma, 16)])
            wdv = wdma.now
            for co in range(CC):
                jj, pp, ww = banks.acquire()
                pg.emit("tensor", lambda e, co=co, xsrc=xsrc, pp=pp: _kmm2(
                    e, pp, wb, xsrc, co),
                    wait=[(wdma, wdv), (evv, mixes_done)] + ww, inc=[(mm, 1)])
                mmv2 = mm.now
                pg.emit("vector", lambda e, dst=dst, co=co, pp=pp: e.tensor_copy(
                    dst[:, co, :], pp), wait=[(mm, mmv2)], drain=True, inc=[(evv, 1)])
                banks.release(jj, evv)
        src_ap = c.inp["L1_attn_W_recept"].ap().rearrange("(kc p) f -> p kc f", p=P)
        pg.emit("gpsimd", lambda e, src_ap=src_ap: e.dma_start(
            out=wb[:, :, :], in_=src_ap), wait=[(mm, mm.now)], inc=[(wdma, 16)])
        wd_rec = wdma.now

        for cc in range(CC):
            pg.emit("vector", lambda e, cc=cc: e.tensor_scalar(
                ekt[:, :], kf[:, cc, :], vecs[:, VI["time_first"], cc:cc + 1], 20.0,
                ALU.add, ALU.min), wait=[(evs, evs.now), (evv, evv.now)], drain=True)
            pg.emit("vector", lambda e: e.tensor_scalar_max(ekt[:, :], ekt[:, :], -20.0),
                    drain=True, inc=[(evv, 1)])
            pg.emit("scalar", lambda e: e.activation(ekt[:, :], ekt[:, :], AF.Exp),
                    wait=[(evv, evv.now)], drain=True, inc=[(evs, 1)])
            pg.emit("vector", lambda e, cc=cc: e.tensor_scalar(
                ek[:, :], kf[:, cc, :], 20.0, -20.0, ALU.min, ALU.max), drain=True,
                inc=[(evv, 1)])
            pg.emit("scalar", lambda e: e.activation(ek[:, :], ek[:, :], AF.Exp),
                    wait=[(evv, evv.now)], drain=True, inc=[(evs, 1)])
            pg.emit("vector", lambda e, cc=cc: e.tensor_tensor(
                ekv[:, :], ek[:, :], vf[:, cc, :], ALU.mult),
                wait=[(evs, evs.now)], drain=True)
            pg.emit("vector", lambda e, cc=cc: e.tensor_copy(abuf[:, 0:1],
                                                             ast[:, cc:cc + 1]),
                    drain=True)
            pg.emit("vector", lambda e, cc=cc: e.tensor_tensor_scan(
                abuf[:, 1:513], dvec[:, DV["ew"], cc:cc + 1].broadcast_to([P, 512]),
                ekv[:, :], ast[:, cc:cc + 1], ALU.mult, ALU.add), drain=True)
            pg.emit("vector", lambda e, cc=cc: e.tensor_copy(ast[:, cc:cc + 1],
                                                             abuf[:, 512:513]),
                    drain=True)
            pg.emit("vector", lambda e, cc=cc: e.tensor_copy(bbuf[:, 0:1],
                                                             bst[:, cc:cc + 1]),
                    drain=True)
            pg.emit("vector", lambda e, cc=cc: e.tensor_tensor_scan(
                bbuf[:, 1:513], dvec[:, DV["ew"], cc:cc + 1].broadcast_to([P, 512]),
                ek[:, :], bst[:, cc:cc + 1], ALU.mult, ALU.add), drain=True)
            pg.emit("vector", lambda e, cc=cc: e.tensor_copy(bst[:, cc:cc + 1],
                                                             bbuf[:, 512:513]),
                    drain=True)
            pg.emit("vector", lambda e, cc=cc: e.tensor_tensor(
                num[:, :], ekt[:, :], vf[:, cc, :], ALU.mult), drain=True)
            pg.emit("vector", lambda e: e.tensor_tensor(
                num[:, :], num[:, :], abuf[:, 0:512], ALU.add), drain=True)
            pg.emit("vector", lambda e: e.scalar_tensor_tensor(
                den[:, :], ekt[:, :], 1e-8, bbuf[:, 0:512], ALU.add, ALU.add),
                drain=True)
            pg.emit("vector", lambda e: e.reciprocal(t5[:, :], den[:, :]), drain=True)
            pg.emit("vector", lambda e, cc=cc: e.tensor_tensor(
                num[:, :], num[:, :], t5[:, :], ALU.mult), drain=True,
                inc=[(evv, 1)])
            jj, pp, ww = banks.acquire()
            pg.emit("tensor", lambda e, cc=cc, pp=pp: _kmm2(e, pp, wb, xr, cc),
                    wait=[(wdma, wd_rec)] + ww, inc=[(mm, 1)])
            mmv3 = mm.now
            pg.emit("scalar", lambda e, pp=pp: e.activation(rcc[:, :], pp, AF.Sigmoid),
                    wait=[(mm, mmv3), (evv, evv.now)], drain=True, inc=[(evs, 1)])
            banks.release(jj, evs)
            pg.emit("vector", lambda e, cc=cc: e.tensor_tensor(
                rwkv[:, cc, :], num[:, :], rcc[:, :], ALU.mult),
                wait=[(evs, evs.now)], drain=True, inc=[(evv, 1)])
        scan_done = evv.now

        src_ap = c.inp["L1_attn_W_out"].ap().rearrange("(kc p) f -> p kc f", p=P)
        pg.emit("gpsimd", lambda e, src_ap=src_ap: e.dma_start(
            out=wb[:, :, :], in_=src_ap), wait=[(mm, mm.now)], inc=[(wdma, 16)])
        wdv = wdma.now
        for co in range(CC):
            jj, pp, ww = banks.acquire()
            pg.emit("tensor", lambda e, co=co, pp=pp: _kmm2(e, pp, wb, rwkv, co),
                    wait=[(wdma, wdv), (evv, scan_done)] + ww, inc=[(mm, 1)])
            mmv2 = mm.now
            pg.emit("vector", lambda e, co=co, pp=pp: e.tensor_tensor(
                xat[:, co, :], xat[:, co, :], pp, ALU.add),
                wait=[(mm, mmv2)], drain=True, inc=[(evv, 1)])
            banks.release(jj, evv)
        pg.emit("gpsimd", lambda e, t0=t0: e.dma_start(
            out=xa_hbm[:, :, t0:t0 + 512].rearrange("cc p t -> p cc t"),
            in_=xat[:, :, :]), wait=[(evv, evv.now), (odma, odma.now)],
            inc=[(odma, 16)])

    c.last_done = [(odma, odma.now), (evv, evv.now), (evs, evs.now), (mm, mm.now)]


def _kmm2(e, psum, wbuf, act, co):
    r = None
    for kc in range(CC):
        r = e.matmul(psum, wbuf[:, kc, co * P:(co + 1) * P], act[:, kc, :],
                     start=(kc == 0), stop=(kc == CC - 1))
    return r


def emit_B3(c):
    nc, pg = c.nc, c.pg
    c.arena_reset()
    xat = c.arena_alloc("b3_xat", [P, CC, 512], F32)
    hb = c.arena_alloc("b3_h", [P, CC, 513], F32)
    hl = c.arena_alloc("b3_hl", [P, CC, 1], F32)
    sq = c.arena_alloc("b3_sq", [P, 512], BF16)
    rows = c.arena_alloc("b3_rows", [1, 4, 512], F32)
    d = c.arena_alloc("b3_d", [P, 512], F32)
    xk2 = c.arena_alloc("b3_xk2", [P, CC, 512], BF16)
    xr2 = c.arena_alloc("b3_xr2", [P, CC, 512], BF16)
    wk2h = c.arena_alloc("b3_wk2", [P, CC, 2048], BF16)
    r2 = c.arena_alloc("b3_r2", [P, CC, 512], BF16)
    k2 = c.arena_alloc("b3_k2", [P, 32, 512], BF16)
    wv2 = [c.arena_alloc(f"b3_wv2{i}", [P, C], BF16) for i in range(2)]
    rl = c.arena_alloc("b3_rl", [P, 512], F32)
    t5 = c.arena_alloc("b3_t5", [P, 512], F32)
    wr2 = c.arena_alloc_at("b3_wr2", [P, CC, C], BF16,
                           nc.lookup_mloc(wk2h).addr - c.arena_addr)

    ld = pg.sem("b3_ld")
    wdma = pg.sem("b3_wdma")
    mm = pg.sem("b3_mm")
    evv = pg.sem("b3_evv")
    evs = pg.sem("b3_evs")
    odma = pg.sem("b3_odma")

    xa_hbm = c.res["xa_hbm"].ap()
    vecs = c.res["vecs"]
    onerf = c.res["ones_row_f"]
    prev = list(c.last_done)

    pg.emit("gpsimd", lambda e: e.memset(hl[:, :, :], 0.0), wait=prev, drain=True,
            inc=[(odma, 1)])
    init_v = odma.now

    banks = Banks([c.pb01[:, 0:512], c.pb01[:, 512:1024], c.pb2[:, :], c.pb3[:, :]])

    for ti in range(4):
        t0 = ti * 512
        w0 = [(odma, init_v)] + list(prev) if ti == 0 else [(evv, evv.now), (odma, odma.now)]
        pg.emit("sync", lambda e, t0=t0: e.dma_start(
            out=xat[:, :, :],
            in_=xa_hbm[:, :, t0:t0 + 512].rearrange("cc p t -> p cc t")),
            wait=w0, inc=[(ld, 16)])
        ldv = ld.now
        _rms_rows(c, pg, xat, sq, rows, mm, evv, evs, [(ld, ldv)], "eps_rms")
        pg.emit("tensor", lambda e: e.matmul(c.pbs[:, :], onerf[0:1, :], rows[:, 3, :],
                                             start=True, stop=True),
                wait=[(evv, evv.now)], inc=[(mm, 1)])
        mmv = mm.now
        for cc in range(CC):
            pg.emit("vector", lambda e, cc=cc: e.scalar_tensor_tensor(
                hb[:, cc, 1:513], xat[:, cc, :], vecs[:, VI["ln2_w"], cc:cc + 1],
                c.pbs[:, :], ALU.mult, ALU.mult),
                wait=[(mm, mmv)], drain=True, inc=[(evv, 1)])
        pg.emit("vector", lambda e: e.tensor_copy(hb[:, :, 0:1], hl[:, :, :]),
                drain=True, inc=[(evv, 1)])
        for cc in range(CC):
            pg.emit("vector", lambda e, cc=cc: e.tensor_tensor(
                d[:, :], hb[:, cc, 1:513], hb[:, cc, 0:512], ALU.subtract),
                drain=True)
            for mix, dst in (("time_mix_k2", xk2), ("time_mix_r2", xr2)):
                pg.emit("vector", lambda e, cc=cc, mix=mix, dst=dst: e.scalar_tensor_tensor(
                    dst[:, cc, :], d[:, :], vecs[:, VI[mix], cc:cc + 1],
                    hb[:, cc, 0:512], ALU.mult, ALU.add), drain=True, inc=[(evv, 1)])
        mixes_done = evv.now
        pg.emit("vector", lambda e: e.tensor_copy(hl[:, :, :], hb[:, :, 512:513]),
                drain=True, inc=[(evv, 1)])

        src_ap = c.inp["L1_attn_W_recept2"].ap().rearrange("(kc p) f -> p kc f", p=P)
        pg.emit("gpsimd", lambda e, src_ap=src_ap: e.dma_start(
            out=wr2[:, :, :], in_=src_ap), wait=[(mm, mm.now)], inc=[(wdma, 16)])
        wdv = wdma.now
        for co in range(CC):
            jj, pp, ww = banks.acquire()
            pg.emit("tensor", lambda e, co=co, pp=pp: _kmm2(e, pp, wr2, xr2, co),
                    wait=[(wdma, wdv), (evv, mixes_done)] + ww, inc=[(mm, 1)])
            pg.emit("scalar", lambda e, co=co, pp=pp: e.activation(
                r2[:, co, :], pp, AF.Sigmoid),
                wait=[(mm, mm.now)], drain=True, inc=[(evs, 1)])
            banks.release(jj, evs)

        for fh in range(2):
            src_ap = c.inp["L1_attn_W_key2"].ap()[:, fh * 2048:(fh + 1) * 2048]
            src_ap = src_ap.rearrange("(kc p) f -> p kc f", p=P)
            pg.emit("gpsimd", lambda e, src_ap=src_ap: e.dma_start(
                out=wk2h[:, :, :], in_=src_ap), wait=[(mm, mm.now)], inc=[(wdma, 16)])
            wdv = wdma.now
            for f2 in range(FC):
                jj, pp, ww = banks.acquire()
                pg.emit("tensor", lambda e, f2=f2, pp=pp: _kmm2(e, pp, wk2h, xk2, f2),
                        wait=[(wdma, wdv)] + ww, inc=[(mm, 1)])
                mmv2 = mm.now
                pg.emit("scalar", lambda e, pp=pp: e.activation(rl[:, :], pp, AF.Relu),
                        wait=[(mm, mmv2), (evs, evs.now)], drain=True, inc=[(evs, 1)])
                banks.release(jj, evs)
                pg.emit("scalar", lambda e, fh=fh, f2=f2: e.activation(
                    k2[:, fh * FC + f2, :], rl[:, :], AF.Square),
                    drain=True, inc=[(evs, 1)])
        k2_done = evs.now

        for ch in range(2):
            pslots = [c.pb01[:, 0:512], c.pb01[:, 512:1024], c.pb2[:, :], c.pb3[:, :]]
            for kc2 in range(32):
                wvb = wv2[kc2 % 2]
                src_ap = c.inp["L1_attn_W_value2"].ap()[kc2 * P:(kc2 + 1) * P, :]
                wwv = [(mm, mm.now)] if kc2 >= 2 else ([(mm, mm.now)] if ch > 0 else [(evs, k2_done), (mm, mm.now)])
                pg.emit("gpsimd", lambda e, wvb=wvb, src_ap=src_ap: e.dma_start(
                    out=wvb[:, :], in_=src_ap), wait=wwv, inc=[(wdma, 16)])
                wdv2 = wdma.now
                def y2mm(e, ch=ch, kc2=kc2, wvb=wvb, pslots=pslots):
                    r = None
                    for q in range(4):
                        ct = ch * 4 + q
                        r = e.matmul(pslots[q], wvb[:, ct * P:(ct + 1) * P],
                                     k2[:, kc2, :], start=(kc2 == 0), stop=(kc2 == 31))
                    return r
                pg.emit("tensor", y2mm, wait=[(wdma, wdv2), (evs, k2_done)] +
                        ([(evv, evv.now)] if kc2 == 0 else []), inc=[(mm, 1)])
            mmv3 = mm.now
            for q in range(4):
                ct = ch * 4 + q
                pg.emit("vector", lambda e, ct=ct, q=q, pslots=pslots: e.tensor_tensor(
                    t5[:, :], pslots[q], r2[:, ct, :], ALU.mult),
                    wait=[(mm, mmv3)], drain=True)
                pg.emit("vector", lambda e, ct=ct: e.tensor_tensor(
                    xat[:, ct, :], xat[:, ct, :], t5[:, :], ALU.add), drain=True,
                    inc=[(evv, 1)])
                banks.free[q] = (evv, evv.now)
            banks.i = 0
        pg.emit("gpsimd", lambda e, t0=t0: e.dma_start(
            out=xa_hbm[:, :, t0:t0 + 512].rearrange("cc p t -> p cc t"),
            in_=xat[:, :, :]), wait=[(evv, evv.now), (odma, odma.now)],
            inc=[(odma, 16)])

    c.last_done = [(odma, odma.now), (evv, evv.now), (evs, evs.now), (mm, mm.now)]


def emit_B4(c):
    nc, pg = c.nc, c.pg
    c.arena_reset()
    wgc = [c.arena_alloc(f"b4_wgc{i}", [P, C], BF16) for i in range(2)]
    wdc = c.arena_alloc("b4_wdc", [P, HID], BF16)
    tb = [c.arena_alloc(f"b4_tb{i}", [P, P], BF16) for i in range(2)]

    ld = pg.sem("b4_ld")
    mm = pg.sem("b4_mm")
    evv = pg.sem("b4_evv")
    evs = pg.sem("b4_evs")
    odma = pg.sem("b4_odma")
    wdma = pg.sem("b4_wdma")

    idb = c.res["ident_bf"]
    cval = c.res["cval"]
    xn_bf = c.res["xn_bf"]
    vecs = c.res["vecs"]
    xa_hbm = c.res["xa_hbm"].ap()
    x1_hbm = c.res["x1_hbm"].ap()
    y_ap = c.res["y"].ap()
    wgT = c.res["wgT_hbm"].ap()
    wuT = c.res["wuT_hbm"].ap()
    wdT = c.res["wdT_hbm"].ap()
    prev = list(c.last_done)

    tpi = [0]

    def tp_block(src_fn, dst_ap_fn, wait_first):
        i = tpi[0]
        tpi[0] += 1
        ps = c.psum_tp if i % 2 == 0 else c.psum_tp2
        tbb = tb[i % 2]
        pg.emit("tensor", lambda e, ps=ps, src_fn=src_fn: e.transpose(
            ps[:, :], src_fn(), idb[:, :]),
            wait=wait_first + [(odma, max(0, 16 * (i - 1)))], inc=[(mm, 1)])
        pg.emit("vector", lambda e, ps=ps, tbb=tbb: e.tensor_copy(tbb[:, :], ps[:, :]),
                wait=[(mm, mm.now)], drain=True, inc=[(evv, 1)])
        pg.emit("sync", lambda e, tbb=tbb, dst_ap_fn=dst_ap_fn: e.dma_start(
            out=dst_ap_fn(), in_=tbb[:, :]), wait=[(evv, evv.now)], inc=[(odma, 16)])

    for wname, dstT in (("L1_ffn_w_gate", wgT), ("L1_ffn_w_up", wuT)):
        for hc in range(HC):
            wgb = wgc[hc % 2]
            src_ap = c.inp[wname].ap()[hc * P:(hc + 1) * P, :]
            w0 = list(prev) if (wname == "L1_ffn_w_gate" and hc == 0) else [(mm, mm.now)]
            pg.emit("gpsimd", lambda e, wgb=wgb, src_ap=src_ap: e.dma_start(
                out=wgb[:, :], in_=src_ap), wait=w0, inc=[(ld, 16)])
            ldv = ld.now
            for dk in range(CC):
                tp_block(lambda wgb=wgb, dk=dk: wgb[:, dk * P:(dk + 1) * P],
                         lambda dstT=dstT, dk=dk, hc=hc: dstT[dk * P:(dk + 1) * P,
                                                              hc * P:(hc + 1) * P],
                         [(ld, ldv)])
    for dk in range(CC):
        src_ap = c.inp["L1_ffn_w_down"].ap()[dk * P:(dk + 1) * P, :]
        pg.emit("gpsimd", lambda e, src_ap=src_ap: e.dma_start(
            out=wdc[:, :], in_=src_ap), wait=[(mm, mm.now)], inc=[(ld, 16)])
        ldv = ld.now
        for hc in range(HC):
            tp_block(lambda hc=hc: wdc[:, hc * P:(hc + 1) * P],
                     lambda dk=dk, hc=hc: wdT[hc * P:(hc + 1) * P,
                                              dk * P:(dk + 1) * P],
                     [(ld, ldv)])
    b4a_done = odma.now

    c.arena_reset()
    xat = c.arena_alloc("b4_xat", [P, CC, 512], F32)
    x1t = c.arena_alloc("b4_x1t", [P, CC, 512], F32)
    hdn = c.arena_alloc("b4_hdn", [P, HC, 512], BF16)
    wgTc = [c.arena_alloc(f"b4_wgT{i}", [P, CC, P], BF16) for i in range(2)]
    wuTc = [c.arena_alloc(f"b4_wuT{i}", [P, CC, P], BF16) for i in range(2)]
    wdTc = [c.arena_alloc(f"b4_wdT{i}", [P, HC, P], BF16) for i in range(2)]
    sgs = c.arena_alloc("b4_sgs", [P, 512], BF16)
    sd = c.arena_alloc("b4_sd", [P, 512], F32)
    t6 = c.arena_alloc("b4_t6", [P, 512], F32)
    ob = c.arena_alloc("b4_ob", [P, CC, 512], F32)
    otm = c.arena_alloc("b4_otm", [P, 4, C], F32)

    banks = Banks([c.pb01[:, 0:512], c.pb01[:, 512:1024], c.pb2[:, :], c.pb3[:, :]])
    iv = CVAL["inv127"]

    for ti in range(4):
        t0 = ti * 512
        w0 = [(odma, b4a_done)] if ti == 0 else [(evv, evv.now)]
        pg.emit("sync", lambda e, t0=t0: e.dma_start(
            out=xat[:, :, :],
            in_=xa_hbm[:, :, t0:t0 + 512].rearrange("cc p t -> p cc t")),
            wait=w0, inc=[(ld, 16)])
        pg.emit("sync", lambda e, t0=t0: e.dma_start(
            out=x1t[:, :, :],
            in_=x1_hbm[:, :, t0:t0 + 512].rearrange("cc p t -> p cc t")),
            wait=[], inc=[(ld, 16)])
        ldv = ld.now
        for hc in range(HC):
            wga = wgTc[hc % 2]
            wua = wuTc[hc % 2]
            wwar = [(mm, mm.now)] if hc >= 2 or ti > 0 else []
            pg.emit("gpsimd", lambda e, wga=wga, hc=hc: e.dma_start(
                out=wga[:, :, :],
                in_=wgT[:, hc * P:(hc + 1) * P].rearrange("(dk p) h -> p dk h", p=P)),
                wait=[(odma, b4a_done)] + wwar, inc=[(wdma, 16)])
            pg.emit("gpsimd", lambda e, wua=wua, hc=hc: e.dma_start(
                out=wua[:, :, :],
                in_=wuT[:, hc * P:(hc + 1) * P].rearrange("(dk p) h -> p dk h", p=P)),
                wait=[], inc=[(wdma, 16)])
            wdv = wdma.now
            jg, pG, wG = banks.acquire()
            ju, pU, wU = banks.acquire()
            def gatemm(e, wga=wga, t0=t0, pG=pG):
                r = None
                for kc in range(CC):
                    r = e.matmul(pG, wga[:, kc, :], xn_bf[:, kc, t0:t0 + 512],
                                 start=(kc == 0), stop=(kc == CC - 1))
                return r
            def upmm(e, wua=wua, t0=t0, pU=pU):
                r = None
                for kc in range(CC):
                    r = e.matmul(pU, wua[:, kc, :], xn_bf[:, kc, t0:t0 + 512],
                                 start=(kc == 0), stop=(kc == CC - 1))
                return r
            pg.emit("tensor", gatemm, wait=[(wdma, wdv), (ld, ldv)] + wG + wU,
                    inc=[(mm, 1)])
            pg.emit("tensor", upmm, wait=[], inc=[(mm, 1)])
            mmv = mm.now
            pg.emit("scalar", lambda e, pG=pG: e.activation(
                sgs[:, :], pG, AF.Silu, scale=cval[:, iv:iv + 1]),
                wait=[(mm, mmv), (evv, evv.now)], drain=True, inc=[(evs, 1)])
            banks.release(jg, evs)
            pg.emit("vector", lambda e, hc=hc, pU=pU: e.scalar_tensor_tensor(
                hdn[:, hc, :], pU, cval[:, iv:iv + 1], sgs[:, :],
                ALU.mult, ALU.mult),
                wait=[(evs, evs.now)], drain=True, inc=[(evv, 1)])
            banks.release(ju, evv)
        hdn_done = evv.now

        for ct in range(CC):
            wda = wdTc[ct % 2]
            wwar = [(mm, mm.now)] if (ct >= 2 or ti > 0) else []
            pg.emit("gpsimd", lambda e, wda=wda, ct=ct: e.dma_start(
                out=wda[:, :, :],
                in_=wdT[:, ct * P:(ct + 1) * P].rearrange("(hk p) d -> p hk d", p=P)),
                wait=wwar, inc=[(wdma, 16)])
            wdv = wdma.now
            jd, pD, wD = banks.acquire()
            def downmm(e, wda=wda, pD=pD):
                r = None
                for hk in range(HC):
                    r = e.matmul(pD, wda[:, hk, :], hdn[:, hk, :],
                                 start=(hk == 0), stop=(hk == HC - 1))
                return r
            pg.emit("tensor", downmm, wait=[(wdma, wdv), (evv, hdn_done)] + wD,
                    inc=[(mm, 1)])
            mmv = mm.now
            pg.emit("scalar", lambda e, pD=pD: e.activation(
                sd[:, :], pD, AF.Silu, scale=cval[:, iv:iv + 1]),
                wait=[(mm, mmv), (evv, evv.now)], drain=True, inc=[(evs, 1)])
            banks.release(jd, evs)
            pg.emit("vector", lambda e, ct=ct: e.tensor_tensor(
                t6[:, :], sd[:, :], xat[:, ct, :], ALU.mult),
                wait=[(evs, evs.now)], drain=True)
            pg.emit("vector", lambda e, ct=ct: e.tensor_tensor(
                ob[:, ct, :], t6[:, :], x1t[:, ct, :], ALU.add), drain=True,
                inc=[(evv, 1)])
        ob_done = evv.now

        for j in range(4):
            for ct in range(CC):
                wtp = [(evv, evv.now)]
                if ct == 0:
                    wtp.append((odma, odma.now))
                pg.emit("tensor", lambda e, j=j, ct=ct: e.transpose(
                    c.pbu[:, 0:P], ob[:, ct, j * P:(j + 1) * P],
                    c.res["ident_f"][:, :]),
                    wait=wtp, inc=[(mm, 1)])
                pg.emit("vector", lambda e, j=j, ct=ct: e.tensor_copy(
                    otm[:, j, ct * P:(ct + 1) * P], c.pbu[:, 0:P]),
                    wait=[(mm, mm.now)], drain=True, inc=[(evv, 1)])
            pg.emit("sync", lambda e, j=j, t0=t0: e.dma_start(
                out=y_ap[t0 + j * P: t0 + (j + 1) * P, :], in_=otm[:, j, :]),
                wait=[(evv, evv.now)], inc=[(odma, 16)])

    c.last_done = [(odma, odma.now), (evv, evv.now), (evs, evs.now)]
    c.final_waits.append((odma, odma.now))



def kernel(**inputs):
    nc, c = build_program()
    in_maps = []
    x = np.ascontiguousarray(inputs["L0_attn_x"])
    for b in range(8):
        m = {"x": np.ascontiguousarray(x[b])}
        for k, v in inputs.items():
            if k == "L0_attn_x":
                continue
            m[k] = np.ascontiguousarray(v)
        in_maps.append(m)
    res = run_bass_kernel_spmd(nc, in_maps, core_ids=list(range(8)))
    return np.stack([res.results[b]["y"] for b in range(8)], axis=0)


# version 03
# speedup vs baseline: 1.1208x; 1.1208x over previous
import numpy as np
import ml_dtypes
import concourse.bass as bass
import concourse.mybir as mybir
from concourse.bass_utils import run_bass_kernel_spmd

F32 = mybir.dt.float32
BF16 = mybir.dt.bfloat16
I32 = mybir.dt.int32
AF = mybir.ActivationFunctionType
ALU = mybir.AluOpType
AX = mybir.AxisListType

P = 128
T = 2048
CC = 8
C = 1024
NHEAD = 16
DHEAD = 64
WIN = 256
NEXP = 8
FF = 2048
FC = FF // P
I2 = 4096
HID = 3072
HC = HID // P

ENGINES = ("sync", "gpsimd", "tensor", "vector", "scalar")


class CB:

    def __init__(self, nc, name):
        self.h = nc.alloc_semaphore(name)
        self.n = 0

    def inc(self, k=1):
        self.n += k
        return self.n

    @property
    def now(self):
        return self.n


class Prog:

    def __init__(self, nc):
        self.nc = nc
        self.steps = {e: [] for e in ENGINES}

    def sem(self, name):
        return CB(self.nc, name)

    def emit(self, eng, fn, wait=(), inc=(), drain=False):
        incs = []
        for cb, k in inc:
            cb.inc(k)
            incs.append((cb, k))
        self.steps[eng].append((fn, tuple(wait), tuple(incs), drain))

    def finalize(self):
        nc = self.nc
        with nc.Block() as block:
            for name in ENGINES:
                steps = self.steps[name]
                section = getattr(block, name)

                def body(eng, steps=steps):
                    for fn, waits, incs, drain in steps:
                        for cb, v in waits:
                            eng.wait_ge(cb.h, v)
                        if drain:
                            eng.drain()
                        inst = fn(eng)
                        if incs:
                            assert inst is not None, "inc on step with no instruction"
                            for cb, k in incs:
                                inst.then_inc(cb.h, k)

                section(body)


class Ctx:

    def __init__(self, nc):
        self.nc = nc
        self.pg = Prog(nc)
        self.res = {}
        self.arena = None
        self.arena_off = 0
        self.arena_marks = []

    def arena_alloc(self, name, shape, dtype):
        nc = self.nc
        per_part = int(np.prod(shape[1:])) * mybir.dt.size(dtype)
        per_part = (per_part + 31) & ~31
        assert self.arena_off + per_part <= self.arena_bytes, (
            f"arena overflow: {name} needs {per_part} at {self.arena_off}, cap {self.arena_bytes}")
        h = nc.virtual_sbuf_tensor(name, list(shape), dtype)
        nc._tensor(h.name, list(h.shape), h.dtype, type="SB")
        mloc = nc.lookup_mloc(h)
        mloc.allocated = True
        mloc.addr = self.arena_addr + self.arena_off
        self.arena_off += per_part
        return h

    def arena_alloc_at(self, name, shape, dtype, offset):
        nc = self.nc
        h = nc.virtual_sbuf_tensor(name, list(shape), dtype)
        nc._tensor(h.name, list(h.shape), h.dtype, type="SB")
        mloc = nc.lookup_mloc(h)
        mloc.allocated = True
        mloc.addr = self.arena_addr + offset
        return h

    def arena_reset(self):
        self.arena_off = 0
        self.arena_marks = []

    def arena_mark(self):
        self.arena_marks.append(self.arena_off)

    def arena_release(self):
        self.arena_off = self.arena_marks.pop()


def dt_np(dtype):
    return {F32: np.float32, BF16: ml_dtypes.bfloat16}[dtype]



VEC_NAMES = [
    "L0_attn_bq", "L0_attn_bk", "L0_attn_bv", "L0_attn_bd",
    "L1_attn_ln1_w", "L1_attn_ln2_w", "L1_attn_time_decay", "L1_attn_time_first",
    "L1_attn_time_mix_k", "L1_attn_time_mix_v", "L1_attn_time_mix_r",
    "L1_attn_time_mix_k2", "L1_attn_time_mix_r2",
]
VI = {n.split("_", 2)[-1]: i for i, n in enumerate(VEC_NAMES)}
DV = {"bq8": 0, "ew": 1}
ND = 2
CVAL = {"eps_ln": 0, "eps_rms": 1, "inv127": 2, "eps_wkv": 3}
CVAL_VALUES = {"eps_ln": 1e-5, "eps_rms": 1e-6, "inv127": 1.0 / 127.0, "eps_wkv": 1e-8}

ALL_PHASES = ("A1", "A2", "A3", "A4", "A5", "B1", "B2", "B3", "B4")


def build_program(phases=ALL_PHASES, dumps=(), dbg_loads=()):
    nc = bass.Bass("TRN2", target_bir_lowering=False, debug=False)
    c = Ctx(nc)
    pg = c.pg

    inp = {}
    inp["x"] = nc.dram_tensor("x", [T, C], F32, kind="ExternalInput")
    for nm, shape, dt in [
        ("L0_attn_Wq", [C, C], F32), ("L0_attn_Wk", [C, C], F32),
        ("L0_attn_Wv", [C, C], F32), ("L0_attn_Wd", [C, C], F32),
        ("L0_attn_bq", [C], F32), ("L0_attn_bk", [C], F32),
        ("L0_attn_bv", [C], F32), ("L0_attn_bd", [C], F32),
        ("L0_ffn_Wgate", [C, NEXP], F32), ("L0_ffn_bgate", [NEXP], F32),
        ("L0_ffn_W1", [NEXP, C, FF], F32), ("L0_ffn_b1", [NEXP, FF], F32),
        ("L0_ffn_W2", [NEXP, C, FF], F32), ("L0_ffn_b2", [NEXP, FF], F32),
        ("L0_ffn_Wg", [NEXP, FF, FF], F32), ("L0_ffn_bg", [NEXP, FF], F32),
        ("L0_ffn_Wl", [NEXP, FF, FF], F32), ("L0_ffn_bl", [NEXP, FF], F32),
        ("L0_ffn_W3", [NEXP, FF, C], F32), ("L0_ffn_b3", [NEXP, C], F32),
        ("L1_attn_ln1_w", [C], F32), ("L1_attn_ln2_w", [C], F32),
        ("L1_attn_time_decay", [C], F32), ("L1_attn_time_first", [C], F32),
        ("L1_attn_time_mix_k", [C], F32), ("L1_attn_time_mix_v", [C], F32),
        ("L1_attn_time_mix_r", [C], F32), ("L1_attn_W_key", [C, C], F32),
        ("L1_attn_W_value", [C, C], F32), ("L1_attn_W_recept", [C, C], F32),
        ("L1_attn_W_out", [C, C], F32), ("L1_attn_time_mix_k2", [C], F32),
        ("L1_attn_time_mix_r2", [C], F32), ("L1_attn_W_key2", [C, I2], F32),
        ("L1_attn_W_value2", [I2, C], F32), ("L1_attn_W_recept2", [C, C], F32),
        ("L1_ffn_w_gate", [HID, C], I32), ("L1_ffn_w_up", [HID, C], I32),
        ("L1_ffn_w_down", [C, HID], I32),
    ]:
        inp[nm] = nc.dram_tensor(nm, shape, dt, kind="ExternalInput")
    c.inp = inp

    y_out = nc.dram_tensor("y", [T, C], F32, kind="ExternalOutput")
    c.res["y"] = y_out

    c.res["F_hbm"] = nc.dram_tensor("F_hbm", [CC, P, T], F32, kind="Internal")
    c.res["x1_hbm"] = nc.dram_tensor("x1_hbm", [CC, P, T], F32, kind="Internal")
    c.res["xa_hbm"] = nc.dram_tensor("xa_hbm", [CC, P, T], F32, kind="Internal")
    c.res["wgT_hbm"] = nc.dram_tensor("wgT_hbm", [C, HID], BF16, kind="Internal")
    c.res["w1b"] = nc.dram_tensor("w1b_hbm", [NEXP, P, FC, 1024], BF16, kind="Internal")
    c.res["w2b"] = nc.dram_tensor("w2b_hbm", [NEXP, P, FC, 1024], BF16, kind="Internal")
    c.res["wgb"] = nc.dram_tensor("wgb_hbm", [NEXP, P, FC, FF], BF16, kind="Internal")
    c.res["wlb"] = nc.dram_tensor("wlb_hbm", [NEXP, P, FC, FF], BF16, kind="Internal")
    c.res["w3b"] = nc.dram_tensor("w3b_hbm", [NEXP, P, FC, 1024], BF16, kind="Internal")
    c.res["wuT_hbm"] = nc.dram_tensor("wuT_hbm", [C, HID], BF16, kind="Internal")
    c.res["wdT_hbm"] = nc.dram_tensor("wdT_hbm", [HID, C], BF16, kind="Internal")

    ident_bf_h = nc.inline_tensor(np.eye(P, dtype=ml_dtypes.bfloat16), name="ident_bf")
    ident_f_h = nc.inline_tensor(np.eye(P, dtype=np.float32), name="ident_f")
    ones_row_f_h = nc.inline_tensor(np.ones((1, P), np.float32), name="ones_row_f")
    ones_col_bf_h = nc.inline_tensor(np.ones((P, 1), ml_dtypes.bfloat16), name="ones_col_bf")
    ones_col_f_h = nc.inline_tensor(np.ones((P, 1), np.float32), name="ones_col_f")
    sel8_np = np.zeros((NEXP, NEXP * P), np.float32)
    for e in range(NEXP):
        sel8_np[e, e * P:(e + 1) * P] = 1.0
    sel8_h = nc.inline_tensor(sel8_np, name="sel8")

    sb = nc.alloc_sbuf_tensor
    c.res["ident_bf"] = sb("ident_bf_s", [P, P], BF16)
    c.res["ident_f"] = sb("ident_f_s", [P, P], F32)
    c.res["ones_row_f"] = sb("ones_row_f_s", [1, P], F32)
    c.res["ones_col_bf"] = sb("ones_col_bf_s", [P, 1], BF16)
    c.res["ones_col_f"] = sb("ones_col_f_s", [P, 1], F32)
    c.res["sel8"] = sb("sel8_s", [NEXP, NEXP * P], F32)
    c.res["vecs"] = sb("vecs_s", [P, len(VEC_NAMES), CC], F32)
    c.res["dvec"] = sb("dvec_s", [P, ND, CC], F32)
    c.res["b1_fm"] = sb("b1_fm", [P, FC, NEXP], F32)
    c.res["b2_fm"] = sb("b2_fm", [P, FC, NEXP], F32)
    c.res["bg_fm"] = sb("bg_fm", [P, FC, NEXP], F32)
    c.res["bl_fm"] = sb("bl_fm", [P, FC, NEXP], F32)
    c.res["b3_f"] = sb("b3_f", [NEXP, C], F32)
    c.res["wgate"] = sb("wgate_s", [P, CC, NEXP], BF16)
    c.res["bgate"] = sb("bgate_s", [NEXP, 1], F32)
    c.res["cb_pf"] = sb("cb_pf", [NEXP, T], F32)
    c.res["xn_bf"] = sb("xn_bf", [P, CC, T], BF16)
    c.res["cval"] = sb("cval_s", [P, 8], F32)
    c.res["wgate_lo"] = sb("wgate_lo", [P, CC, NEXP], BF16)
    c.res["wgate_f"] = sb("wgate_f", [P, CC, NEXP], F32)

    rem = nc.sbuf_bytes_remaining - 512
    c.arena = sb("arena", [P, rem], mybir.dt.uint8)
    c.arena_addr = nc.lookup_mloc(c.arena).addr
    c.arena_bytes = rem

    c.pb01 = nc.alloc_psum_tensor("pb01", [P, 1024], F32)
    c.pb2 = nc.alloc_psum_tensor("pb2", [P, 512], F32)
    c.pb3 = nc.alloc_psum_tensor("pb3", [P, 512], F32)
    c.pbu = nc.alloc_psum_tensor("pbu", [P, 512], F32)
    c.pbs = nc.alloc_psum_tensor("pbs", [P, 512], F32)
    c.psum_tp = nc.alloc_psum_tensor("pb_tp", [P, 128], BF16)
    c.psum_tp2 = nc.alloc_psum_tensor("pb_tp2", [P, 128], BF16)

    c.consts_dram = dict(ident_bf=ident_bf_h, ident_f=ident_f_h,
                         ones_row_f=ones_row_f_h, ones_col_bf=ones_col_bf_h,
                         ones_col_f=ones_col_f_h, sel8=sel8_h)

    emit_setup(c)
    ph_fns = {"A1": emit_A1, "A2": emit_A2, "A3": emit_A3, "A4": emit_A4,
              "A5": emit_A5, "B1": emit_B1, "B2": emit_B2, "B3": emit_B3,
              "B4": emit_B4}
    c.last_done = list(c.setup_done_waits)
    for name in dbg_loads:
        emit_dbg_load(c, name)
    for ph in phases:
        ph_fns[ph](c)
    for name in dumps:
        emit_dump(c, name)
    emit_tail(c)

    pg.finalize()
    return nc, c



DBG_SPECS = {
    "xn_bf": ("xn_bf", [P, CC, T], BF16),
    "cb_pf": ("cb_pf", [NEXP, T], F32),
}
DBG_HBM = {"F_hbm": [CC, P, T], "x1_hbm": [CC, P, T], "xa_hbm": [CC, P, T]}


def emit_dbg_load(c, name):
    nc, pg = c.nc, c.pg
    if name in DBG_SPECS:
        key, shape, dt = DBG_SPECS[name]
        h = nc.dram_tensor("dbg_" + name, shape, dt, kind="ExternalInput")
        dst = c.res[key]
        s = pg.sem("dbgl_" + name)
        pg.emit("sync", lambda e, dst=dst, h=h: e.dma_start(
            out=dst[tuple(slice(None) for _ in dst.shape)], in_=h.ap()), inc=[(s, 16)])
        c.setup_done_waits.append((s, s.now))
        c.last_done.append((s, s.now))
    elif name in DBG_HBM:
        h = nc.dram_tensor("dbg_" + name, DBG_HBM[name], F32, kind="ExternalInput")
        dst = c.res[name]
        s = pg.sem("dbgl_" + name)
        pg.emit("sync", lambda e, dst=dst, h=h: e.dma_start(
            out=dst.ap(), in_=h.ap()), inc=[(s, 16)])
        c.setup_done_waits.append((s, s.now))
        c.last_done.append((s, s.now))
    else:
        raise KeyError(name)


def emit_dump(c, name):
    nc, pg = c.nc, c.pg
    w = list(c.last_done)
    if name not in DBG_SPECS and name not in DBG_HBM and name in c.res:
        src_t = c.res[name]
        shape = list(src_t.shape)
        h = nc.dram_tensor("dump_" + name, shape, src_t.dtype, kind="ExternalOutput")
        s = pg.sem("dmp_" + name)
        pg.emit("sync", lambda e, src_t=src_t, h=h: e.dma_start(
            out=h.ap(), in_=src_t[tuple(slice(None) for _ in src_t.shape)]),
            wait=w, inc=[(s, 16)])
        c.final_waits.append((s, s.now))
        return
    if name in DBG_SPECS:
        key, shape, dt = DBG_SPECS[name]
        h = nc.dram_tensor("dump_" + name, shape, dt, kind="ExternalOutput")
        src = c.res[key]
        s = pg.sem("dmp_" + name)
        pg.emit("sync", lambda e, src=src, h=h: e.dma_start(
            out=h.ap(), in_=src[tuple(slice(None) for _ in src.shape)]),
            wait=w, inc=[(s, 16)])
        c.final_waits.append((s, s.now))
    elif name in DBG_HBM:
        h = nc.dram_tensor("dump_" + name, DBG_HBM[name], F32, kind="ExternalOutput")
        src = c.res[name]
        s = pg.sem("dmp_" + name)
        pg.emit("sync", lambda e, src=src, h=h: e.dma_start(out=h.ap(), in_=src.ap()),
                wait=w, inc=[(s, 16)])
        c.final_waits.append((s, s.now))
    else:
        raise KeyError(name)


def emit_tail(c):
    pg = c.pg
    def tail(e):
        return None
    waits = list(c.final_waits)
    pg.emit("sync", lambda e: e.nop(), wait=waits)



def emit_setup(c):
    nc, pg = c.nc, c.pg
    ld = pg.sem("s_ld")
    c.final_waits = []
    c.last_done = []

    for key in ("ident_bf", "ident_f", "ones_row_f", "ones_col_bf", "ones_col_f", "sel8"):
        h = c.consts_dram[key]
        dst = c.res[key]
        pg.emit("sync", lambda e, dst=dst, h=h: e.dma_start(
            out=dst[tuple(slice(None) for _ in dst.shape)], in_=h.ap()), inc=[(ld, 16)])

    c.arena_mark()
    stage = c.arena_alloc("stage_vec", [NEXP, len(VEC_NAMES), P], F32)
    for j, nm in enumerate(VEC_NAMES):
        src = c.inp[nm].ap().rearrange("(a b) -> a b", b=P)
        pg.emit("sync", lambda e, j=j, src=src: e.dma_start(
            out=stage[:, j, :], in_=src), inc=[(ld, 16)])

    stage_b = c.arena_alloc("stage_b", [NEXP, 4, FF], F32)
    for j, nm in enumerate(("L0_ffn_b1", "L0_ffn_b2", "L0_ffn_bg", "L0_ffn_bl")):
        pg.emit("sync", lambda e, j=j, nm=nm: e.dma_start(
            out=stage_b[:, j, :], in_=c.inp[nm].ap()), inc=[(ld, 16)])
    pg.emit("sync", lambda e: e.dma_start(out=c.res["b3_f"][:, :],
                                          in_=c.inp["L0_ffn_b3"].ap()), inc=[(ld, 16)])
    pg.emit("sync", lambda e: e.dma_start(out=c.res["bgate"][:, 0],
                                          in_=c.inp["L0_ffn_bgate"].ap()), inc=[(ld, 16)])
    pg.emit("gpsimd", lambda e: e.dma_start(
        out=c.res["wgate"][:, :, :],
        in_=c.inp["L0_ffn_Wgate"].ap().rearrange("(kc p) f -> p kc f", p=P)),
        inc=[(ld, 16)])
    pg.emit("sync", lambda e: e.dma_start(
        out=c.res["wgate_f"][:, :, :],
        in_=c.inp["L0_ffn_Wgate"].ap().rearrange("(kc p) f -> p kc f", p=P)),
        inc=[(ld, 16)])
    n_ld = ld.now

    cvt = pg.sem("s_cval")
    cval = c.res["cval"]
    for k, j in CVAL.items():
        pg.emit("gpsimd", lambda e, j=j, v=CVAL_VALUES[k]: e.memset(cval[:, j:j + 1], v),
                drain=True, inc=[(cvt, 1)])

    prep = pg.sem("s_prep")
    c.prep_ready = []
    for ex in range(NEXP):
        for wname, dstk, style in (("L0_ffn_W1", "w1b", "fh"), ("L0_ffn_W2", "w2b", "fh"),
                                   ("L0_ffn_Wg", "wgb", "kf"), ("L0_ffn_Wl", "wlb", "kf"),
                                   ("L0_ffn_W3", "w3b", "kf")):
            if style == "fh":
                src_ap = c.inp[wname].ap()[ex].rearrange(
                    "(kc p) (fh g) -> p kc fh g", p=P, g=1024)
                dst_ap = c.res[dstk].ap()[ex].rearrange(
                    "p (kc fh) g -> p kc fh g", kc=CC)
            else:
                src_ap = c.inp[wname].ap()[ex].rearrange("(kc p) f -> p kc f", p=P)
                dst_ap = c.res[dstk].ap()[ex]
            pg.emit("gpsimd", lambda e, src_ap=src_ap, dst_ap=dst_ap: e.dma_start(
                out=dst_ap, in_=src_ap),
                inc=[(prep, 16)])
        c.prep_ready.append((prep, prep.now))
    c.final_waits.append((prep, prep.now))

    tp = pg.sem("s_tp")
    ev = pg.sem("s_ev")
    vecs = c.res["vecs"]
    idf = c.res["ident_f"]
    pu = c.pbu
    def tpev(src_ap_fn, dst_ap_fn, first_wait):
        j = tp.now
        pg.emit("tensor", lambda e, s=src_ap_fn: e.transpose(
            pu[:, 0:8], s(), idf[0:NEXP, 0:NEXP]),
            wait=first_wait + [(ev, j)], inc=[(tp, 1)])
        pg.emit("vector", lambda e, d=dst_ap_fn: e.tensor_copy(d(), pu[:, 0:8]),
            wait=[(tp, tp.now)], inc=[(ev, 1)], drain=True)

    for j in range(len(VEC_NAMES)):
        tpev(lambda j=j: stage[:, j, :], lambda j=j: vecs[:, j, :],
             [(ld, n_ld)] if j == 0 else [])
    for j, key in enumerate(("b1_fm", "b2_fm", "bg_fm", "bl_fm")):
        dstt = c.res[key]
        for fc in range(FC):
            tpev(lambda j=j, fc=fc: stage_b[:, j, fc * P:(fc + 1) * P],
                 lambda dstt=dstt, fc=fc: dstt[:, fc, :], [])

    dvec = c.res["dvec"]
    dv = pg.sem("s_dv")
    pg.emit("vector", lambda e: e.tensor_scalar_mul(
        dvec[:, DV["bq8"], :], vecs[:, VI["bq"], :], 0.125),
        wait=[(ev, ev.now)], drain=True, inc=[(dv, 1)])
    pg.emit("scalar", lambda e: e.activation(
        dvec[:, DV["ew"], :], vecs[:, VI["time_decay"], :], AF.Exp),
        wait=[(ev, ev.now)], inc=[(dv, 1)])
    pg.emit("vector", lambda e: e.tensor_scalar(
        dvec[:, DV["ew"], :], dvec[:, DV["ew"], :], -1.0, 20.0, ALU.mult, ALU.min),
        wait=[(dv, dv.now)], drain=True, inc=[(dv, 1)])
    pg.emit("vector", lambda e: e.tensor_scalar_max(
        dvec[:, DV["ew"], :], dvec[:, DV["ew"], :], -20.0), drain=True, inc=[(dv, 1)])
    pg.emit("scalar", lambda e: e.activation(
        dvec[:, DV["ew"], :], dvec[:, DV["ew"], :], AF.Exp),
        wait=[(dv, dv.now)], drain=True, inc=[(dv, 1)])

    pg.emit("vector", lambda e: e.tensor_tensor(
        c.res["wgate_lo"][:, :, :], c.res["wgate_f"][:, :, :],
        c.res["wgate"][:, :, :], ALU.subtract),
        wait=[(ld, n_ld)], drain=True, inc=[(dv, 1)])

    c.arena_release()
    c.setup_done_waits = [(ld, n_ld), (ev, ev.now), (dv, dv.now),
                          (cvt, cvt.now)]



def emit_A1(c):
    nc, pg = c.nc, c.pg
    c.res["xn_lo"] = c.arena_alloc("xn_lo", [P, CC, T], BF16)
    c.arena_mark()
    x_ring = [c.arena_alloc(f"a1_x{i}", [P, C], F32) for i in range(2)]
    sq = c.arena_alloc("a1_sq", [P, C], F32)
    st_ring = [c.arena_alloc(f"a1_st{i}", [P, 8], F32) for i in range(2)]
    xn_tm = [c.arena_alloc(f"a1_xn{i}", [P, C], BF16) for i in range(2)]
    xn_lo_tm = [c.arena_alloc(f"a1_xnl{i}", [P, C], BF16) for i in range(2)]
    xnf = [c.arena_alloc(f"a1_xnf{i}", [P, C], F32) for i in range(2)]
    xn_lo = c.res["xn_lo"]

    ldx = pg.sem("a1_ldx")
    stat = pg.sem("a1_stat")
    xnr = pg.sem("a1_xnr")
    tpd = pg.sem("a1_tpd")
    evd = pg.sem("a1_evd")
    setup_w = c.setup_done_waits

    x_h = c.inp["x"].ap()
    xn_bf = c.res["xn_bf"]
    idb = c.res["ident_bf"]

    for i in range(16):
        ring = x_ring[i % 2]
        xnt = xn_tm[i % 2]
        st = st_ring[i % 2]
        w = list(setup_w) if i < 2 else []
        if i >= 2:
            w.append((xnr, i - 1))
        pg.emit("sync", lambda e, ring=ring, i=i: e.dma_start(
            out=ring[:, :], in_=x_h[i * P:(i + 1) * P, :]), wait=w, inc=[(ldx, 16)])
        wld = [(ldx, 16 * (i + 1))]
        if i >= 2:
            wld = wld + [(xnr, i - 1)]
        pg.emit("scalar", lambda e, ring=ring, st=st: e.activation(
            sq[:, :], ring[:, :], AF.Square, accum_out=st[:, 1:2]),
            wait=wld, drain=True, inc=[(stat, 1)])
        pg.emit("vector", lambda e, ring=ring, st=st: e.reduce_sum(
            st[:, 0:1], ring[:, :], axis=AX.X), wait=wld, drain=True, inc=[(stat, 1)])
        pg.emit("vector", lambda e, st=st: e.tensor_scalar_mul(st[:, 2:3], st[:, 0:1], 1.0 / C),
                wait=[(stat, stat.now)], drain=True)
        pg.emit("vector", lambda e, st=st: e.tensor_scalar_mul(st[:, 3:4], st[:, 2:3], -1.0),
                drain=True)
        pg.emit("vector", lambda e, st=st: e.tensor_tensor(st[:, 4:5], st[:, 2:3], st[:, 2:3],
                                                    ALU.mult), drain=True)
        pg.emit("vector", lambda e, st=st: e.scalar_tensor_tensor(
            st[:, 5:6], st[:, 1:2], 1.0 / C, st[:, 4:5], ALU.mult, ALU.subtract),
            drain=True, inc=[(stat, 1)])
        pg.emit("scalar", lambda e, st=st: e.activation(st[:, 6:7], st[:, 5:6], AF.Sqrt,
                                                 bias=c.res["cval"][:, CVAL["eps_ln"]:CVAL["eps_ln"] + 1]),
                wait=[(stat, stat.now)], drain=True, inc=[(stat, 1)])
        pg.emit("vector", lambda e, st=st: e.reciprocal(st[:, 7:8], st[:, 6:7]),
                wait=[(stat, stat.now)], drain=True, inc=[(stat, 1)])
        xf = xnf[i % 2]
        xlt = xn_lo_tm[i % 2]
        pg.emit("vector", lambda e, ring=ring, xf=xf, st=st: e.tensor_scalar(
            xf[:, :], ring[:, :], st[:, 3:4], st[:, 7:8], ALU.add, ALU.mult),
            wait=[(stat, stat.now)] + ([(evd, 16 * (i - 1))] if i >= 2 else []),
            drain=True)
        pg.emit("vector", lambda e, xf=xf, xnt=xnt: e.tensor_copy(xnt[:, :], xf[:, :]),
                drain=True)
        pg.emit("vector", lambda e, xf=xf, xnt=xnt, xlt=xlt: e.tensor_tensor(
            xlt[:, :], xf[:, :], xnt[:, :], ALU.subtract),
            drain=True, inc=[(xnr, 1)])
        for cc in range(CC):
            for half, (srcm, dstm) in enumerate(((xnt, xn_bf), (xlt, xn_lo))):
                g = i * 16 + cc * 2 + half
                ps = c.psum_tp if g % 2 == 0 else c.psum_tp2
                w2 = [(xnr, xnr.now)]
                if g >= 2:
                    w2.append((evd, g - 1))
                pg.emit("tensor", lambda e, ps=ps, srcm=srcm, cc=cc: e.transpose(
                    ps[:, :], srcm[:, cc * P:(cc + 1) * P], idb[:, :]),
                    wait=w2, inc=[(tpd, 1)])
                pg.emit("vector", lambda e, ps=ps, dstm=dstm, cc=cc, i=i: e.tensor_copy(
                    dstm[:, cc, i * P:(i + 1) * P], ps[:, :]),
                    wait=[(tpd, tpd.now)], drain=True, inc=[(evd, 1)])

    c.arena_release()
    c.last_done = [(evd, evd.now)]



def emit_A2(c):
    nc, pg = c.nc, c.pg
    c.arena_mark()
    gf = c.arena_alloc("a2_gf", [NEXP, 512], F32)
    gtm = c.arena_alloc("a2_gtm", [P, NEXP], F32)
    st = c.arena_alloc("a2_st", [P, 8], F32)
    eq1 = c.arena_alloc("a2_eq1", [P, NEXP], F32)
    eq2 = c.arena_alloc("a2_eq2", [P, NEXP], F32)
    gmask = c.arena_alloc("a2_gm", [P, NEXP], F32)
    t1 = c.arena_alloc("a2_t1", [P, NEXP], F32)
    cbtm = c.arena_alloc("a2_cb", [P, NEXP], F32)

    mm = pg.sem("a2_mm")
    ev = pg.sem("a2_ev")
    dn = pg.sem("a2_dn")

    xn_bf = c.res["xn_bf"]
    wg = c.res["wgate"]
    bg = c.res["bgate"]
    cb_pf = c.res["cb_pf"]
    pbs = c.pbs
    pbu = c.pbu
    idf = c.res["ident_f"]
    prev = list(c.last_done)

    for tt in range(4):
        def gmm(e, tt=tt):
            r = None
            xl = c.res["xn_lo"]
            wgl = c.res["wgate_lo"]
            terms = [(wg, xn_bf), (wg, xl), (wgl, xn_bf)]
            n = len(terms) * CC
            i = 0
            for wgt, act in terms:
                for kc in range(CC):
                    r = e.matmul(pbs[0:NEXP, :], wgt[:, kc, :],
                                 act[:, kc, tt * 512:(tt + 1) * 512],
                                 start=(i == 0), stop=(i == n - 1))
                    i += 1
            return r
        pg.emit("tensor", gmm, wait=prev + [(ev, ev.now)], inc=[(mm, 1)])
        pg.emit("vector", lambda e: e.tensor_scalar_add(gf[:, :], pbs[0:NEXP, :],
                                                        bg[:, 0:1]),
                wait=[(mm, mm.now)], drain=True, inc=[(ev, 1)])
        for q in range(4):
            t0 = tt * 512 + q * P
            pg.emit("tensor", lambda e, q=q: e.transpose(
                pbu[:, 0:NEXP], gf[:, q * P:(q + 1) * P], idf[0:NEXP, 0:NEXP]),
                wait=[(ev, ev.now)], inc=[(mm, 1)])
            pg.emit("vector", lambda e: e.tensor_copy(gtm[:, :], pbu[:, 0:NEXP]),
                    wait=[(mm, mm.now)], drain=True)
            pg.emit("vector", lambda e: e.reduce_max(st[:, 0:1], gtm[:, :], axis=AX.X),
                    drain=True)
            pg.emit("vector", lambda e: e.tensor_single_scalar(
                eq1[:, :], gtm[:, :], st[:, 0:1], ALU.is_equal), drain=True)
            pg.emit("vector", lambda e: e.scalar_tensor_tensor(
                gmask[:, :], eq1[:, :], -1e30, gtm[:, :], ALU.mult, ALU.add), drain=True)
            pg.emit("vector", lambda e: e.reduce_max(st[:, 1:2], gmask[:, :], axis=AX.X),
                    drain=True)
            pg.emit("vector", lambda e: e.tensor_single_scalar(
                eq2[:, :], gmask[:, :], st[:, 1:2], ALU.is_equal), drain=True)
            pg.emit("vector", lambda e: e.tensor_scalar_mul(st[:, 2:3], st[:, 0:1], -1.0),
                    drain=True)
            pg.emit("vector", lambda e: e.tensor_scalar_mul(st[:, 3:4], st[:, 1:2], -1.0),
                    drain=True, inc=[(dn, 1)])
            pg.emit("scalar", lambda e: e.activation(st[:, 4:5], st[:, 0:1], AF.Sigmoid,
                                                     bias=st[:, 3:4]),
                    wait=[(dn, dn.now)], drain=True)
            pg.emit("scalar", lambda e: e.activation(st[:, 5:6], st[:, 1:2], AF.Sigmoid,
                                                     bias=st[:, 2:3]),
                    drain=True, inc=[(dn, 1)])
            pg.emit("vector", lambda e: e.tensor_scalar_mul(t1[:, :], eq1[:, :], st[:, 4:5]),
                    wait=[(dn, dn.now)], drain=True)
            pg.emit("vector", lambda e: e.scalar_tensor_tensor(
                cbtm[:, :], eq2[:, :], st[:, 5:6], t1[:, :], ALU.mult, ALU.add),
                drain=True, inc=[(dn, 1)])
            pg.emit("tensor", lambda e: e.transpose(
                pbu[0:NEXP, 0:P], cbtm[:, :], idf[:, :]),
                wait=[(dn, dn.now)], inc=[(mm, 1)])
            pg.emit("vector", lambda e, t0=t0: e.tensor_copy(
                cb_pf[0:NEXP, t0:t0 + P], pbu[0:NEXP, 0:P]), wait=[(mm, mm.now)],
                drain=True, inc=[(ev, 1)])

    c.arena_release()
    c.last_done = [(ev, ev.now), (dn, dn.now)]


class Banks:

    def __init__(self, slots):
        self.slots = slots
        self.free = [None] * len(slots)
        self.i = 0

    def acquire(self):
        j = self.i % len(self.slots)
        self.i += 1
        ap = self.slots[j]
        w = self.free[j]
        return j, ap, ([w] if w else [])

    def release(self, j, cb):
        self.free[j] = (cb, cb.now)



def emit_A3(c):
    nc, pg = c.nc, c.pg
    c.arena_reset()
    z = c.arena_alloc("a3_z", [P, FC, 512], BF16)
    o = c.arena_alloc("a3_o", [P, FC, 512], BF16)
    slots_t = [c.arena_alloc(f"a3_w{i}", [P, FC, 1024], BF16) for i in range(3)]
    sg = c.arena_alloc("a3_sg", [P, NEXP, 512], BF16)
    cbE = c.arena_alloc("a3_cbE", [P, 512], F32)
    x1s = c.arena_alloc("a3_x1s", [P, 512], F32)
    ys = c.arena_alloc("a3_ys", [P, 512], F32)
    ls = c.arena_alloc("a3_ls", [P, 512], F32)

    for nm, t in (("a3_z", z), ("a3_o", o), ("a3_sg", sg), ("a3_cbE", cbE)):
        c.res[nm] = t
    wdma = pg.sem("a3_wdma")
    fdma = pg.sem("a3_fdma")
    mm = pg.sem("a3_mm")
    evv = pg.sem("a3_evv")
    evs = pg.sem("a3_evs")

    xn_bf = c.res["xn_bf"]
    cb_pf = c.res["cb_pf"]
    sel8 = c.res["sel8"]
    b3_f = c.res["b3_f"]
    F_hbm = c.res["F_hbm"].ap()
    b1_fm, b2_fm = c.res["b1_fm"], c.res["b2_fm"]
    bg_fm, bl_fm = c.res["bg_fm"], c.res["bl_fm"]

    banks = Banks([c.pb01[:, 0:512], c.pb01[:, 512:1024], c.pb2[:, :], c.pb3[:, :]])
    ubank_free = [None]
    prev = list(c.last_done)

    def v_w12(i):
        return slots_t[i][:, :, :].rearrange("p (kc fh) g -> p kc (fh g)", kc=CC)

    def v_half(i):
        return slots_t[i][:, :, :]

    for tt in range(4):
        for ct in range(CC):
            t0 = tt * 512
            w0 = list(prev) if (tt == 0 and ct == 0) else []
            pg.emit("tensor",
                    lambda e, ct=ct, t0=t0: e.matmul(
                        c.pbu[:, :], b3_f[0:NEXP, ct * P:(ct + 1) * P],
                        cb_pf[0:NEXP, t0:t0 + 512], start=True, stop=True),
                    wait=w0 + ([ubank_free[0]] if ubank_free[0] else []), inc=[(mm, 1)])
            pg.emit("vector", lambda e: e.tensor_copy(ys[:, :], c.pbu[:, :]),
                    wait=[(mm, mm.now), (fdma, fdma.now)], drain=True, inc=[(evv, 1)])
            ubank_free[0] = (evv, evv.now)
            pg.emit("gpsimd", lambda e, ct=ct, t0=t0: e.dma_start(
                out=F_hbm[ct, :, t0:t0 + 512], in_=ys[:, :]),
                wait=[(evv, evv.now)], inc=[(fdma, 16)])

    slot_free = [None, None, None]
    slot_i = [0]

    def load(src_ap, nkc, extra_wait=()):
        i = slot_i[0] % 3
        slot_i[0] += 1
        w = list(extra_wait)
        if slot_free[i] is not None:
            w.append(slot_free[i])
        pg.emit("gpsimd", lambda e, i=i, src_ap=src_ap: e.dma_start(
            out=slots_t[i][:, :, :], in_=src_ap), wait=w, inc=[(wdma, 16)])
        return i, wdma.now

    import os
    n_tt2 = int(os.environ.get("A3_TT2", "2"))
    n_ex = int(os.environ.get("A3_EX", str(NEXP)))
    n_s = int(os.environ.get("A3_S", "2"))
    for tt2 in range(n_tt2):
        tb = tt2 * 1024
        for ex in range(n_ex):
            for s in range(n_s):
                t0 = tb + s * 512
                pw = [c.prep_ready[ex]] if tt2 == 0 else []
                stage = os.environ.get("A3_STAGE", "y")
                if stage != "cb":
                    l_w1 = load(c.res["w1b"].ap()[ex], CC, pw)
                    l_w2 = load(c.res["w2b"].ap()[ex], CC, pw)
                pg.emit("tensor",
                        lambda e, ex=ex, t0=t0: e.matmul(
                            c.pbu[:, :], sel8[0:NEXP, ex * P:(ex + 1) * P],
                            cb_pf[0:NEXP, t0:t0 + 512], start=True, stop=True),
                        wait=([ubank_free[0]] if ubank_free[0] else []), inc=[(mm, 1)])
                pg.emit("vector", lambda e: e.tensor_copy(cbE[:, :], c.pbu[:, :]),
                        wait=[(mm, mm.now)], drain=True, inc=[(evv, 1)])
                ubank_free[0] = (evv, evv.now)

                if stage == "cb":
                    continue
                if stage == "loads":
                    slot_free[l_w1[0]] = (wdma, l_w1[1])
                    slot_free[l_w2[0]] = (wdma, l_w2[1])
                    continue
                w1v, w2v = v_w12(l_w1[0]), v_w12(l_w2[0])
                for fc in range(FC):
                    j1, pA, wA = banks.acquire()
                    j2, pB, wB = banks.acquire()
                    pg.emit("tensor",
                            lambda e, fc=fc, pA=pA, w1v=w1v, t0=t0: _kmm(
                                e, pA, w1v, xn_bf, fc, CC, t0),
                            wait=[(wdma, l_w1[1])] + wA + wB, inc=[(mm, 1)])
                    pg.emit("tensor",
                            lambda e, fc=fc, pB=pB, w2v=w2v, t0=t0: _kmm(
                                e, pB, w2v, xn_bf, fc, CC, t0),
                            wait=[(wdma, l_w2[1])], inc=[(mm, 1)])
                    mmv = mm.now
                    pg.emit("vector", lambda e, fc=fc, ex=ex, pA=pA: e.tensor_scalar_add(
                        x1s[:, :], pA, b1_fm[:, fc, ex:ex + 1]),
                        wait=[(mm, mmv)], drain=True, inc=[(evv, 1)])
                    banks.release(j1, evv)
                    pg.emit("vector",
                            lambda e, fc=fc, ex=ex, pB=pB: e.scalar_tensor_tensor(
                                z[:, fc, :], pB, b2_fm[:, fc, ex:ex + 1],
                                x1s[:, :], ALU.add, ALU.mult),
                            drain=True, inc=[(evv, 1)])
                    banks.release(j2, evv)
                z_done = evv.now
                slot_free[l_w1[0]] = (mm, mm.now)
                slot_free[l_w2[0]] = (mm, mm.now)
                if stage == "z":
                    continue

                for h in range(2):
                    l_wg = load(c.res["wgb"].ap()[ex][:, :, h * 1024:(h + 1) * 1024],
                                FC, pw)
                    l_wl = load(c.res["wlb"].ap()[ex][:, :, h * 1024:(h + 1) * 1024],
                                FC, pw)
                    wgv, wlv = v_half(l_wg[0]), v_half(l_wl[0])
                    for gf in range(NEXP):
                        gt = h * NEXP + gf
                        jg, pG, wG = banks.acquire()
                        pg.emit("tensor",
                                lambda e, gf=gf, pG=pG, wgv=wgv: _kmm(
                                    e, pG, wgv, z, gf, FC, 0),
                                wait=[(wdma, l_wg[1]), (evv, z_done)] + wG,
                                inc=[(mm, 1)])
                        mmv = mm.now
                        pg.emit("scalar", lambda e, gt=gt, gf=gf, ex=ex, pG=pG: e.activation(
                            sg[:, gf, :], pG, AF.Silu, bias=bg_fm[:, gt, ex:ex + 1]),
                            wait=[(mm, mmv), (evv, evv.now)], drain=True, inc=[(evs, 1)])
                        banks.release(jg, evs)
                    for gf in range(NEXP):
                        gt = h * NEXP + gf
                        jl, pL, wL = banks.acquire()
                        pg.emit("tensor",
                                lambda e, gf=gf, pL=pL, wlv=wlv: _kmm(
                                    e, pL, wlv, z, gf, FC, 0),
                                wait=[(wdma, l_wl[1])] + wL, inc=[(mm, 1)])
                        mmv = mm.now
                        pg.emit("vector",
                                lambda e, gt=gt, ex=ex, pL=pL: e.scalar_tensor_tensor(
                                    ls[:, :], pL, bl_fm[:, gt, ex:ex + 1],
                                    cbE[:, :], ALU.add, ALU.mult),
                                wait=[(mm, mmv), (evs, evs.now)], drain=True,
                                inc=[(evv, 1)])
                        banks.release(jl, evv)
                        pg.emit("vector", lambda e, gt=gt, gf=gf: e.tensor_tensor(
                            o[:, gt, :], ls[:, :], sg[:, gf, :],
                            ALU.mult), drain=True, inc=[(evv, 1)])
                    slot_free[l_wg[0]] = (mm, mm.now)
                    slot_free[l_wl[0]] = (mm, mm.now)
                o_done = evv.now
                if stage == "gl":
                    continue

                l_w3 = load(c.res["w3b"].ap()[ex], FC, pw)
                w3v = v_half(l_w3[0])
                for ct in range(CC):
                    jy, pY, wY = banks.acquire()
                    pg.emit("tensor",
                            lambda e, ct=ct, pY=pY, w3v=w3v: _kmm(
                                e, pY, w3v, o, ct, FC, 0),
                            wait=[(wdma, l_w3[1]), (evv, o_done)] + wY, inc=[(mm, 1)])
                    mmv = mm.now
                    pg.emit("vector", lambda e, pY=pY: e.tensor_copy(ys[:, :], pY),
                            wait=[(mm, mmv), (fdma, fdma.now)], drain=True,
                            inc=[(evv, 1)])
                    banks.release(jy, evv)
                    pg.emit("gpsimd", lambda e, ct=ct, t0=t0: e.dma_start(
                        out=F_hbm[ct, :, t0:t0 + 512],
                        in_=ys[:, :], accum_op=ALU.add),
                        wait=[(evv, evv.now)], inc=[(fdma, 16)])
                slot_free[l_w3[0]] = (mm, mm.now)

    c.last_done = [(fdma, fdma.now), (evv, evv.now), (evs, evs.now), (mm, mm.now)]


def _kmm(e, psum, wbuf, act, ftile, nk, t0):
    r = None
    for kc in range(nk):
        r = e.matmul(psum, wbuf[:, kc, ftile * P:(ftile + 1) * P],
                     act[:, kc, t0:t0 + 512], start=(kc == 0), stop=(kc == nk - 1))
    return r


def emit_A4(c):
    nc, pg = c.nc, c.pg
    c.arena_reset()
    Q = c.arena_alloc("a4_Q", [P, CC, T], BF16)
    Kf = c.arena_alloc("a4_K", [P, CC, T], BF16)
    Vt = c.arena_alloc("a4_V", [P, 16, C], BF16)
    wpa = c.arena_alloc("a4_wpa", [P, CC, C], BF16)
    wpb = c.arena_alloc("a4_wpb", [P, CC, C], BF16)
    c.a5 = dict(Q=Q, Kf=Kf, Vt=Vt, wpa=wpa, wpb=wpb)

    wdma = pg.sem("a4_wdma")
    mm = pg.sem("a4_mm")
    evv = pg.sem("a4_evv")

    xn_bf = c.res["xn_bf"]
    vecs = c.res["vecs"]
    dvec = c.res["dvec"]
    banks = Banks([c.pb01[:, 0:512], c.pb01[:, 512:1024], c.pb2[:, :], c.pb3[:, :]])
    prev = list(c.last_done)

    for wname, dst, w0 in (("L0_attn_Wq", wpa, prev), ("L0_attn_Wk", wpb, prev)):
        src_ap = c.inp[wname].ap().rearrange("(kc p) f -> p kc f", p=P)
        pg.emit("gpsimd", lambda e, dst=dst, src_ap=src_ap: e.dma_start(
            out=dst[:, :, :], in_=src_ap), wait=w0, inc=[(wdma, 16)])
    wd_qk = wdma.now

    for co in range(CC):
        for s in range(4):
            jq, pQ, wQ = banks.acquire()
            jk, pK, wK = banks.acquire()
            pg.emit("tensor", lambda e, co=co, s=s, pQ=pQ: _kmm(
                e, pQ, wpa, xn_bf, co, CC, s * 512),
                wait=[(wdma, wd_qk)] + wQ + wK, inc=[(mm, 1)])
            pg.emit("tensor", lambda e, co=co, s=s, pK=pK: _kmm(
                e, pK, wpb, xn_bf, co, CC, s * 512),
                wait=[], inc=[(mm, 1)])
            mmv = mm.now
            pg.emit("vector", lambda e, co=co, s=s, pQ=pQ: e.tensor_scalar(
                Q[:, co, s * 512:(s + 1) * 512], pQ, 0.125,
                dvec[:, DV["bq8"], co:co + 1], ALU.mult, ALU.add),
                wait=[(mm, mmv)], drain=True, inc=[(evv, 1)])
            banks.release(jq, evv)
            pg.emit("vector", lambda e, co=co, s=s, pK=pK: e.tensor_scalar_add(
                Kf[:, co, s * 512:(s + 1) * 512], pK, vecs[:, VI["bk"], co:co + 1]),
                drain=True, inc=[(evv, 1)])
            banks.release(jk, evv)

    src_ap = c.inp["L0_attn_Wv"].ap().rearrange("(kc p) f -> p kc f", p=P)
    pg.emit("gpsimd", lambda e, src_ap=src_ap: e.dma_start(
        out=wpa[:, :, :], in_=src_ap), wait=[(mm, mm.now)], inc=[(wdma, 16)])
    wd_v = wdma.now

    for i in range(16):
        for half in range(2):
            jv, pV, wV = banks.acquire()
            pg.emit("tensor", lambda e, i=i, half=half, pV=pV: _vmm(
                e, pV, xn_bf, wpa, i, half),
                wait=[(wdma, wd_v)] + wV, inc=[(mm, 1)])
            mmv = mm.now
            pg.emit("vector", lambda e, i=i, half=half, pV=pV: e.tensor_copy(
                Vt[:, i, half * 512:(half + 1) * 512], pV),
                wait=[(mm, mmv)], drain=True, inc=[(evv, 1)])
            banks.release(jv, evv)

    c.last_done = [(mm, mm.now), (evv, evv.now)]


def _vmm(e, psum, xn_bf, wv, i, half):
    r = None
    for kc in range(CC):
        r = e.matmul(psum, xn_bf[:, kc, i * P:(i + 1) * P],
                     wv[:, kc, half * 512:(half + 1) * 512],
                     start=(kc == 0), stop=(kc == CC - 1))
    return r



def emit_A5(c):
    nc, pg = c.nc, c.pg
    Q, Kf, Vt = c.a5["Q"], c.a5["Kf"], c.a5["Vt"]
    wpb = c.a5["wpb"]
    masks = c.arena_alloc("a5_masks", [P, 5, 640], F32)
    sc = c.arena_alloc("a5_sc", [P, 640], F32)
    pbf = c.arena_alloc("a5_pbf", [P, 640], BF16)
    pbf2 = c.arena_alloc("a5_pbf2", [P, 640], BF16)
    pT = c.arena_alloc("a5_pT", [P, 5, P], BF16)
    ao = c.arena_alloc("a5_ao", [P, CC, P], BF16)
    stt_ = c.arena_alloc("a5_st", [P, 4], F32)
    x1b = c.arena_alloc("a5_x1b", [P, CC, P], F32)
    sf = c.arena_alloc("a5_sf", [P, P], F32)
    ss = c.arena_alloc("a5_ss", [P, P], F32)
    ss2 = c.arena_alloc("a5_ss2", [P, P], F32)
    wpa_off = 3 * ((CC * T * 2 + 31) & ~31)
    blk32 = (CC * T * 2 + 31) & ~31
    blk16 = (CC * C * 2 + 31) & ~31
    wpa_off = 3 * blk32
    xtm = [c.arena_alloc_at(f"a5_x{i}", [P, C], F32, wpa_off + i * 4096) for i in range(2)]
    xfm = c.arena_alloc_at("a5_xfm", [P, CC, P], F32, wpa_off + 8192)
    Ftile = c.arena_alloc_at("a5_Ft", [P, CC, P], F32, wpa_off + 12288)

    ld = pg.sem("a5_ld")
    fl = pg.sem("a5_fl")
    mm = pg.sem("a5_mm")
    evv = pg.sem("a5_evv")
    evs = pg.sem("a5_evs")
    odma = pg.sem("a5_odma")
    wdma = pg.sem("a5_wdma")

    vecs = c.res["vecs"]
    idb = c.res["ident_bf"]
    idf = c.res["ident_f"]
    x_h = c.inp["x"].ap()
    F_hbm = c.res["F_hbm"].ap()
    x1_hbm = c.res["x1_hbm"].ap()
    prev = list(c.last_done)

    src_ap = c.inp["L0_attn_Wd"].ap().rearrange("(kc p) f -> p kc f", p=P)
    pg.emit("gpsimd", lambda e, src_ap=src_ap: e.dma_start(
        out=wpb[:, :, :], in_=src_ap), wait=prev, inc=[(wdma, 16)])
    wd_d = wdma.now
    mk = pg.sem("a5_mask")
    for mi, off in enumerate((0, 128, 256, 384, 512)):
        pg.emit("gpsimd", lambda e, mi=mi: e.memset(masks[:, mi, :], 0.0),
                wait=(list(prev) if mi == 0 else []), drain=True)
        pg.emit("gpsimd", lambda e, mi=mi, off=off: e.affine_select(
            masks[:, mi, :], masks[:, mi, :], [[-1, 640]], ALU.is_ge, -1e9,
            base=WIN + off, channel_multiplier=1), drain=True)
        pg.emit("gpsimd", lambda e, mi=mi, off=off: e.affine_select(
            masks[:, mi, :], masks[:, mi, :], [[1, 640]], ALU.is_ge, -1e9,
            base=WIN - off, channel_multiplier=-1), drain=True,
            inc=[(mk, 1)])

    for qt in range(16):
        q0 = qt * P
        ks = min(max(q0 - 2 * WIN // 2, 0), T - 640)
        ks = min(max(q0 - 256, 0), T - 640)
        mi = (q0 - ks) // 128
        xr = xtm[qt % 2]
        wx = [(evv, evv.now)] if qt >= 2 else list(prev)
        pg.emit("sync", lambda e, xr=xr, q0=q0: e.dma_start(
            out=xr[:, :], in_=x_h[q0:q0 + P, :]), wait=wx, inc=[(ld, 16)])
        pg.emit("sync", lambda e, q0=q0: e.dma_start(
            out=Ftile[:, :, :],
            in_=F_hbm[:, :, q0:q0 + P].rearrange("cc p t -> p cc t")),
            wait=[(evv, evv.now)] + (list(prev) if qt == 0 else []), inc=[(fl, 16)])
        ld_t = ld.now
        fl_t = fl.now

        for cc in range(CC):
            pg.emit("tensor", lambda e, xr=xr, cc=cc: e.transpose(
                c.pbu[:, 0:P], xr[:, cc * P:(cc + 1) * P], idf[:, :]),
                wait=[(ld, ld_t), (evv, evv.now)], inc=[(mm, 1)])
            pg.emit("vector", lambda e, cc=cc: e.tensor_copy(
                xfm[:, cc, :], c.pbu[:, 0:P]),
                wait=[(mm, mm.now)], drain=True, inc=[(evv, 1)])

        for h in range(NHEAD):
            pb = (h % 2) * 64
            hc = h // 2
            def smm(e, h=h, pb=pb, hc=hc, q0=q0, ks=ks):
                r = e.matmul(c.pb01[:, 0:512], Q[pb:pb + 64, hc, q0:q0 + P],
                             Kf[pb:pb + 64, hc, ks:ks + 512], start=True, stop=True)
                r = e.matmul(c.pb01[:, 512:640], Q[pb:pb + 64, hc, q0:q0 + P],
                             Kf[pb:pb + 64, hc, ks + 512:ks + 640],
                             start=True, stop=True)
                return r
            pg.emit("tensor", smm, wait=[(evv, evv.now), (wdma, wd_d)], inc=[(mm, 1)])
            pg.emit("vector", lambda e, mi=mi: e.tensor_tensor(
                sc[:, :], c.pb01[:, 0:640], masks[:, mi, :], ALU.add),
                wait=[(mm, mm.now), (mk, 5)], drain=True, inc=[(evv, 1)])
            pg.emit("vector", lambda e: e.reduce_max(
                stt_[:, 0:1], sc[:, :], axis=AX.X, negate=True), drain=True,
                inc=[(evv, 1)])
            pg.emit("scalar", lambda e: e.activation(
                pbf[:, :], sc[:, :], AF.Exp, bias=stt_[:, 0:1],
                accum_out=stt_[:, 1:2]),
                wait=[(evv, evv.now)], drain=True, inc=[(evs, 1)])
            pg.emit("vector", lambda e: e.reciprocal(stt_[:, 2:3], stt_[:, 1:2]),
                    wait=[(evs, evs.now)], drain=True)
            pg.emit("vector", lambda e: e.tensor_scalar_mul(
                pbf2[:, :], pbf[:, :], stt_[:, 2:3]), drain=True, inc=[(evv, 1)])
            for ki in range(5):
                ps = c.psum_tp if ki % 2 == 0 else c.psum_tp2
                pg.emit("tensor", lambda e, ki=ki, ps=ps: e.transpose(
                    ps[:, :], pbf2[:, ki * P:(ki + 1) * P], idb[:, :]),
                    wait=[(evv, evv.now)] if ki == 0 else [(evv, evv.now)],
                    inc=[(mm, 1)])
                pg.emit("vector", lambda e, ki=ki, ps=ps: e.tensor_copy(
                    pT[:, ki, :], ps[:, :]),
                    wait=[(mm, mm.now)], drain=True, inc=[(evv, 1)])
            pvb = c.pb2 if h % 2 == 0 else c.pb3
            def pvmm(e, h=h, pb=pb, ks=ks, pvb=pvb):
                r = None
                for ki in range(5):
                    r = e.matmul(pvb[pb:pb + 64, 0:P], Vt[:, ks // P + ki, h * 64:(h + 1) * 64],
                                 pT[:, ki, :], start=(ki == 0), stop=(ki == 4))
                return r
            pg.emit("tensor", pvmm, wait=[(evv, evv.now)] + (
                [(evv, evv.now)] if h < 2 else [(evv, evv.now)]), inc=[(mm, 1)])
            mmv = mm.now
            pg.emit("vector", lambda e, h=h, pb=pb, pvb=pvb: e.tensor_scalar_add(
                ao[pb:pb + 64, h // 2, :], pvb[pb:pb + 64, 0:P],
                vecs[pb:pb + 64, VI["bv"], h // 2:h // 2 + 1]),
                wait=[(mm, mmv)], drain=True, inc=[(evv, 1)])

        for ct in range(CC):
            jy = None
            pg.emit("tensor", lambda e, ct=ct: _wdmm(e, c.pbs[:, 0:P], wpb, ao, ct),
                    wait=[(evv, evv.now), (fl, fl_t)], inc=[(mm, 1)])
            mmv = mm.now
            pg.emit("scalar", lambda e, ct=ct: e.activation(
                sf[:, :], Ftile[:, ct, :], AF.Silu),
                wait=[(fl, fl_t), (evv, evv.now)], drain=True, inc=[(evs, 1)])
            pg.emit("vector", lambda e, ct=ct: e.tensor_scalar_add(
                ss[:, :], c.pbs[:, 0:P], vecs[:, VI["bd"], ct:ct + 1]),
                wait=[(mm, mmv), (evs, evs.now)], drain=True)
            pg.emit("vector", lambda e: e.tensor_tensor(
                ss2[:, :], ss[:, :], sf[:, :], ALU.mult), drain=True)
            pg.emit("vector", lambda e, ct=ct: e.tensor_tensor(
                x1b[:, ct, :], ss2[:, :], xfm[:, ct, :], ALU.add),
                wait=([(odma, odma.now)] if ct == 0 else []),
                drain=True, inc=[(evv, 1)])
        pg.emit("sync", lambda e, q0=q0: e.dma_start(
            out=x1_hbm[:, :, q0:q0 + P].rearrange("cc p t -> p cc t"),
            in_=x1b[:, :, :]),
            wait=[(evv, evv.now), (odma, odma.now)], inc=[(odma, 16)])

    c.last_done = [(odma, odma.now), (evv, evv.now), (evs, evs.now), (mm, mm.now)]


def _wdmm(e, psum, wd, ao, ct):
    r = None
    for kc in range(CC):
        r = e.matmul(psum, wd[:, kc, ct * P:(ct + 1) * P], ao[:, kc, :],
                     start=(kc == 0), stop=(kc == CC - 1))
    return r


def emit_B1(c):
    nc, pg = c.nc, c.pg
    c.arena_reset()
    x1t = c.arena_alloc("b1_x1t", [P, CC, 512], F32)
    xat = c.arena_alloc("b1_xat", [P, CC, 512], F32)
    sq = c.arena_alloc("b1_sq", [P, 512], BF16)
    rows = c.arena_alloc("b1_rows", [1, 8, 512], F32)
    t1 = c.arena_alloc("b1_t1", [P, 512], F32)

    ld = pg.sem("b1_ld")
    mm = pg.sem("b1_mm")
    evv = pg.sem("b1_evv")
    evs = pg.sem("b1_evs")
    odma = pg.sem("b1_odma")

    x1_hbm = c.res["x1_hbm"].ap()
    xa_hbm = c.res["xa_hbm"].ap()
    xn_bf = c.res["xn_bf"]
    onef = c.res["ones_col_f"]
    oneb = c.res["ones_col_bf"]
    onerf = c.res["ones_row_f"]
    cval = c.res["cval"]
    prev = list(c.last_done)

    for tt in range(4):
        t0 = tt * 512
        w0 = list(prev) if tt == 0 else [(evv, evv.now)]
        pg.emit("sync", lambda e, t0=t0: e.dma_start(
            out=x1t[:, :, :],
            in_=x1_hbm[:, :, t0:t0 + 512].rearrange("cc p t -> p cc t")),
            wait=w0, inc=[(ld, 16)])
        ldv = ld.now
        def sxmm(e):
            r = None
            for cc in range(CC):
                r = e.matmul(c.pbs[0:1, :], onef[:, 0:1], x1t[:, cc, :],
                             start=(cc == 0), stop=(cc == CC - 1))
            return r
        pg.emit("tensor", sxmm, wait=[(ld, ldv), (evv, evv.now)], inc=[(mm, 1)])
        pg.emit("vector", lambda e: e.tensor_copy(rows[:, 0, :], c.pbs[0:1, :]),
                wait=[(mm, mm.now)], drain=True, inc=[(evv, 1)])
        for cc in range(CC):
            pg.emit("scalar", lambda e, cc=cc: e.activation(
                sq[:, :], x1t[:, cc, :], AF.Square),
                wait=[(ld, ldv), (mm, mm.now)],
                drain=True, inc=[(evs, 1)])
            pg.emit("tensor", lambda e, cc=cc: e.matmul(
                c.pbu[0:1, :], oneb[:, 0:1], sq[:, :],
                start=(cc == 0), stop=(cc == CC - 1)),
                wait=[(evs, evs.now)], inc=[(mm, 1)])
        pg.emit("vector", lambda e: e.tensor_copy(rows[:, 1, :], c.pbu[0:1, :]),
                wait=[(mm, mm.now)], drain=True, inc=[(evv, 1)])
        pg.emit("vector", lambda e: e.tensor_scalar_mul(rows[:, 2, :], rows[:, 0, :],
                                                        1.0 / C), drain=True)
        pg.emit("vector", lambda e: e.tensor_scalar_mul(rows[:, 3, :], rows[:, 2, :],
                                                        -1.0), drain=True)
        pg.emit("vector", lambda e: e.tensor_tensor(rows[:, 4, :], rows[:, 2, :],
                                                    rows[:, 2, :], ALU.mult), drain=True)
        pg.emit("vector", lambda e: e.scalar_tensor_tensor(
            rows[:, 5, :], rows[:, 1, :], 1.0 / C, rows[:, 4, :],
            ALU.mult, ALU.subtract), drain=True, inc=[(evv, 1)])
        pg.emit("scalar", lambda e: e.activation(rows[:, 6, :], rows[:, 5, :], AF.Sqrt,
                                                 bias=cval[0:1, CVAL["eps_ln"]:CVAL["eps_ln"] + 1]),
                wait=[(evv, evv.now)], drain=True, inc=[(evs, 1)])
        pg.emit("vector", lambda e: e.reciprocal(rows[:, 7, :], rows[:, 6, :]),
                wait=[(evs, evs.now)], drain=True, inc=[(evv, 1)])
        pg.emit("tensor", lambda e: e.matmul(c.pbu[:, :], onerf[0:1, :], rows[:, 3, :],
                                             start=True, stop=True),
                wait=[(evv, evv.now)], inc=[(mm, 1)])
        pg.emit("tensor", lambda e: e.matmul(c.pbs[:, :], onerf[0:1, :], rows[:, 7, :],
                                             start=True, stop=True),
                wait=[], inc=[(mm, 1)])
        mmv = mm.now
        for cc in range(CC):
            pg.emit("vector", lambda e, cc=cc: e.tensor_tensor(
                t1[:, :], x1t[:, cc, :], c.pbu[:, :], ALU.add),
                wait=[(mm, mmv)] + ([(odma, odma.now)] if cc == 0 else []),
                drain=True)
            pg.emit("vector", lambda e, cc=cc: e.tensor_tensor(
                xat[:, cc, :], t1[:, :], c.pbs[:, :], ALU.mult), drain=True)
            pg.emit("vector", lambda e, cc=cc, t0=t0: e.tensor_copy(
                xn_bf[:, cc, t0:t0 + 512], xat[:, cc, :]), drain=True, inc=[(evv, 1)])
        pg.emit("gpsimd", lambda e, t0=t0: e.dma_start(
            out=xa_hbm[:, :, t0:t0 + 512].rearrange("cc p t -> p cc t"),
            in_=xat[:, :, :]), wait=[(evv, evv.now)], inc=[(odma, 16)])

    c.last_done = [(odma, odma.now), (evv, evv.now)]


def _rms_rows(c, pg, src_tile, sq, rows, mm, evv, evs, ld_w, eps_key):
    oneb = c.res["ones_col_bf"]
    cval = c.res["cval"]
    for cc in range(CC):
        pg.emit("scalar", lambda e, cc=cc: e.activation(
            sq[:, :], src_tile[:, cc, :], AF.Square),
            wait=list(ld_w) + [(mm, mm.now)],
            drain=True, inc=[(evs, 1)])
        pg.emit("tensor", lambda e, cc=cc: e.matmul(
            c.pbu[0:1, :], oneb[:, 0:1], sq[:, :],
            start=(cc == 0), stop=(cc == CC - 1)),
            wait=[(evs, evs.now)], inc=[(mm, 1)])
    pg.emit("vector", lambda e: e.tensor_copy(rows[:, 0, :], c.pbu[0:1, :]),
            wait=[(mm, mm.now)], drain=True, inc=[(evv, 1)])
    pg.emit("vector", lambda e: e.tensor_scalar_mul(rows[:, 1, :], rows[:, 0, :],
                                                    1.0 / C), drain=True, inc=[(evv, 1)])
    pg.emit("scalar", lambda e: e.activation(
        rows[:, 2, :], rows[:, 1, :], AF.Sqrt,
        bias=cval[0:1, CVAL[eps_key]:CVAL[eps_key] + 1]),
        wait=[(evv, evv.now)], drain=True, inc=[(evs, 1)])
    pg.emit("vector", lambda e: e.reciprocal(rows[:, 3, :], rows[:, 2, :]),
            wait=[(evs, evs.now)], drain=True, inc=[(evv, 1)])


def emit_B2(c):
    nc, pg = c.nc, c.pg
    c.arena_reset()
    xat = c.arena_alloc("b2_xat", [P, CC, 512], F32)
    hb = c.arena_alloc("b2_h", [P, CC, 513], F32)
    hl = c.arena_alloc("b2_hl", [P, CC, 1], F32)
    sq = c.arena_alloc("b2_sq", [P, 512], BF16)
    rows = c.arena_alloc("b2_rows", [1, 4, 512], F32)
    d = c.arena_alloc("b2_d", [P, 512], F32)
    xk = c.arena_alloc("b2_xk", [P, CC, 512], BF16)
    xv = c.arena_alloc("b2_xv", [P, CC, 512], BF16)
    xr = c.arena_alloc("b2_xr", [P, CC, 512], BF16)
    wb = c.arena_alloc("b2_wb", [P, CC, C], BF16)
    kf = c.arena_alloc("b2_kf", [P, CC, 512], F32)
    vf = c.arena_alloc("b2_vf", [P, CC, 512], F32)
    rcc = c.arena_alloc("b2_rcc", [P, 512], F32)
    rwkv = c.arena_alloc("b2_rwkv", [P, CC, 512], BF16)
    abuf_off = c.arena_off
    abuf = c.arena_alloc("b2_ab", [P, 513], F32)
    bbuf = c.arena_alloc("b2_bb", [P, 513], F32)
    ekt = c.arena_alloc("b2_ekt", [P, 512], F32)
    ek = c.arena_alloc("b2_ek", [P, 512], F32)
    ekv = c.arena_alloc("b2_ekv", [P, 512], F32)
    num = ekv
    den = ek
    t5 = c.arena_alloc_at("b2_t5", [P, 512], F32, abuf_off)
    ast = c.arena_alloc("b2_ast", [P, CC], F32)
    bst = c.arena_alloc("b2_bst", [P, CC], F32)

    for nm, t in (("b2_kf", kf), ("b2_vf", vf), ("b2_rwkv", rwkv),
                  ("b2_xk", xk), ("b2_h", hb), ("b2_ast", ast), ("b2_bst", bst)):
        c.res[nm] = t

    ld = pg.sem("b2_ld")
    wdma = pg.sem("b2_wdma")
    mm = pg.sem("b2_mm")
    evv = pg.sem("b2_evv")
    evs = pg.sem("b2_evs")
    odma = pg.sem("b2_odma")

    xa_hbm = c.res["xa_hbm"].ap()
    vecs = c.res["vecs"]
    dvec = c.res["dvec"]
    cval = c.res["cval"]
    onerf = c.res["ones_row_f"]
    prev = list(c.last_done)

    pg.emit("gpsimd", lambda e: e.memset(ast[:, :], 0.0), wait=prev, drain=True,
            inc=[(odma, 1)])
    pg.emit("gpsimd", lambda e: e.memset(bst[:, :], 0.0), drain=True, inc=[(odma, 1)])
    pg.emit("gpsimd", lambda e: e.memset(hl[:, :, :], 0.0), drain=True, inc=[(odma, 1)])
    init_v = odma.now

    banks = Banks([c.pb01[:, 0:512], c.pb01[:, 512:1024], c.pb2[:, :], c.pb3[:, :]])

    for ti in range(4):
        t0 = ti * 512
        w0 = [(odma, init_v)] if ti == 0 else [(evv, evv.now), (odma, odma.now)]
        pg.emit("sync", lambda e, t0=t0: e.dma_start(
            out=xat[:, :, :],
            in_=xa_hbm[:, :, t0:t0 + 512].rearrange("cc p t -> p cc t")),
            wait=w0 + list(prev if ti == 0 else []), inc=[(ld, 16)])
        ldv = ld.now
        _rms_rows(c, pg, xat, sq, rows, mm, evv, evs, [(ld, ldv)], "eps_rms")
        pg.emit("tensor", lambda e: e.matmul(c.pbs[:, :], onerf[0:1, :], rows[:, 3, :],
                                             start=True, stop=True),
                wait=[(evv, evv.now)], inc=[(mm, 1)])
        mmv = mm.now
        for cc in range(CC):
            pg.emit("vector", lambda e, cc=cc: e.scalar_tensor_tensor(
                hb[:, cc, 1:513], xat[:, cc, :], vecs[:, VI["ln1_w"], cc:cc + 1],
                c.pbs[:, :], ALU.mult, ALU.mult),
                wait=[(mm, mmv)], drain=True, inc=[(evv, 1)])
        pg.emit("vector", lambda e: e.tensor_copy(hb[:, :, 0:1], hl[:, :, :]),
                drain=True, inc=[(evv, 1)])
        for cc in range(CC):
            pg.emit("vector", lambda e, cc=cc: e.tensor_tensor(
                d[:, :], hb[:, cc, 1:513], hb[:, cc, 0:512], ALU.subtract),
                drain=True)
            for mix, dst in (("time_mix_k", xk), ("time_mix_v", xv), ("time_mix_r", xr)):
                pg.emit("vector", lambda e, cc=cc, mix=mix, dst=dst: e.scalar_tensor_tensor(
                    dst[:, cc, :], d[:, :], vecs[:, VI[mix], cc:cc + 1],
                    hb[:, cc, 0:512], ALU.mult, ALU.add), drain=True, inc=[(evv, 1)])
        mixes_done = evv.now
        pg.emit("vector", lambda e: e.tensor_copy(hl[:, :, :], hb[:, :, 512:513]),
                drain=True, inc=[(evv, 1)])

        for wname, xsrc, dst in (("L1_attn_W_key", xk, kf),
                                 ("L1_attn_W_value", xv, vf)):
            src_ap = c.inp[wname].ap().rearrange("(kc p) f -> p kc f", p=P)
            pg.emit("gpsimd", lambda e, src_ap=src_ap: e.dma_start(
                out=wb[:, :, :], in_=src_ap), wait=[(mm, mm.now)], inc=[(wdma, 16)])
            wdv = wdma.now
            for co in range(CC):
                jj, pp, ww = banks.acquire()
                pg.emit("tensor", lambda e, co=co, xsrc=xsrc, pp=pp: _kmm2(
                    e, pp, wb, xsrc, co),
                    wait=[(wdma, wdv), (evv, mixes_done)] + ww, inc=[(mm, 1)])
                mmv2 = mm.now
                pg.emit("vector", lambda e, dst=dst, co=co, pp=pp: e.tensor_copy(
                    dst[:, co, :], pp), wait=[(mm, mmv2)], drain=True, inc=[(evv, 1)])
                banks.release(jj, evv)
        src_ap = c.inp["L1_attn_W_recept"].ap().rearrange("(kc p) f -> p kc f", p=P)
        pg.emit("gpsimd", lambda e, src_ap=src_ap: e.dma_start(
            out=wb[:, :, :], in_=src_ap), wait=[(mm, mm.now)], inc=[(wdma, 16)])
        wd_rec = wdma.now

        for cc in range(CC):
            pg.emit("vector", lambda e, cc=cc: e.tensor_scalar(
                ekt[:, :], kf[:, cc, :], vecs[:, VI["time_first"], cc:cc + 1], 20.0,
                ALU.add, ALU.min), wait=[(evs, evs.now), (evv, evv.now)], drain=True)
            pg.emit("vector", lambda e: e.tensor_scalar_max(ekt[:, :], ekt[:, :], -20.0),
                    drain=True, inc=[(evv, 1)])
            pg.emit("scalar", lambda e: e.activation(ekt[:, :], ekt[:, :], AF.Exp),
                    wait=[(evv, evv.now)], drain=True, inc=[(evs, 1)])
            pg.emit("vector", lambda e, cc=cc: e.tensor_scalar(
                ek[:, :], kf[:, cc, :], 20.0, -20.0, ALU.min, ALU.max), drain=True,
                inc=[(evv, 1)])
            pg.emit("scalar", lambda e: e.activation(ek[:, :], ek[:, :], AF.Exp),
                    wait=[(evv, evv.now)], drain=True, inc=[(evs, 1)])
            pg.emit("vector", lambda e, cc=cc: e.tensor_tensor(
                ekv[:, :], ek[:, :], vf[:, cc, :], ALU.mult),
                wait=[(evs, evs.now)], drain=True)
            pg.emit("vector", lambda e, cc=cc: e.tensor_copy(abuf[:, 0:1],
                                                             ast[:, cc:cc + 1]),
                    drain=True)
            pg.emit("vector", lambda e, cc=cc: e.tensor_tensor_scan(
                abuf[:, 1:513], dvec[:, DV["ew"], cc:cc + 1].broadcast_to([P, 512]),
                ekv[:, :], ast[:, cc:cc + 1], ALU.mult, ALU.add), drain=True)
            pg.emit("vector", lambda e, cc=cc: e.tensor_copy(ast[:, cc:cc + 1],
                                                             abuf[:, 512:513]),
                    drain=True)
            pg.emit("vector", lambda e, cc=cc: e.tensor_copy(bbuf[:, 0:1],
                                                             bst[:, cc:cc + 1]),
                    drain=True)
            pg.emit("vector", lambda e, cc=cc: e.tensor_tensor_scan(
                bbuf[:, 1:513], dvec[:, DV["ew"], cc:cc + 1].broadcast_to([P, 512]),
                ek[:, :], bst[:, cc:cc + 1], ALU.mult, ALU.add), drain=True)
            pg.emit("vector", lambda e, cc=cc: e.tensor_copy(bst[:, cc:cc + 1],
                                                             bbuf[:, 512:513]),
                    drain=True)
            pg.emit("vector", lambda e, cc=cc: e.tensor_tensor(
                num[:, :], ekt[:, :], vf[:, cc, :], ALU.mult), drain=True)
            pg.emit("vector", lambda e: e.tensor_tensor(
                num[:, :], num[:, :], abuf[:, 0:512], ALU.add), drain=True)
            pg.emit("vector", lambda e: e.scalar_tensor_tensor(
                den[:, :], ekt[:, :], 1e-8, bbuf[:, 0:512], ALU.add, ALU.add),
                drain=True)
            pg.emit("vector", lambda e: e.reciprocal(t5[:, :], den[:, :]), drain=True)
            pg.emit("vector", lambda e, cc=cc: e.tensor_tensor(
                num[:, :], num[:, :], t5[:, :], ALU.mult), drain=True,
                inc=[(evv, 1)])
            jj, pp, ww = banks.acquire()
            pg.emit("tensor", lambda e, cc=cc, pp=pp: _kmm2(e, pp, wb, xr, cc),
                    wait=[(wdma, wd_rec)] + ww, inc=[(mm, 1)])
            mmv3 = mm.now
            pg.emit("scalar", lambda e, pp=pp: e.activation(rcc[:, :], pp, AF.Sigmoid),
                    wait=[(mm, mmv3), (evv, evv.now)], drain=True, inc=[(evs, 1)])
            banks.release(jj, evs)
            pg.emit("vector", lambda e, cc=cc: e.tensor_tensor(
                rwkv[:, cc, :], num[:, :], rcc[:, :], ALU.mult),
                wait=[(evs, evs.now)], drain=True, inc=[(evv, 1)])
        scan_done = evv.now

        src_ap = c.inp["L1_attn_W_out"].ap().rearrange("(kc p) f -> p kc f", p=P)
        pg.emit("gpsimd", lambda e, src_ap=src_ap: e.dma_start(
            out=wb[:, :, :], in_=src_ap), wait=[(mm, mm.now)], inc=[(wdma, 16)])
        wdv = wdma.now
        for co in range(CC):
            jj, pp, ww = banks.acquire()
            pg.emit("tensor", lambda e, co=co, pp=pp: _kmm2(e, pp, wb, rwkv, co),
                    wait=[(wdma, wdv), (evv, scan_done)] + ww, inc=[(mm, 1)])
            mmv2 = mm.now
            pg.emit("vector", lambda e, co=co, pp=pp: e.tensor_tensor(
                xat[:, co, :], xat[:, co, :], pp, ALU.add),
                wait=[(mm, mmv2)], drain=True, inc=[(evv, 1)])
            banks.release(jj, evv)
        pg.emit("gpsimd", lambda e, t0=t0: e.dma_start(
            out=xa_hbm[:, :, t0:t0 + 512].rearrange("cc p t -> p cc t"),
            in_=xat[:, :, :]), wait=[(evv, evv.now), (odma, odma.now)],
            inc=[(odma, 16)])

    c.last_done = [(odma, odma.now), (evv, evv.now), (evs, evs.now), (mm, mm.now)]


def _kmm2(e, psum, wbuf, act, co):
    r = None
    for kc in range(CC):
        r = e.matmul(psum, wbuf[:, kc, co * P:(co + 1) * P], act[:, kc, :],
                     start=(kc == 0), stop=(kc == CC - 1))
    return r


def emit_B3(c):
    nc, pg = c.nc, c.pg
    c.arena_reset()
    xat = c.arena_alloc("b3_xat", [P, CC, 512], F32)
    hb = c.arena_alloc("b3_h", [P, CC, 513], F32)
    hl = c.arena_alloc("b3_hl", [P, CC, 1], F32)
    sq = c.arena_alloc("b3_sq", [P, 512], BF16)
    rows = c.arena_alloc("b3_rows", [1, 4, 512], F32)
    d = c.arena_alloc("b3_d", [P, 512], F32)
    xk2 = c.arena_alloc("b3_xk2", [P, CC, 512], BF16)
    xr2 = c.arena_alloc("b3_xr2", [P, CC, 512], BF16)
    wk2h = c.arena_alloc("b3_wk2", [P, CC, 2048], BF16)
    r2 = c.arena_alloc("b3_r2", [P, CC, 512], BF16)
    k2 = c.arena_alloc("b3_k2", [P, 32, 512], BF16)
    wv2 = [c.arena_alloc(f"b3_wv2{i}", [P, C], BF16) for i in range(2)]
    rl = c.arena_alloc("b3_rl", [P, 512], F32)
    t5 = c.arena_alloc("b3_t5", [P, 512], F32)
    wr2 = c.arena_alloc_at("b3_wr2", [P, CC, C], BF16,
                           nc.lookup_mloc(wk2h).addr - c.arena_addr)

    ld = pg.sem("b3_ld")
    wdma = pg.sem("b3_wdma")
    mm = pg.sem("b3_mm")
    evv = pg.sem("b3_evv")
    evs = pg.sem("b3_evs")
    odma = pg.sem("b3_odma")

    xa_hbm = c.res["xa_hbm"].ap()
    vecs = c.res["vecs"]
    onerf = c.res["ones_row_f"]
    prev = list(c.last_done)

    pg.emit("gpsimd", lambda e: e.memset(hl[:, :, :], 0.0), wait=prev, drain=True,
            inc=[(odma, 1)])
    init_v = odma.now

    banks = Banks([c.pb01[:, 0:512], c.pb01[:, 512:1024], c.pb2[:, :], c.pb3[:, :]])

    for ti in range(4):
        t0 = ti * 512
        w0 = [(odma, init_v)] + list(prev) if ti == 0 else [(evv, evv.now), (odma, odma.now)]
        pg.emit("sync", lambda e, t0=t0: e.dma_start(
            out=xat[:, :, :],
            in_=xa_hbm[:, :, t0:t0 + 512].rearrange("cc p t -> p cc t")),
            wait=w0, inc=[(ld, 16)])
        ldv = ld.now
        _rms_rows(c, pg, xat, sq, rows, mm, evv, evs, [(ld, ldv)], "eps_rms")
        pg.emit("tensor", lambda e: e.matmul(c.pbs[:, :], onerf[0:1, :], rows[:, 3, :],
                                             start=True, stop=True),
                wait=[(evv, evv.now)], inc=[(mm, 1)])
        mmv = mm.now
        for cc in range(CC):
            pg.emit("vector", lambda e, cc=cc: e.scalar_tensor_tensor(
                hb[:, cc, 1:513], xat[:, cc, :], vecs[:, VI["ln2_w"], cc:cc + 1],
                c.pbs[:, :], ALU.mult, ALU.mult),
                wait=[(mm, mmv)], drain=True, inc=[(evv, 1)])
        pg.emit("vector", lambda e: e.tensor_copy(hb[:, :, 0:1], hl[:, :, :]),
                drain=True, inc=[(evv, 1)])
        for cc in range(CC):
            pg.emit("vector", lambda e, cc=cc: e.tensor_tensor(
                d[:, :], hb[:, cc, 1:513], hb[:, cc, 0:512], ALU.subtract),
                drain=True)
            for mix, dst in (("time_mix_k2", xk2), ("time_mix_r2", xr2)):
                pg.emit("vector", lambda e, cc=cc, mix=mix, dst=dst: e.scalar_tensor_tensor(
                    dst[:, cc, :], d[:, :], vecs[:, VI[mix], cc:cc + 1],
                    hb[:, cc, 0:512], ALU.mult, ALU.add), drain=True, inc=[(evv, 1)])
        mixes_done = evv.now
        pg.emit("vector", lambda e: e.tensor_copy(hl[:, :, :], hb[:, :, 512:513]),
                drain=True, inc=[(evv, 1)])

        src_ap = c.inp["L1_attn_W_recept2"].ap().rearrange("(kc p) f -> p kc f", p=P)
        pg.emit("gpsimd", lambda e, src_ap=src_ap: e.dma_start(
            out=wr2[:, :, :], in_=src_ap), wait=[(mm, mm.now)], inc=[(wdma, 16)])
        wdv = wdma.now
        for co in range(CC):
            jj, pp, ww = banks.acquire()
            pg.emit("tensor", lambda e, co=co, pp=pp: _kmm2(e, pp, wr2, xr2, co),
                    wait=[(wdma, wdv), (evv, mixes_done)] + ww, inc=[(mm, 1)])
            pg.emit("scalar", lambda e, co=co, pp=pp: e.activation(
                r2[:, co, :], pp, AF.Sigmoid),
                wait=[(mm, mm.now)], drain=True, inc=[(evs, 1)])
            banks.release(jj, evs)

        for fh in range(2):
            src_ap = c.inp["L1_attn_W_key2"].ap()[:, fh * 2048:(fh + 1) * 2048]
            src_ap = src_ap.rearrange("(kc p) f -> p kc f", p=P)
            pg.emit("gpsimd", lambda e, src_ap=src_ap: e.dma_start(
                out=wk2h[:, :, :], in_=src_ap), wait=[(mm, mm.now)], inc=[(wdma, 16)])
            wdv = wdma.now
            for f2 in range(FC):
                jj, pp, ww = banks.acquire()
                pg.emit("tensor", lambda e, f2=f2, pp=pp: _kmm2(e, pp, wk2h, xk2, f2),
                        wait=[(wdma, wdv)] + ww, inc=[(mm, 1)])
                mmv2 = mm.now
                pg.emit("scalar", lambda e, pp=pp: e.activation(rl[:, :], pp, AF.Relu),
                        wait=[(mm, mmv2), (evs, evs.now)], drain=True, inc=[(evs, 1)])
                banks.release(jj, evs)
                pg.emit("scalar", lambda e, fh=fh, f2=f2: e.activation(
                    k2[:, fh * FC + f2, :], rl[:, :], AF.Square),
                    drain=True, inc=[(evs, 1)])
        k2_done = evs.now

        chunk_hist = []
        entry_mm = mm.now
        for ch in range(2):
            pslots = [c.pb01[:, 0:512], c.pb01[:, 512:1024], c.pb2[:, :], c.pb3[:, :]]
            for kc2 in range(32):
                gidx = len(chunk_hist)
                wvb = wv2[gidx % 2]
                src_ap = c.inp["L1_attn_W_value2"].ap()[kc2 * P:(kc2 + 1) * P, :]
                wwv = [(mm, chunk_hist[gidx - 2])] if gidx >= 2 else [(mm, entry_mm)]
                pg.emit("gpsimd", lambda e, wvb=wvb, src_ap=src_ap: e.dma_start(
                    out=wvb[:, :], in_=src_ap), wait=wwv, inc=[(wdma, 16)])
                wdv2 = wdma.now
                def y2mm(e, ch=ch, kc2=kc2, wvb=wvb, pslots=pslots):
                    r = None
                    for q in range(4):
                        ct = ch * 4 + q
                        r = e.matmul(pslots[q], wvb[:, ct * P:(ct + 1) * P],
                                     k2[:, kc2, :], start=(kc2 == 0), stop=(kc2 == 31))
                    return r
                pg.emit("tensor", y2mm, wait=[(wdma, wdv2), (evs, k2_done)] +
                        ([(evv, evv.now)] if kc2 == 0 else []), inc=[(mm, 1)])
                chunk_hist.append(mm.now)
            mmv3 = mm.now
            for q in range(4):
                ct = ch * 4 + q
                pg.emit("vector", lambda e, ct=ct, q=q, pslots=pslots: e.tensor_tensor(
                    t5[:, :], pslots[q], r2[:, ct, :], ALU.mult),
                    wait=[(mm, mmv3)], drain=True)
                pg.emit("vector", lambda e, ct=ct: e.tensor_tensor(
                    xat[:, ct, :], xat[:, ct, :], t5[:, :], ALU.add), drain=True,
                    inc=[(evv, 1)])
                banks.free[q] = (evv, evv.now)
            banks.i = 0
        pg.emit("gpsimd", lambda e, t0=t0: e.dma_start(
            out=xa_hbm[:, :, t0:t0 + 512].rearrange("cc p t -> p cc t"),
            in_=xat[:, :, :]), wait=[(evv, evv.now), (odma, odma.now)],
            inc=[(odma, 16)])

    c.last_done = [(odma, odma.now), (evv, evv.now), (evs, evs.now), (mm, mm.now)]


def emit_B4(c):
    nc, pg = c.nc, c.pg
    c.arena_reset()
    wgc = [c.arena_alloc(f"b4_wgc{i}", [P, C], BF16) for i in range(2)]
    wdc = c.arena_alloc("b4_wdc", [P, HID], BF16)

    ld = pg.sem("b4_ld")
    mm = pg.sem("b4_mm")
    evv = pg.sem("b4_evv")
    evs = pg.sem("b4_evs")
    odma = pg.sem("b4_odma")
    wdma = pg.sem("b4_wdma")

    idb = c.res["ident_bf"]
    cval = c.res["cval"]
    xn_bf = c.res["xn_bf"]
    vecs = c.res["vecs"]
    xa_hbm = c.res["xa_hbm"].ap()
    x1_hbm = c.res["x1_hbm"].ap()
    y_ap = c.res["y"].ap()
    wgT = c.res["wgT_hbm"].ap()
    wuT = c.res["wuT_hbm"].ap()
    wdT = c.res["wdT_hbm"].ap()
    prev = list(c.last_done)

    tbg = [c.arena_alloc(f"b4_tbg{i}", [P, HC, P], BF16) for i in range(2)]
    tp_hist = []
    grp = [0]
    grp_dma = []

    def tp_group(n_blocks, src_fn, dst_ap_fn, wait_first):
        gi = grp[0]
        grp[0] += 1
        buf = tbg[gi % 2]
        wgrp = list(wait_first)
        if gi >= 2:
            wgrp.append((odma, grp_dma[gi - 2]))
        for b in range(n_blocks):
            i = len(tp_hist)
            ps = c.psum_tp if i % 2 == 0 else c.psum_tp2
            w = list(wgrp) if b == 0 else []
            if i >= 2:
                w.append((evv, tp_hist[i - 2]))
            pg.emit("tensor", lambda e, ps=ps, b=b, src_fn=src_fn: e.transpose(
                ps[:, :], src_fn(b), idb[:, :]), wait=w, inc=[(mm, 1)])
            pg.emit("vector", lambda e, ps=ps, b=b, buf=buf: e.tensor_copy(
                buf[:, b, :], ps[:, :]), wait=[(mm, mm.now)], drain=True,
                inc=[(evv, 1)])
            tp_hist.append(evv.now)
        pg.emit("sync", lambda e, buf=buf, n_blocks=n_blocks, dst_ap_fn=dst_ap_fn:
                e.dma_start(out=dst_ap_fn(), in_=buf[:, 0:n_blocks, :]),
                wait=[(evv, evv.now)], inc=[(odma, 16)])
        grp_dma.append(odma.now)

    ld_hist = []
    for wi, (wname, dstT) in enumerate((("L1_ffn_w_gate", wgT), ("L1_ffn_w_up", wuT))):
        for hc in range(HC):
            gidx = wi * HC + hc
            wgb = wgc[gidx % 2]
            src_ap = c.inp[wname].ap()[hc * P:(hc + 1) * P, :]
            w0 = list(prev) if gidx == 0 else []
            if gidx >= 2:
                w0.append((mm, ld_hist[gidx - 2]))
            pg.emit("gpsimd", lambda e, wgb=wgb, src_ap=src_ap: e.dma_start(
                out=wgb[:, :], in_=src_ap), wait=w0, inc=[(ld, 16)])
            ldv = ld.now
            tp_group(CC, lambda b, wgb=wgb: wgb[:, b * P:(b + 1) * P],
                     lambda dstT=dstT, hc=hc: dstT[:, hc * P:(hc + 1) * P]
                     .rearrange("(dk p) h -> p dk h", p=P),
                     [(ld, ldv)])
            ld_hist.append(mm.now)
    for dk in range(CC):
        gidx = 2 * HC + dk
        src_ap = c.inp["L1_ffn_w_down"].ap()[dk * P:(dk + 1) * P, :]
        w0 = [(mm, ld_hist[gidx - 1])]
        pg.emit("gpsimd", lambda e, src_ap=src_ap: e.dma_start(
            out=wdc[:, :], in_=src_ap), wait=w0, inc=[(ld, 16)])
        ldv = ld.now
        tp_group(HC, lambda b: wdc[:, b * P:(b + 1) * P],
                 lambda dk=dk: wdT[:, dk * P:(dk + 1) * P]
                 .rearrange("(hk p) d -> p hk d", p=P),
                 [(ld, ldv)])
        ld_hist.append(mm.now)
    b4a_done = odma.now

    c.arena_reset()
    xat = c.arena_alloc("b4_xat", [P, CC, 512], F32)
    x1t = c.arena_alloc("b4_x1t", [P, CC, 512], F32)
    hdn = c.arena_alloc("b4_hdn", [P, HC, 512], BF16)
    wgTc = [c.arena_alloc(f"b4_wgT{i}", [P, CC, P], BF16) for i in range(2)]
    wuTc = [c.arena_alloc(f"b4_wuT{i}", [P, CC, P], BF16) for i in range(2)]
    wdTc = [c.arena_alloc(f"b4_wdT{i}", [P, HC, P], BF16) for i in range(2)]
    sgs = c.arena_alloc("b4_sgs", [P, 512], BF16)
    sd = c.arena_alloc("b4_sd", [P, 512], F32)
    t6 = c.arena_alloc("b4_t6", [P, 512], F32)
    ob = c.arena_alloc("b4_ob", [P, CC, 512], F32)
    otm = c.arena_alloc("b4_otm", [P, 4, C], F32)

    banks = Banks([c.pb01[:, 0:512], c.pb01[:, 512:1024], c.pb2[:, :], c.pb3[:, :]])
    iv = CVAL["inv127"]
    c.b4_hist = []
    c.b4_entry = mm.now
    c.b4d_hist = []

    for ti in range(4):
        t0 = ti * 512
        w0 = [(odma, b4a_done)] if ti == 0 else [(evv, evv.now)]
        pg.emit("sync", lambda e, t0=t0: e.dma_start(
            out=xat[:, :, :],
            in_=xa_hbm[:, :, t0:t0 + 512].rearrange("cc p t -> p cc t")),
            wait=w0, inc=[(ld, 16)])
        pg.emit("sync", lambda e, t0=t0: e.dma_start(
            out=x1t[:, :, :],
            in_=x1_hbm[:, :, t0:t0 + 512].rearrange("cc p t -> p cc t")),
            wait=[], inc=[(ld, 16)])
        ldv = ld.now
        for hc in range(HC):
            wga = wgTc[hc % 2]
            wua = wuTc[hc % 2]
            gidx = ti * HC + hc
            if gidx >= 2:
                wwar = [(mm, c.b4_hist[gidx - 2])]
            else:
                wwar = [(mm, c.b4_entry)]
            pg.emit("gpsimd", lambda e, wga=wga, hc=hc: e.dma_start(
                out=wga[:, :, :],
                in_=wgT[:, hc * P:(hc + 1) * P].rearrange("(dk p) h -> p dk h", p=P)),
                wait=[(odma, b4a_done)] + wwar, inc=[(wdma, 16)])
            pg.emit("gpsimd", lambda e, wua=wua, hc=hc: e.dma_start(
                out=wua[:, :, :],
                in_=wuT[:, hc * P:(hc + 1) * P].rearrange("(dk p) h -> p dk h", p=P)),
                wait=[], inc=[(wdma, 16)])
            wdv = wdma.now
            jg, pG, wG = banks.acquire()
            ju, pU, wU = banks.acquire()
            def gatemm(e, wga=wga, t0=t0, pG=pG):
                r = None
                for kc in range(CC):
                    r = e.matmul(pG, wga[:, kc, :], xn_bf[:, kc, t0:t0 + 512],
                                 start=(kc == 0), stop=(kc == CC - 1))
                return r
            def upmm(e, wua=wua, t0=t0, pU=pU):
                r = None
                for kc in range(CC):
                    r = e.matmul(pU, wua[:, kc, :], xn_bf[:, kc, t0:t0 + 512],
                                 start=(kc == 0), stop=(kc == CC - 1))
                return r
            pg.emit("tensor", gatemm, wait=[(wdma, wdv), (ld, ldv)] + wG + wU,
                    inc=[(mm, 1)])
            pg.emit("tensor", upmm, wait=[], inc=[(mm, 1)])
            mmv = mm.now
            c.b4_hist.append(mm.now)
            pg.emit("scalar", lambda e, pG=pG: e.activation(
                sgs[:, :], pG, AF.Silu, scale=cval[:, iv:iv + 1]),
                wait=[(mm, mmv), (evv, evv.now)], drain=True, inc=[(evs, 1)])
            banks.release(jg, evs)
            pg.emit("vector", lambda e, hc=hc, pU=pU: e.scalar_tensor_tensor(
                hdn[:, hc, :], pU, cval[:, iv:iv + 1], sgs[:, :],
                ALU.mult, ALU.mult),
                wait=[(evs, evs.now)], drain=True, inc=[(evv, 1)])
            banks.release(ju, evv)
        hdn_done = evv.now

        for ct in range(CC):
            wda = wdTc[ct % 2]
            gidx2 = ti * CC + ct
            if gidx2 >= 2:
                wwar = [(mm, c.b4d_hist[gidx2 - 2])]
            else:
                wwar = [(mm, c.b4_entry)]
            pg.emit("gpsimd", lambda e, wda=wda, ct=ct: e.dma_start(
                out=wda[:, :, :],
                in_=wdT[:, ct * P:(ct + 1) * P].rearrange("(hk p) d -> p hk d", p=P)),
                wait=wwar, inc=[(wdma, 16)])
            wdv = wdma.now
            jd, pD, wD = banks.acquire()
            def downmm(e, wda=wda, pD=pD):
                r = None
                for hk in range(HC):
                    r = e.matmul(pD, wda[:, hk, :], hdn[:, hk, :],
                                 start=(hk == 0), stop=(hk == HC - 1))
                return r
            pg.emit("tensor", downmm, wait=[(wdma, wdv), (evv, hdn_done)] + wD,
                    inc=[(mm, 1)])
            mmv = mm.now
            c.b4d_hist.append(mm.now)
            pg.emit("scalar", lambda e, pD=pD: e.activation(
                sd[:, :], pD, AF.Silu, scale=cval[:, iv:iv + 1]),
                wait=[(mm, mmv), (evv, evv.now)], drain=True, inc=[(evs, 1)])
            banks.release(jd, evs)
            pg.emit("vector", lambda e, ct=ct: e.tensor_tensor(
                t6[:, :], sd[:, :], xat[:, ct, :], ALU.mult),
                wait=[(evs, evs.now)], drain=True)
            pg.emit("vector", lambda e, ct=ct: e.tensor_tensor(
                ob[:, ct, :], t6[:, :], x1t[:, ct, :], ALU.add), drain=True,
                inc=[(evv, 1)])
        ob_done = evv.now

        for j in range(4):
            for ct in range(CC):
                wtp = [(evv, evv.now)]
                if ct == 0:
                    wtp.append((odma, odma.now))
                pg.emit("tensor", lambda e, j=j, ct=ct: e.transpose(
                    c.pbu[:, 0:P], ob[:, ct, j * P:(j + 1) * P],
                    c.res["ident_f"][:, :]),
                    wait=wtp, inc=[(mm, 1)])
                pg.emit("vector", lambda e, j=j, ct=ct: e.tensor_copy(
                    otm[:, j, ct * P:(ct + 1) * P], c.pbu[:, 0:P]),
                    wait=[(mm, mm.now)], drain=True, inc=[(evv, 1)])
            pg.emit("sync", lambda e, j=j, t0=t0: e.dma_start(
                out=y_ap[t0 + j * P: t0 + (j + 1) * P, :], in_=otm[:, j, :]),
                wait=[(evv, evv.now)], inc=[(odma, 16)])

    c.last_done = [(odma, odma.now), (evv, evv.now), (evs, evs.now)]
    c.final_waits.append((odma, odma.now))



def kernel(**inputs):
    nc, c = build_program()
    in_maps = []
    x = np.ascontiguousarray(inputs["L0_attn_x"])
    for b in range(8):
        m = {"x": np.ascontiguousarray(x[b])}
        for k, v in inputs.items():
            if k == "L0_attn_x":
                continue
            m[k] = np.ascontiguousarray(v)
        in_maps.append(m)
    res = run_bass_kernel_spmd(nc, in_maps, core_ids=list(range(8)))
    return np.stack([res.results[b]["y"] for b in range(8)], axis=0)


# version 06
# speedup vs baseline: 1.2652x; 1.1288x over previous
import numpy as np
import ml_dtypes
import concourse.bass as bass
import concourse.mybir as mybir
from concourse.bass_utils import run_bass_kernel_spmd

F32 = mybir.dt.float32
BF16 = mybir.dt.bfloat16
I32 = mybir.dt.int32
AF = mybir.ActivationFunctionType
ALU = mybir.AluOpType
AX = mybir.AxisListType

P = 128
T = 2048
CC = 8
C = 1024
NHEAD = 16
DHEAD = 64
WIN = 256
NEXP = 8
FF = 2048
FC = FF // P
I2 = 4096
HID = 3072
HC = HID // P

ENGINES = ("sync", "gpsimd", "tensor", "vector", "scalar")


class CB:

    def __init__(self, nc, name):
        self.h = nc.alloc_semaphore(name)
        self.n = 0

    def inc(self, k=1):
        self.n += k
        return self.n

    @property
    def now(self):
        return self.n


class Prog:

    def __init__(self, nc):
        self.nc = nc
        self.steps = {e: [] for e in ENGINES}

    def sem(self, name):
        return CB(self.nc, name)

    def emit(self, eng, fn, wait=(), inc=(), drain=False):
        incs = []
        for cb, k in inc:
            cb.inc(k)
            incs.append((cb, k))
        self.steps[eng].append((fn, tuple(wait), tuple(incs), drain))

    def finalize(self):
        nc = self.nc
        with nc.Block() as block:
            for name in ENGINES:
                steps = self.steps[name]
                section = getattr(block, name)

                def body(eng, steps=steps):
                    for fn, waits, incs, drain in steps:
                        for cb, v in waits:
                            eng.wait_ge(cb.h, v)
                        if drain:
                            eng.drain()
                        inst = fn(eng)
                        if incs:
                            assert inst is not None, "inc on step with no instruction"
                            for cb, k in incs:
                                inst.then_inc(cb.h, k)

                section(body)


class Ctx:

    def __init__(self, nc):
        self.nc = nc
        self.pg = Prog(nc)
        self.res = {}
        self.arena = None
        self.arena_off = 0
        self.arena_marks = []

    def arena_alloc(self, name, shape, dtype):
        nc = self.nc
        per_part = int(np.prod(shape[1:])) * mybir.dt.size(dtype)
        per_part = (per_part + 31) & ~31
        assert self.arena_off + per_part <= self.arena_bytes, (
            f"arena overflow: {name} needs {per_part} at {self.arena_off}, cap {self.arena_bytes}")
        h = nc.virtual_sbuf_tensor(name, list(shape), dtype)
        nc._tensor(h.name, list(h.shape), h.dtype, type="SB")
        mloc = nc.lookup_mloc(h)
        mloc.allocated = True
        mloc.addr = self.arena_addr + self.arena_off
        self.arena_off += per_part
        return h

    def arena_alloc_at(self, name, shape, dtype, offset):
        nc = self.nc
        h = nc.virtual_sbuf_tensor(name, list(shape), dtype)
        nc._tensor(h.name, list(h.shape), h.dtype, type="SB")
        mloc = nc.lookup_mloc(h)
        mloc.allocated = True
        mloc.addr = self.arena_addr + offset
        return h

    def arena_reset(self):
        self.arena_off = 0
        self.arena_marks = []

    def arena_mark(self):
        self.arena_marks.append(self.arena_off)

    def arena_release(self):
        self.arena_off = self.arena_marks.pop()


def dt_np(dtype):
    return {F32: np.float32, BF16: ml_dtypes.bfloat16}[dtype]



VEC_NAMES = [
    "L0_attn_bq", "L0_attn_bk", "L0_attn_bv", "L0_attn_bd",
    "L1_attn_ln1_w", "L1_attn_ln2_w", "L1_attn_time_decay", "L1_attn_time_first",
    "L1_attn_time_mix_k", "L1_attn_time_mix_v", "L1_attn_time_mix_r",
    "L1_attn_time_mix_k2", "L1_attn_time_mix_r2",
]
VI = {n.split("_", 2)[-1]: i for i, n in enumerate(VEC_NAMES)}
DV = {"bq8": 0, "ew": 1}
ND = 2
CVAL = {"eps_ln": 0, "eps_rms": 1, "inv127": 2, "eps_wkv": 3}
CVAL_VALUES = {"eps_ln": 1e-5, "eps_rms": 1e-6, "inv127": 1.0 / 127.0, "eps_wkv": 1e-8}

ALL_PHASES = ("A1", "A2", "A3", "A4", "A5", "B1", "B2", "B3", "B4")


def build_program(phases=ALL_PHASES, dumps=(), dbg_loads=()):
    nc = bass.Bass("TRN2", target_bir_lowering=False, debug=False)
    c = Ctx(nc)
    pg = c.pg

    inp = {}
    inp["x"] = nc.dram_tensor("x", [T, C], F32, kind="ExternalInput")
    for nm, shape, dt in [
        ("L0_attn_Wq", [C, C], F32), ("L0_attn_Wk", [C, C], F32),
        ("L0_attn_Wv", [C, C], F32), ("L0_attn_Wd", [C, C], F32),
        ("L0_attn_bq", [C], F32), ("L0_attn_bk", [C], F32),
        ("L0_attn_bv", [C], F32), ("L0_attn_bd", [C], F32),
        ("L0_ffn_Wgate", [C, NEXP], F32), ("L0_ffn_bgate", [NEXP], F32),
        ("L0_ffn_W1", [NEXP, C, FF], F32), ("L0_ffn_b1", [NEXP, FF], F32),
        ("L0_ffn_W2", [NEXP, C, FF], F32), ("L0_ffn_b2", [NEXP, FF], F32),
        ("L0_ffn_Wg", [NEXP, FF, FF], F32), ("L0_ffn_bg", [NEXP, FF], F32),
        ("L0_ffn_Wl", [NEXP, FF, FF], F32), ("L0_ffn_bl", [NEXP, FF], F32),
        ("L0_ffn_W3", [NEXP, FF, C], F32), ("L0_ffn_b3", [NEXP, C], F32),
        ("L1_attn_ln1_w", [C], F32), ("L1_attn_ln2_w", [C], F32),
        ("L1_attn_time_decay", [C], F32), ("L1_attn_time_first", [C], F32),
        ("L1_attn_time_mix_k", [C], F32), ("L1_attn_time_mix_v", [C], F32),
        ("L1_attn_time_mix_r", [C], F32), ("L1_attn_W_key", [C, C], F32),
        ("L1_attn_W_value", [C, C], F32), ("L1_attn_W_recept", [C, C], F32),
        ("L1_attn_W_out", [C, C], F32), ("L1_attn_time_mix_k2", [C], F32),
        ("L1_attn_time_mix_r2", [C], F32), ("L1_attn_W_key2", [C, I2], F32),
        ("L1_attn_W_value2", [I2, C], F32), ("L1_attn_W_recept2", [C, C], F32),
        ("L1_ffn_w_gate", [HID, C], I32), ("L1_ffn_w_up", [HID, C], I32),
        ("L1_ffn_w_down", [C, HID], I32),
    ]:
        inp[nm] = nc.dram_tensor(nm, shape, dt, kind="ExternalInput")
    c.inp = inp

    y_out = nc.dram_tensor("y", [T, C], F32, kind="ExternalOutput")
    c.res["y"] = y_out

    c.res["F_hbm"] = nc.dram_tensor("F_hbm", [CC, P, T], F32, kind="Internal")
    c.res["x1_hbm"] = nc.dram_tensor("x1_hbm", [CC, P, T], F32, kind="Internal")
    c.res["xa_hbm"] = nc.dram_tensor("xa_hbm", [CC, P, T], F32, kind="Internal")
    c.res["wgT_hbm"] = nc.dram_tensor("wgT_hbm", [C, HID], BF16, kind="Internal")
    c.res["w1b"] = nc.dram_tensor("w1b_hbm", [NEXP, P, FC, 1024], BF16, kind="Internal")
    c.res["w2b"] = nc.dram_tensor("w2b_hbm", [NEXP, P, FC, 1024], BF16, kind="Internal")
    c.res["wgb"] = nc.dram_tensor("wgb_hbm", [NEXP, P, FC, FF], BF16, kind="Internal")
    c.res["wlb"] = nc.dram_tensor("wlb_hbm", [NEXP, P, FC, FF], BF16, kind="Internal")
    c.res["w3b"] = nc.dram_tensor("w3b_hbm", [NEXP, P, FC, 1024], BF16, kind="Internal")
    c.res["wuT_hbm"] = nc.dram_tensor("wuT_hbm", [C, HID], BF16, kind="Internal")
    c.res["wdT_hbm"] = nc.dram_tensor("wdT_hbm", [HID, C], BF16, kind="Internal")

    ident_bf_h = nc.inline_tensor(np.eye(P, dtype=ml_dtypes.bfloat16), name="ident_bf")
    ident_f_h = nc.inline_tensor(np.eye(P, dtype=np.float32), name="ident_f")
    ones_row_f_h = nc.inline_tensor(np.ones((1, P), np.float32), name="ones_row_f")
    ones_col_bf_h = nc.inline_tensor(np.ones((P, 1), ml_dtypes.bfloat16), name="ones_col_bf")
    ones_col_f_h = nc.inline_tensor(np.ones((P, 1), np.float32), name="ones_col_f")
    sel8_np = np.zeros((NEXP, NEXP * P), np.float32)
    for e in range(NEXP):
        sel8_np[e, e * P:(e + 1) * P] = 1.0
    sel8_h = nc.inline_tensor(sel8_np, name="sel8")

    sb = nc.alloc_sbuf_tensor
    c.res["ident_bf"] = sb("ident_bf_s", [P, P], BF16)
    c.res["ident_f"] = sb("ident_f_s", [P, P], F32)
    c.res["ones_row_f"] = sb("ones_row_f_s", [1, P], F32)
    c.res["ones_col_bf"] = sb("ones_col_bf_s", [P, 1], BF16)
    c.res["ones_col_f"] = sb("ones_col_f_s", [P, 1], F32)
    c.res["sel8"] = sb("sel8_s", [NEXP, NEXP * P], F32)
    c.res["vecs"] = sb("vecs_s", [P, len(VEC_NAMES), CC], F32)
    c.res["dvec"] = sb("dvec_s", [P, ND, CC], F32)
    c.res["b1_fm"] = sb("b1_fm", [P, FC, NEXP], F32)
    c.res["b2_fm"] = sb("b2_fm", [P, FC, NEXP], F32)
    c.res["bg_fm"] = sb("bg_fm", [P, FC, NEXP], F32)
    c.res["bl_fm"] = sb("bl_fm", [P, FC, NEXP], F32)
    c.res["b3_f"] = sb("b3_f", [NEXP, C], F32)
    c.res["wgate"] = sb("wgate_s", [P, CC, NEXP], BF16)
    c.res["bgate"] = sb("bgate_s", [NEXP, 1], F32)
    c.res["cb_pf"] = sb("cb_pf", [NEXP, T], F32)
    c.res["xn_bf"] = sb("xn_bf", [P, CC, T], BF16)
    c.res["cval"] = sb("cval_s", [P, 8], F32)
    c.res["wgate_lo"] = sb("wgate_lo", [P, CC, NEXP], BF16)
    c.res["wgate_f"] = sb("wgate_f", [P, CC, NEXP], F32)

    rem = nc.sbuf_bytes_remaining - 512
    c.arena = sb("arena", [P, rem], mybir.dt.uint8)
    c.arena_addr = nc.lookup_mloc(c.arena).addr
    c.arena_bytes = rem

    c.pb01 = nc.alloc_psum_tensor("pb01", [P, 1024], F32)
    c.pb23 = nc.alloc_psum_tensor("pb23", [P, 1024], F32)
    c.pb2 = c.pb23[:, 0:512]
    c.pb3 = c.pb23[:, 512:1024]
    c.pbu = nc.alloc_psum_tensor("pbu", [P, 512], F32)
    c.pbs = nc.alloc_psum_tensor("pbs", [P, 512], F32)
    c.psum_tp = nc.alloc_psum_tensor("pb_tp", [P, 5, 128], BF16)
    c.psum_tp2 = nc.alloc_psum_tensor("pb_tp2", [P, 5, 128], BF16)

    c.consts_dram = dict(ident_bf=ident_bf_h, ident_f=ident_f_h,
                         ones_row_f=ones_row_f_h, ones_col_bf=ones_col_bf_h,
                         ones_col_f=ones_col_f_h, sel8=sel8_h)

    emit_setup(c)
    ph_fns = {"A1": emit_A1, "A2": emit_A2, "A3": emit_A3, "A4": emit_A4,
              "A5": emit_A5, "B1": emit_B1, "B2": emit_B2, "B3": emit_B3,
              "B4": emit_B4}
    c.last_done = list(c.setup_done_waits)
    for name in dbg_loads:
        emit_dbg_load(c, name)
    for ph in phases:
        ph_fns[ph](c)
    for name in dumps:
        emit_dump(c, name)
    emit_tail(c)

    pg.finalize()
    return nc, c



DBG_SPECS = {
    "xn_bf": ("xn_bf", [P, CC, T], BF16),
    "cb_pf": ("cb_pf", [NEXP, T], F32),
}
DBG_HBM = {"F_hbm": [CC, P, T], "x1_hbm": [CC, P, T], "xa_hbm": [CC, P, T]}


def emit_dbg_load(c, name):
    nc, pg = c.nc, c.pg
    if name in DBG_SPECS:
        key, shape, dt = DBG_SPECS[name]
        h = nc.dram_tensor("dbg_" + name, shape, dt, kind="ExternalInput")
        dst = c.res[key]
        s = pg.sem("dbgl_" + name)
        pg.emit("sync", lambda e, dst=dst, h=h: e.dma_start(
            out=dst[tuple(slice(None) for _ in dst.shape)], in_=h.ap()), inc=[(s, 16)])
        c.setup_done_waits.append((s, s.now))
        c.last_done.append((s, s.now))
    elif name in DBG_HBM:
        h = nc.dram_tensor("dbg_" + name, DBG_HBM[name], F32, kind="ExternalInput")
        dst = c.res[name]
        s = pg.sem("dbgl_" + name)
        pg.emit("sync", lambda e, dst=dst, h=h: e.dma_start(
            out=dst.ap(), in_=h.ap()), inc=[(s, 16)])
        c.setup_done_waits.append((s, s.now))
        c.last_done.append((s, s.now))
    else:
        raise KeyError(name)


def emit_dump(c, name):
    nc, pg = c.nc, c.pg
    w = list(c.last_done)
    if name not in DBG_SPECS and name not in DBG_HBM and name in c.res:
        src_t = c.res[name]
        shape = list(src_t.shape)
        h = nc.dram_tensor("dump_" + name, shape, src_t.dtype, kind="ExternalOutput")
        s = pg.sem("dmp_" + name)
        pg.emit("sync", lambda e, src_t=src_t, h=h: e.dma_start(
            out=h.ap(), in_=src_t[tuple(slice(None) for _ in src_t.shape)]),
            wait=w, inc=[(s, 16)])
        c.final_waits.append((s, s.now))
        return
    if name in DBG_SPECS:
        key, shape, dt = DBG_SPECS[name]
        h = nc.dram_tensor("dump_" + name, shape, dt, kind="ExternalOutput")
        src = c.res[key]
        s = pg.sem("dmp_" + name)
        pg.emit("sync", lambda e, src=src, h=h: e.dma_start(
            out=h.ap(), in_=src[tuple(slice(None) for _ in src.shape)]),
            wait=w, inc=[(s, 16)])
        c.final_waits.append((s, s.now))
    elif name in DBG_HBM:
        h = nc.dram_tensor("dump_" + name, DBG_HBM[name], F32, kind="ExternalOutput")
        src = c.res[name]
        s = pg.sem("dmp_" + name)
        pg.emit("sync", lambda e, src=src, h=h: e.dma_start(out=h.ap(), in_=src.ap()),
                wait=w, inc=[(s, 16)])
        c.final_waits.append((s, s.now))
    else:
        raise KeyError(name)


def emit_tail(c):
    pg = c.pg
    def tail(e):
        return None
    waits = list(c.final_waits)
    pg.emit("sync", lambda e: e.nop(), wait=waits)



def emit_setup(c):
    nc, pg = c.nc, c.pg
    ld = pg.sem("s_ld")
    c.final_waits = []
    c.last_done = []

    for key in ("ident_bf", "ident_f", "ones_row_f", "ones_col_bf", "ones_col_f", "sel8"):
        h = c.consts_dram[key]
        dst = c.res[key]
        pg.emit("sync", lambda e, dst=dst, h=h: e.dma_start(
            out=dst[tuple(slice(None) for _ in dst.shape)], in_=h.ap()), inc=[(ld, 16)])

    c.arena_mark()
    stage = c.arena_alloc("stage_vec", [NEXP, len(VEC_NAMES), P], F32)
    for j, nm in enumerate(VEC_NAMES):
        src = c.inp[nm].ap().rearrange("(a b) -> a b", b=P)
        pg.emit("sync", lambda e, j=j, src=src: e.dma_start(
            out=stage[:, j, :], in_=src), inc=[(ld, 16)])

    stage_b = c.arena_alloc("stage_b", [NEXP, 4, FF], F32)
    for j, nm in enumerate(("L0_ffn_b1", "L0_ffn_b2", "L0_ffn_bg", "L0_ffn_bl")):
        pg.emit("sync", lambda e, j=j, nm=nm: e.dma_start(
            out=stage_b[:, j, :], in_=c.inp[nm].ap()), inc=[(ld, 16)])
    pg.emit("sync", lambda e: e.dma_start(out=c.res["b3_f"][:, :],
                                          in_=c.inp["L0_ffn_b3"].ap()), inc=[(ld, 16)])
    pg.emit("sync", lambda e: e.dma_start(out=c.res["bgate"][:, 0],
                                          in_=c.inp["L0_ffn_bgate"].ap()), inc=[(ld, 16)])
    pg.emit("gpsimd", lambda e: e.dma_start(
        out=c.res["wgate"][:, :, :],
        in_=c.inp["L0_ffn_Wgate"].ap().rearrange("(kc p) f -> p kc f", p=P)),
        inc=[(ld, 16)])
    pg.emit("sync", lambda e: e.dma_start(
        out=c.res["wgate_f"][:, :, :],
        in_=c.inp["L0_ffn_Wgate"].ap().rearrange("(kc p) f -> p kc f", p=P)),
        inc=[(ld, 16)])
    n_ld = ld.now

    cvt = pg.sem("s_cval")
    cval = c.res["cval"]
    for k, j in CVAL.items():
        pg.emit("gpsimd", lambda e, j=j, v=CVAL_VALUES[k]: e.memset(cval[:, j:j + 1], v),
                drain=True, inc=[(cvt, 1)])

    prep = pg.sem("s_prep")
    c.prep_ready = []
    for ex in range(NEXP):
        for wname, dstk, style in (("L0_ffn_W1", "w1b", "fh"), ("L0_ffn_W2", "w2b", "fh"),
                                   ("L0_ffn_Wg", "wgb", "kf"), ("L0_ffn_Wl", "wlb", "kf"),
                                   ("L0_ffn_W3", "w3b", "kf")):
            if style == "fh":
                src_ap = c.inp[wname].ap()[ex].rearrange(
                    "(kc p) (fh g) -> p kc fh g", p=P, g=1024)
                dst_ap = c.res[dstk].ap()[ex].rearrange(
                    "p (kc fh) g -> p kc fh g", kc=CC)
            else:
                src_ap = c.inp[wname].ap()[ex].rearrange("(kc p) f -> p kc f", p=P)
                dst_ap = c.res[dstk].ap()[ex]
            pg.emit("gpsimd", lambda e, src_ap=src_ap, dst_ap=dst_ap: e.dma_start(
                out=dst_ap, in_=src_ap),
                inc=[(prep, 16)])
        c.prep_ready.append((prep, prep.now))
    c.final_waits.append((prep, prep.now))

    tp = pg.sem("s_tp")
    ev = pg.sem("s_ev")
    vecs = c.res["vecs"]
    idf = c.res["ident_f"]
    pu = c.pbu
    def tpev(src_ap_fn, dst_ap_fn, first_wait):
        j = tp.now
        pg.emit("tensor", lambda e, s=src_ap_fn: e.transpose(
            pu[:, 0:8], s(), idf[0:NEXP, 0:NEXP]),
            wait=first_wait + [(ev, j)], inc=[(tp, 1)])
        pg.emit("vector", lambda e, d=dst_ap_fn: e.tensor_copy(d(), pu[:, 0:8]),
            wait=[(tp, tp.now)], inc=[(ev, 1)], drain=True)

    for j in range(len(VEC_NAMES)):
        tpev(lambda j=j: stage[:, j, :], lambda j=j: vecs[:, j, :],
             [(ld, n_ld)] if j == 0 else [])
    for j, key in enumerate(("b1_fm", "b2_fm", "bg_fm", "bl_fm")):
        dstt = c.res[key]
        for fc in range(FC):
            tpev(lambda j=j, fc=fc: stage_b[:, j, fc * P:(fc + 1) * P],
                 lambda dstt=dstt, fc=fc: dstt[:, fc, :], [])

    dvec = c.res["dvec"]
    dv = pg.sem("s_dv")
    pg.emit("vector", lambda e: e.tensor_scalar_mul(
        dvec[:, DV["bq8"], :], vecs[:, VI["bq"], :], 0.125),
        wait=[(ev, ev.now)], drain=True, inc=[(dv, 1)])
    pg.emit("scalar", lambda e: e.activation(
        dvec[:, DV["ew"], :], vecs[:, VI["time_decay"], :], AF.Exp),
        wait=[(ev, ev.now)], inc=[(dv, 1)])
    pg.emit("vector", lambda e: e.tensor_scalar(
        dvec[:, DV["ew"], :], dvec[:, DV["ew"], :], -1.0, 20.0, ALU.mult, ALU.min),
        wait=[(dv, dv.now)], drain=True, inc=[(dv, 1)])
    pg.emit("vector", lambda e: e.tensor_scalar_max(
        dvec[:, DV["ew"], :], dvec[:, DV["ew"], :], -20.0), drain=True, inc=[(dv, 1)])
    pg.emit("scalar", lambda e: e.activation(
        dvec[:, DV["ew"], :], dvec[:, DV["ew"], :], AF.Exp),
        wait=[(dv, dv.now)], drain=True, inc=[(dv, 1)])

    pg.emit("vector", lambda e: e.tensor_tensor(
        c.res["wgate_lo"][:, :, :], c.res["wgate_f"][:, :, :],
        c.res["wgate"][:, :, :], ALU.subtract),
        wait=[(ld, n_ld)], drain=True, inc=[(dv, 1)])

    c.arena_release()
    c.setup_done_waits = [(ld, n_ld), (ev, ev.now), (dv, dv.now),
                          (cvt, cvt.now)]



def emit_A1(c):
    nc, pg = c.nc, c.pg
    c.res["xn_lo"] = c.arena_alloc("xn_lo", [P, CC, T], BF16)
    c.arena_mark()
    x_ring = [c.arena_alloc(f"a1_x{i}", [P, C], F32) for i in range(2)]
    sq = c.arena_alloc("a1_sq", [P, C], F32)
    st_ring = [c.arena_alloc(f"a1_st{i}", [P, 8], F32) for i in range(2)]
    xn_tm = [c.arena_alloc(f"a1_xn{i}", [P, C], BF16) for i in range(2)]
    xn_lo_tm = [c.arena_alloc(f"a1_xnl{i}", [P, C], BF16) for i in range(2)]
    xnf = [c.arena_alloc(f"a1_xnf{i}", [P, C], F32) for i in range(2)]
    xn_lo = c.res["xn_lo"]

    ldx = pg.sem("a1_ldx")
    stat = pg.sem("a1_stat")
    xnr = pg.sem("a1_xnr")
    tpd = pg.sem("a1_tpd")
    evd = pg.sem("a1_evd")
    setup_w = c.setup_done_waits

    x_h = c.inp["x"].ap()
    xn_bf = c.res["xn_bf"]
    idb = c.res["ident_bf"]

    for i in range(16):
        ring = x_ring[i % 2]
        xnt = xn_tm[i % 2]
        st = st_ring[i % 2]
        w = list(setup_w) if i < 2 else []
        if i >= 2:
            w.append((xnr, i - 1))
        pg.emit("sync", lambda e, ring=ring, i=i: e.dma_start(
            out=ring[:, :], in_=x_h[i * P:(i + 1) * P, :]), wait=w, inc=[(ldx, 16)])
        wld = [(ldx, 16 * (i + 1))]
        if i >= 2:
            wld = wld + [(xnr, i - 1)]
        pg.emit("scalar", lambda e, ring=ring, st=st: e.activation(
            sq[:, :], ring[:, :], AF.Square, accum_out=st[:, 1:2]),
            wait=wld, drain=True, inc=[(stat, 1)])
        pg.emit("vector", lambda e, ring=ring, st=st: e.reduce_sum(
            st[:, 0:1], ring[:, :], axis=AX.X), wait=wld, drain=True, inc=[(stat, 1)])
        pg.emit("vector", lambda e, st=st: e.tensor_scalar_mul(st[:, 2:3], st[:, 0:1], 1.0 / C),
                wait=[(stat, stat.now)], drain=True)
        pg.emit("vector", lambda e, st=st: e.tensor_scalar_mul(st[:, 3:4], st[:, 2:3], -1.0),
                drain=True)
        pg.emit("vector", lambda e, st=st: e.tensor_tensor(st[:, 4:5], st[:, 2:3], st[:, 2:3],
                                                    ALU.mult), drain=True)
        pg.emit("vector", lambda e, st=st: e.scalar_tensor_tensor(
            st[:, 5:6], st[:, 1:2], 1.0 / C, st[:, 4:5], ALU.mult, ALU.subtract),
            drain=True, inc=[(stat, 1)])
        pg.emit("scalar", lambda e, st=st: e.activation(st[:, 6:7], st[:, 5:6], AF.Sqrt,
                                                 bias=c.res["cval"][:, CVAL["eps_ln"]:CVAL["eps_ln"] + 1]),
                wait=[(stat, stat.now)], drain=True, inc=[(stat, 1)])
        pg.emit("vector", lambda e, st=st: e.reciprocal(st[:, 7:8], st[:, 6:7]),
                wait=[(stat, stat.now)], drain=True, inc=[(stat, 1)])
        xf = xnf[i % 2]
        xlt = xn_lo_tm[i % 2]
        pg.emit("vector", lambda e, ring=ring, xf=xf, st=st: e.tensor_scalar(
            xf[:, :], ring[:, :], st[:, 3:4], st[:, 7:8], ALU.add, ALU.mult),
            wait=[(stat, stat.now)] + ([(evd, 16 * (i - 1))] if i >= 2 else []),
            drain=True)
        pg.emit("vector", lambda e, xf=xf, xnt=xnt: e.tensor_copy(xnt[:, :], xf[:, :]),
                drain=True)
        pg.emit("vector", lambda e, xf=xf, xnt=xnt, xlt=xlt: e.tensor_tensor(
            xlt[:, :], xf[:, :], xnt[:, :], ALU.subtract),
            drain=True, inc=[(xnr, 1)])
        for cc in range(CC):
            for half, (srcm, dstm) in enumerate(((xnt, xn_bf), (xlt, xn_lo))):
                g = i * 16 + cc * 2 + half
                ps = c.psum_tp if g % 2 == 0 else c.psum_tp2
                w2 = [(xnr, xnr.now)]
                if g >= 2:
                    w2.append((evd, g - 1))
                pg.emit("tensor", lambda e, ps=ps, srcm=srcm, cc=cc: e.transpose(
                    ps[:, 0, :], srcm[:, cc * P:(cc + 1) * P], idb[:, :]),
                    wait=w2, inc=[(tpd, 1)])
                pg.emit("vector", lambda e, ps=ps, dstm=dstm, cc=cc, i=i: e.tensor_copy(
                    dstm[:, cc, i * P:(i + 1) * P], ps[:, 0, :]),
                    wait=[(tpd, tpd.now)], drain=True, inc=[(evd, 1)])

    c.arena_release()
    c.last_done = [(evd, evd.now)]



def emit_A2(c):
    nc, pg = c.nc, c.pg
    c.arena_mark()
    gf = c.arena_alloc("a2_gf", [NEXP, 512], F32)
    gtm = c.arena_alloc("a2_gtm", [P, NEXP], F32)
    st = c.arena_alloc("a2_st", [P, 8], F32)
    eq1 = c.arena_alloc("a2_eq1", [P, NEXP], F32)
    eq2 = c.arena_alloc("a2_eq2", [P, NEXP], F32)
    gmask = c.arena_alloc("a2_gm", [P, NEXP], F32)
    t1 = c.arena_alloc("a2_t1", [P, NEXP], F32)
    cbtm = c.arena_alloc("a2_cb", [P, NEXP], F32)

    mm = pg.sem("a2_mm")
    ev = pg.sem("a2_ev")
    dn = pg.sem("a2_dn")

    xn_bf = c.res["xn_bf"]
    wg = c.res["wgate"]
    bg = c.res["bgate"]
    cb_pf = c.res["cb_pf"]
    pbs = c.pbs
    pbu = c.pbu
    idf = c.res["ident_f"]
    prev = list(c.last_done)

    for tt in range(4):
        def gmm(e, tt=tt):
            r = None
            xl = c.res["xn_lo"]
            wgl = c.res["wgate_lo"]
            terms = [(wg, xn_bf), (wg, xl), (wgl, xn_bf)]
            n = len(terms) * CC
            i = 0
            for wgt, act in terms:
                for kc in range(CC):
                    r = e.matmul(pbs[0:NEXP, :], wgt[:, kc, :],
                                 act[:, kc, tt * 512:(tt + 1) * 512],
                                 start=(i == 0), stop=(i == n - 1))
                    i += 1
            return r
        pg.emit("tensor", gmm, wait=prev + [(ev, ev.now)], inc=[(mm, 1)])
        pg.emit("vector", lambda e: e.tensor_scalar_add(gf[:, :], pbs[0:NEXP, :],
                                                        bg[:, 0:1]),
                wait=[(mm, mm.now)], drain=True, inc=[(ev, 1)])
        for q in range(4):
            t0 = tt * 512 + q * P
            pg.emit("tensor", lambda e, q=q: e.transpose(
                pbu[:, 0:NEXP], gf[:, q * P:(q + 1) * P], idf[0:NEXP, 0:NEXP]),
                wait=[(ev, ev.now)], inc=[(mm, 1)])
            pg.emit("vector", lambda e: e.tensor_copy(gtm[:, :], pbu[:, 0:NEXP]),
                    wait=[(mm, mm.now)], drain=True)
            pg.emit("vector", lambda e: e.reduce_max(st[:, 0:1], gtm[:, :], axis=AX.X),
                    drain=True)
            pg.emit("vector", lambda e: e.tensor_single_scalar(
                eq1[:, :], gtm[:, :], st[:, 0:1], ALU.is_equal), drain=True)
            pg.emit("vector", lambda e: e.scalar_tensor_tensor(
                gmask[:, :], eq1[:, :], -1e30, gtm[:, :], ALU.mult, ALU.add), drain=True)
            pg.emit("vector", lambda e: e.reduce_max(st[:, 1:2], gmask[:, :], axis=AX.X),
                    drain=True)
            pg.emit("vector", lambda e: e.tensor_single_scalar(
                eq2[:, :], gmask[:, :], st[:, 1:2], ALU.is_equal), drain=True)
            pg.emit("vector", lambda e: e.tensor_scalar_mul(st[:, 2:3], st[:, 0:1], -1.0),
                    drain=True)
            pg.emit("vector", lambda e: e.tensor_scalar_mul(st[:, 3:4], st[:, 1:2], -1.0),
                    drain=True, inc=[(dn, 1)])
            pg.emit("scalar", lambda e: e.activation(st[:, 4:5], st[:, 0:1], AF.Sigmoid,
                                                     bias=st[:, 3:4]),
                    wait=[(dn, dn.now)], drain=True)
            pg.emit("scalar", lambda e: e.activation(st[:, 5:6], st[:, 1:2], AF.Sigmoid,
                                                     bias=st[:, 2:3]),
                    drain=True, inc=[(dn, 1)])
            pg.emit("vector", lambda e: e.tensor_scalar_mul(t1[:, :], eq1[:, :], st[:, 4:5]),
                    wait=[(dn, dn.now)], drain=True)
            pg.emit("vector", lambda e: e.scalar_tensor_tensor(
                cbtm[:, :], eq2[:, :], st[:, 5:6], t1[:, :], ALU.mult, ALU.add),
                drain=True, inc=[(dn, 1)])
            pg.emit("tensor", lambda e: e.transpose(
                pbu[0:NEXP, 0:P], cbtm[:, :], idf[:, :]),
                wait=[(dn, dn.now)], inc=[(mm, 1)])
            pg.emit("vector", lambda e, t0=t0: e.tensor_copy(
                cb_pf[0:NEXP, t0:t0 + P], pbu[0:NEXP, 0:P]), wait=[(mm, mm.now)],
                drain=True, inc=[(ev, 1)])

    c.arena_release()
    c.last_done = [(ev, ev.now), (dn, dn.now)]


class Banks:

    def __init__(self, slots):
        self.slots = slots
        self.free = [None] * len(slots)
        self.i = 0

    def acquire(self):
        j = self.i % len(self.slots)
        self.i += 1
        ap = self.slots[j]
        w = self.free[j]
        return j, ap, ([w] if w else [])

    def release(self, j, cb):
        self.free[j] = (cb, cb.now)



def emit_A3(c):
    nc, pg = c.nc, c.pg
    c.arena_reset()
    z = c.arena_alloc("a3_z", [P, FC, 512], BF16)
    o = c.arena_alloc("a3_o", [P, FC, 512], BF16)
    slots_t = [c.arena_alloc(f"a3_w{i}", [P, FC, 1024], BF16) for i in range(3)]
    sg = c.arena_alloc("a3_sg", [P, NEXP, 512], BF16)
    cbE = c.arena_alloc("a3_cbE", [P, 512], F32)
    x1s = c.arena_alloc("a3_x1s", [P, 512], F32)
    ys = c.arena_alloc("a3_ys", [P, 512], F32)
    ls = c.arena_alloc("a3_ls", [P, 512], F32)

    for nm, t in (("a3_z", z), ("a3_o", o), ("a3_sg", sg), ("a3_cbE", cbE)):
        c.res[nm] = t
    wdma = pg.sem("a3_wdma")
    fdma = pg.sem("a3_fdma")
    mm = pg.sem("a3_mm")
    evv = pg.sem("a3_evv")
    evs = pg.sem("a3_evs")

    xn_bf = c.res["xn_bf"]
    cb_pf = c.res["cb_pf"]
    sel8 = c.res["sel8"]
    b3_f = c.res["b3_f"]
    F_hbm = c.res["F_hbm"].ap()
    b1_fm, b2_fm = c.res["b1_fm"], c.res["b2_fm"]
    bg_fm, bl_fm = c.res["bg_fm"], c.res["bl_fm"]

    banks = Banks([c.pb01[:, 0:512], c.pb01[:, 512:1024], c.pb2, c.pb3])
    ubank_free = [None]
    prev = list(c.last_done)

    def v_w12(i):
        return slots_t[i][:, :, :].rearrange("p (kc fh) g -> p kc (fh g)", kc=CC)

    def v_half(i):
        return slots_t[i][:, :, :]

    for tt in range(4):
        for ct in range(CC):
            t0 = tt * 512
            w0 = list(prev) if (tt == 0 and ct == 0) else []
            pg.emit("tensor",
                    lambda e, ct=ct, t0=t0: e.matmul(
                        c.pbu[:, :], b3_f[0:NEXP, ct * P:(ct + 1) * P],
                        cb_pf[0:NEXP, t0:t0 + 512], start=True, stop=True),
                    wait=w0 + ([ubank_free[0]] if ubank_free[0] else []), inc=[(mm, 1)])
            pg.emit("vector", lambda e: e.tensor_copy(ys[:, :], c.pbu[:, :]),
                    wait=[(mm, mm.now), (fdma, fdma.now)], drain=True, inc=[(evv, 1)])
            ubank_free[0] = (evv, evv.now)
            pg.emit("gpsimd", lambda e, ct=ct, t0=t0: e.dma_start(
                out=F_hbm[ct, :, t0:t0 + 512], in_=ys[:, :]),
                wait=[(evv, evv.now)], inc=[(fdma, 16)])

    slot_free = [None, None, None]
    slot_i = [0]

    def load(src_ap, nkc, extra_wait=()):
        i = slot_i[0] % 3
        slot_i[0] += 1
        w = list(extra_wait)
        if slot_free[i] is not None:
            w.append(slot_free[i])
        pg.emit("gpsimd", lambda e, i=i, src_ap=src_ap: e.dma_start(
            out=slots_t[i][:, :, :], in_=src_ap), wait=w, inc=[(wdma, 16)])
        return i, wdma.now

    import os
    n_tt2 = int(os.environ.get("A3_TT2", "2"))
    n_ex = int(os.environ.get("A3_EX", str(NEXP)))
    n_s = int(os.environ.get("A3_S", "2"))
    iters = [(tt2, ex, s) for tt2 in range(n_tt2) for ex in range(n_ex)
             for s in range(n_s)]
    it_state = {"next12": None}
    for it_idx, (tt2, ex, s) in enumerate(iters):
        tb = tt2 * 1024
        if True:
            if True:
                t0 = tb + s * 512
                pw = [c.prep_ready[ex]] if tt2 == 0 else []
                stage = os.environ.get("A3_STAGE", "y")
                if it_state["next12"] is not None:
                    l_w1, l_w2 = it_state["next12"]
                    it_state["next12"] = None
                else:
                    l_w1 = load(c.res["w1b"].ap()[ex], CC, pw)
                    l_w2 = load(c.res["w2b"].ap()[ex], CC, pw)
                pg.emit("tensor",
                        lambda e, ex=ex, t0=t0: e.matmul(
                            c.pbu[:, :], sel8[0:NEXP, ex * P:(ex + 1) * P],
                            cb_pf[0:NEXP, t0:t0 + 512], start=True, stop=True),
                        wait=([ubank_free[0]] if ubank_free[0] else []), inc=[(mm, 1)])
                pg.emit("vector", lambda e: e.tensor_copy(cbE[:, :], c.pbu[:, :]),
                        wait=[(mm, mm.now)], drain=True, inc=[(evv, 1)])
                ubank_free[0] = (evv, evv.now)

                if stage == "cb":
                    continue
                if stage == "loads":
                    slot_free[l_w1[0]] = (wdma, l_w1[1])
                    slot_free[l_w2[0]] = (wdma, l_w2[1])
                    continue
                w1v, w2v = v_w12(l_w1[0]), v_w12(l_w2[0])
                for fc in range(FC):
                    j1, pA, wA = banks.acquire()
                    j2, pB, wB = banks.acquire()
                    pg.emit("tensor",
                            lambda e, fc=fc, pA=pA, w1v=w1v, t0=t0: _kmm(
                                e, pA, w1v, xn_bf, fc, CC, t0),
                            wait=[(wdma, l_w1[1])] + wA + wB, inc=[(mm, 1)])
                    pg.emit("tensor",
                            lambda e, fc=fc, pB=pB, w2v=w2v, t0=t0: _kmm(
                                e, pB, w2v, xn_bf, fc, CC, t0),
                            wait=[(wdma, l_w2[1])], inc=[(mm, 1)])
                    mmv = mm.now
                    pg.emit("vector", lambda e, fc=fc, ex=ex, pA=pA: e.tensor_scalar_add(
                        x1s[:, :], pA, b1_fm[:, fc, ex:ex + 1]),
                        wait=[(mm, mmv)], drain=True, inc=[(evv, 1)])
                    banks.release(j1, evv)
                    pg.emit("vector",
                            lambda e, fc=fc, ex=ex, pB=pB: e.scalar_tensor_tensor(
                                z[:, fc, :], pB, b2_fm[:, fc, ex:ex + 1],
                                x1s[:, :], ALU.add, ALU.mult),
                            drain=True, inc=[(evv, 1)])
                    banks.release(j2, evv)
                z_done = evv.now
                slot_free[l_w1[0]] = (mm, mm.now)
                slot_free[l_w2[0]] = (mm, mm.now)
                if stage == "z":
                    continue

                for h in range(2):
                    l_wg = load(c.res["wgb"].ap()[ex][:, :, h * 1024:(h + 1) * 1024],
                                FC, pw)
                    l_wl = load(c.res["wlb"].ap()[ex][:, :, h * 1024:(h + 1) * 1024],
                                FC, pw)
                    wgv, wlv = v_half(l_wg[0]), v_half(l_wl[0])
                    for gf in range(NEXP):
                        gt = h * NEXP + gf
                        jg, pG, wG = banks.acquire()
                        pg.emit("tensor",
                                lambda e, gf=gf, pG=pG, wgv=wgv: _kmm(
                                    e, pG, wgv, z, gf, FC, 0),
                                wait=[(wdma, l_wg[1]), (evv, z_done)] + wG,
                                inc=[(mm, 1)])
                        mmv = mm.now
                        pg.emit("scalar", lambda e, gt=gt, gf=gf, ex=ex, pG=pG: e.activation(
                            sg[:, gf, :], pG, AF.Silu, bias=bg_fm[:, gt, ex:ex + 1]),
                            wait=[(mm, mmv), (evv, evv.now)], drain=True, inc=[(evs, 1)])
                        banks.release(jg, evs)
                    for gf in range(NEXP):
                        gt = h * NEXP + gf
                        jl, pL, wL = banks.acquire()
                        pg.emit("tensor",
                                lambda e, gf=gf, pL=pL, wlv=wlv: _kmm(
                                    e, pL, wlv, z, gf, FC, 0),
                                wait=[(wdma, l_wl[1])] + wL, inc=[(mm, 1)])
                        mmv = mm.now
                        pg.emit("vector",
                                lambda e, gt=gt, ex=ex, pL=pL: e.scalar_tensor_tensor(
                                    ls[:, :], pL, bl_fm[:, gt, ex:ex + 1],
                                    cbE[:, :], ALU.add, ALU.mult),
                                wait=[(mm, mmv), (evs, evs.now)], drain=True,
                                inc=[(evv, 1)])
                        banks.release(jl, evv)
                        pg.emit("vector", lambda e, gt=gt, gf=gf: e.tensor_tensor(
                            o[:, gt, :], ls[:, :], sg[:, gf, :],
                            ALU.mult), drain=True, inc=[(evv, 1)])
                    slot_free[l_wg[0]] = (mm, mm.now)
                    slot_free[l_wl[0]] = (mm, mm.now)
                o_done = evv.now
                if stage == "gl":
                    continue

                l_w3 = load(c.res["w3b"].ap()[ex], FC, pw)
                w3v = v_half(l_w3[0])
                def y_ct(ct):
                    jy, pY, wY = banks.acquire()
                    pg.emit("tensor",
                            lambda e, ct=ct, pY=pY, w3v=w3v: _kmm(
                                e, pY, w3v, o, ct, FC, 0),
                            wait=[(wdma, l_w3[1]), (evv, o_done)] + wY, inc=[(mm, 1)])
                    mmv = mm.now
                    pg.emit("vector", lambda e, pY=pY: e.tensor_copy(ys[:, :], pY),
                            wait=[(mm, mmv), (fdma, fdma.now)], drain=True,
                            inc=[(evv, 1)])
                    banks.release(jy, evv)
                    pg.emit("gpsimd", lambda e, ct=ct, t0=t0: e.dma_start(
                        out=F_hbm[ct, :, t0:t0 + 512],
                        in_=ys[:, :], accum_op=ALU.add),
                        wait=[(evv, evv.now)], inc=[(fdma, 16)])
                for ct in range(4):
                    y_ct(ct)
                if it_idx + 1 < len(iters):
                    ntt2, nex, _ = iters[it_idx + 1]
                    npw = [c.prep_ready[nex]] if ntt2 == 0 else []
                    it_state["next12"] = (
                        load(c.res["w1b"].ap()[nex], CC, npw),
                        load(c.res["w2b"].ap()[nex], CC, npw))
                for ct in range(4, CC):
                    y_ct(ct)
                slot_free[l_w3[0]] = (mm, mm.now)

    c.last_done = [(fdma, fdma.now), (evv, evv.now), (evs, evs.now), (mm, mm.now)]


def _kmm(e, psum, wbuf, act, ftile, nk, t0):
    r = None
    for kc in range(nk):
        r = e.matmul(psum, wbuf[:, kc, ftile * P:(ftile + 1) * P],
                     act[:, kc, t0:t0 + 512], start=(kc == 0), stop=(kc == nk - 1))
    return r


def emit_A4(c):
    nc, pg = c.nc, c.pg
    c.arena_reset()
    Q = c.arena_alloc("a4_Q", [P, CC, T], BF16)
    Kf = c.arena_alloc("a4_K", [P, CC, T], BF16)
    Vt = c.arena_alloc("a4_V", [P, 16, C], BF16)
    wpa = c.arena_alloc("a4_wpa", [P, CC, C], BF16)
    wpb = c.arena_alloc("a4_wpb", [P, CC, C], BF16)
    c.a5 = dict(Q=Q, Kf=Kf, Vt=Vt, wpa=wpa, wpb=wpb)

    wdma = pg.sem("a4_wdma")
    mm = pg.sem("a4_mm")
    evv = pg.sem("a4_evv")

    xn_bf = c.res["xn_bf"]
    vecs = c.res["vecs"]
    dvec = c.res["dvec"]
    banks = Banks([c.pb01[:, 0:512], c.pb01[:, 512:1024], c.pb2, c.pb3])
    prev = list(c.last_done)

    for wname, dst, w0 in (("L0_attn_Wq", wpa, prev), ("L0_attn_Wk", wpb, prev)):
        src_ap = c.inp[wname].ap().rearrange("(kc p) f -> p kc f", p=P)
        pg.emit("gpsimd", lambda e, dst=dst, src_ap=src_ap: e.dma_start(
            out=dst[:, :, :], in_=src_ap), wait=w0, inc=[(wdma, 16)])
    wd_qk = wdma.now

    for co in range(CC):
        for s in range(4):
            jq, pQ, wQ = banks.acquire()
            jk, pK, wK = banks.acquire()
            pg.emit("tensor", lambda e, co=co, s=s, pQ=pQ: _kmm(
                e, pQ, wpa, xn_bf, co, CC, s * 512),
                wait=[(wdma, wd_qk)] + wQ + wK, inc=[(mm, 1)])
            pg.emit("tensor", lambda e, co=co, s=s, pK=pK: _kmm(
                e, pK, wpb, xn_bf, co, CC, s * 512),
                wait=[], inc=[(mm, 1)])
            mmv = mm.now
            pg.emit("vector", lambda e, co=co, s=s, pQ=pQ: e.tensor_scalar(
                Q[:, co, s * 512:(s + 1) * 512], pQ, 0.125,
                dvec[:, DV["bq8"], co:co + 1], ALU.mult, ALU.add),
                wait=[(mm, mmv)], drain=True, inc=[(evv, 1)])
            banks.release(jq, evv)
            pg.emit("vector", lambda e, co=co, s=s, pK=pK: e.tensor_scalar_add(
                Kf[:, co, s * 512:(s + 1) * 512], pK, vecs[:, VI["bk"], co:co + 1]),
                drain=True, inc=[(evv, 1)])
            banks.release(jk, evv)

    src_ap = c.inp["L0_attn_Wv"].ap().rearrange("(kc p) f -> p kc f", p=P)
    pg.emit("gpsimd", lambda e, src_ap=src_ap: e.dma_start(
        out=wpa[:, :, :], in_=src_ap), wait=[(mm, mm.now)], inc=[(wdma, 16)])
    wd_v = wdma.now

    for i in range(16):
        for half in range(2):
            jv, pV, wV = banks.acquire()
            pg.emit("tensor", lambda e, i=i, half=half, pV=pV: _vmm(
                e, pV, xn_bf, wpa, i, half),
                wait=[(wdma, wd_v)] + wV, inc=[(mm, 1)])
            mmv = mm.now
            pg.emit("vector", lambda e, i=i, half=half, pV=pV: e.tensor_copy(
                Vt[:, i, half * 512:(half + 1) * 512], pV),
                wait=[(mm, mmv)], drain=True, inc=[(evv, 1)])
            banks.release(jv, evv)

    c.last_done = [(mm, mm.now), (evv, evv.now)]


def _vmm(e, psum, xn_bf, wv, i, half):
    r = None
    for kc in range(CC):
        r = e.matmul(psum, xn_bf[:, kc, i * P:(i + 1) * P],
                     wv[:, kc, half * 512:(half + 1) * 512],
                     start=(kc == 0), stop=(kc == CC - 1))
    return r



def emit_A5(c):
    nc, pg = c.nc, c.pg
    Q, Kf, Vt = c.a5["Q"], c.a5["Kf"], c.a5["Vt"]
    wpb = c.a5["wpb"]
    masks = c.arena_alloc("a5_masks", [P, 5, 640], BF16)
    sc = [c.arena_alloc(f"a5_sc{i}", [P, 640], F32) for i in range(2)]
    pbf = [c.arena_alloc(f"a5_pbf{i}", [P, 640], BF16) for i in range(2)]
    pbf2 = [c.arena_alloc(f"a5_pbf2{i}", [P, 640], BF16) for i in range(2)]
    pT = [c.arena_alloc(f"a5_pT{i}", [P, 5, P], BF16) for i in range(2)]
    ao = c.arena_alloc("a5_ao", [P, CC, P], BF16)
    stt_ = [c.arena_alloc(f"a5_st{i}", [P, 4], F32) for i in range(2)]
    x1b = c.arena_alloc("a5_x1b", [P, CC, P], F32)
    sf = c.arena_alloc("a5_sf", [P, P], F32)
    ss = c.arena_alloc("a5_ss", [P, P], F32)
    ss2 = c.arena_alloc("a5_ss2", [P, P], F32)
    wpa_off = 3 * ((CC * T * 2 + 31) & ~31)
    blk32 = (CC * T * 2 + 31) & ~31
    blk16 = (CC * C * 2 + 31) & ~31
    wpa_off = 3 * blk32
    xtm = [c.arena_alloc_at(f"a5_x{i}", [P, C], F32, wpa_off + i * 4096) for i in range(2)]
    xfm = c.arena_alloc_at("a5_xfm", [P, CC, P], F32, wpa_off + 8192)
    Ftile = c.arena_alloc_at("a5_Ft", [P, CC, P], F32, wpa_off + 12288)

    ld = pg.sem("a5_ld")
    fl = pg.sem("a5_fl")
    mm = pg.sem("a5_mm")
    evv = pg.sem("a5_evv")
    evs = pg.sem("a5_evs")
    odma = pg.sem("a5_odma")
    wdma = pg.sem("a5_wdma")

    vecs = c.res["vecs"]
    idb = c.res["ident_bf"]
    idf = c.res["ident_f"]
    x_h = c.inp["x"].ap()
    F_hbm = c.res["F_hbm"].ap()
    x1_hbm = c.res["x1_hbm"].ap()
    prev = list(c.last_done)

    src_ap = c.inp["L0_attn_Wd"].ap().rearrange("(kc p) f -> p kc f", p=P)
    pg.emit("gpsimd", lambda e, src_ap=src_ap: e.dma_start(
        out=wpb[:, :, :], in_=src_ap), wait=prev, inc=[(wdma, 16)])
    wd_d = wdma.now
    mk = pg.sem("a5_mask")
    for mi, off in enumerate((0, 128, 256, 384, 512)):
        pg.emit("gpsimd", lambda e, mi=mi: e.memset(masks[:, mi, :], 0.0),
                wait=(list(prev) if mi == 0 else []), drain=True)
        pg.emit("gpsimd", lambda e, mi=mi, off=off: e.affine_select(
            masks[:, mi, :], masks[:, mi, :], [[-1, 640]], ALU.is_ge, -1e9,
            base=WIN + off, channel_multiplier=1), drain=True)
        pg.emit("gpsimd", lambda e, mi=mi, off=off: e.affine_select(
            masks[:, mi, :], masks[:, mi, :], [[1, 640]], ALU.is_ge, -1e9,
            base=WIN - off, channel_multiplier=-1), drain=True,
            inc=[(mk, 1)])

    for qt in range(16):
        q0 = qt * P
        ks = min(max(q0 - 2 * WIN // 2, 0), T - 640)
        ks = min(max(q0 - 256, 0), T - 640)
        mi = (q0 - ks) // 128
        xr = xtm[qt % 2]
        wx = [(evv, evv.now)] if qt >= 2 else list(prev)
        pg.emit("sync", lambda e, xr=xr, q0=q0: e.dma_start(
            out=xr[:, :], in_=x_h[q0:q0 + P, :]), wait=wx, inc=[(ld, 16)])
        pg.emit("sync", lambda e, q0=q0: e.dma_start(
            out=Ftile[:, :, :],
            in_=F_hbm[:, :, q0:q0 + P].rearrange("cc p t -> p cc t")),
            wait=[(evv, evv.now)] + (list(prev) if qt == 0 else []), inc=[(fl, 16)])
        ld_t = ld.now
        fl_t = fl.now

        for cc in range(CC):
            pg.emit("tensor", lambda e, xr=xr, cc=cc: e.transpose(
                c.pbu[:, 0:P], xr[:, cc * P:(cc + 1) * P], idf[:, :]),
                wait=[(ld, ld_t), (evv, evv.now)], inc=[(mm, 1)])
            pg.emit("vector", lambda e, cc=cc: e.tensor_copy(
                xfm[:, cc, :], c.pbu[:, 0:P]),
                wait=[(mm, mm.now)], drain=True, inc=[(evv, 1)])

        if qt == 0:
            c.a5p = {
                "scfree": [None, None],
                "pbffree": [None, None],
                "pb2free": [None, None],
                "ptfree": [None, None],
                "pvfree": [None, None],
                "maskdone": [None, None],
                "tpfree": [None, None],
            }
        st5 = c.a5p
        for h in range(NHEAD):
            par = h % 2
            pb = (h % 2) * 64
            hc = h // 2
            sc_p = sc[par]; pbf_p = pbf[par]; pbf2_p = pbf2[par]
            pT_p = pT[par]; stt_p = stt_[par]
            spsum = c.pb01[:, 0:640] if par == 0 else c.pb23[:, 0:640]
            spsum_hi = (c.pb01[:, 512:640] if par == 0 else c.pb23[:, 512:640])
            spsum_lo = (c.pb01[:, 0:512] if par == 0 else c.pb23[:, 0:512])

            def smm(e, pb=pb, hc=hc, q0=q0, ks=ks, spsum_lo=spsum_lo,
                    spsum_hi=spsum_hi):
                e.matmul(spsum_lo, Q[pb:pb + 64, hc, q0:q0 + P],
                         Kf[pb:pb + 64, hc, ks:ks + 512], start=True, stop=True)
                return e.matmul(spsum_hi, Q[pb:pb + 64, hc, q0:q0 + P],
                                Kf[pb:pb + 64, hc, ks + 512:ks + 640],
                                start=True, stop=True)
            wsm = [(wdma, wd_d)]
            if st5.get("lastnorm"):
                wsm.append(st5["lastnorm"])
            if st5.get("lastexp"):
                wsm.append(st5["lastexp"])
            if st5["maskdone"][par]:
                wsm.append(st5["maskdone"][par])
            if qt == 0 and h < 2:
                wsm.append((evv, evv.now))
            pg.emit("tensor", smm, wait=wsm, inc=[(mm, 1)])
            wmt = [(mm, mm.now), (mk, 5)]
            if st5["scfree"][par]:
                wmt.append(st5["scfree"][par])
            pg.emit("vector", lambda e, mi=mi, sc_p=sc_p, spsum=spsum: e.tensor_tensor(
                sc_p[:, :], spsum, masks[:, mi, :], ALU.add),
                wait=wmt, drain=True, inc=[(evv, 1)])
            st5["maskdone"][par] = (evv, evv.now)
            pg.emit("vector", lambda e, sc_p=sc_p, stt_p=stt_p: e.reduce_max(
                stt_p[:, 0:1], sc_p[:, :], axis=AX.X, negate=True), drain=True,
                inc=[(evv, 1)])
            wex = [(evv, evv.now)]
            if st5["pbffree"][par]:
                wex.append(st5["pbffree"][par])
            pg.emit("scalar", lambda e, sc_p=sc_p, pbf_p=pbf_p, stt_p=stt_p: e.activation(
                pbf_p[:, :], sc_p[:, :], AF.Exp, bias=stt_p[:, 0:1],
                accum_out=stt_p[:, 1:2]),
                wait=wex, drain=True, inc=[(evs, 1)])
            st5["scfree"][par] = (evs, evs.now)
            st5["lastexp"] = (evs, evs.now)
            pg.emit("vector", lambda e, stt_p=stt_p: e.reciprocal(
                stt_p[:, 2:3], stt_p[:, 1:2]),
                wait=[(evs, evs.now)], drain=True)
            wnm = []
            if st5["pb2free"][par]:
                wnm.append(st5["pb2free"][par])
            pg.emit("vector", lambda e, pbf_p=pbf_p, pbf2_p=pbf2_p, stt_p=stt_p:
                    e.tensor_scalar_mul(pbf2_p[:, :], pbf_p[:, :], stt_p[:, 2:3]),
                    wait=wnm, drain=True, inc=[(evv, 1)])
            st5["pbffree"][par] = (evv, evv.now)
            st5["lastnorm"] = (evv, evv.now)
            ps5 = c.psum_tp if par == 0 else c.psum_tp2
            wtp = [(evv, evv.now)]
            if st5["tpfree"][par]:
                wtp.append(st5["tpfree"][par])

            def tps(e, ps5=ps5, pbf2_p=pbf2_p):
                r = None
                for ki in range(5):
                    r = e.transpose(ps5[:, ki, :], pbf2_p[:, ki * P:(ki + 1) * P],
                                    idb[:, :])
                return r
            pg.emit("tensor", tps, wait=wtp, inc=[(mm, 1)])
            st5["pb2free"][par] = (mm, mm.now)
            wev = [(mm, mm.now)]
            if st5["ptfree"][par]:
                wev.append(st5["ptfree"][par])
            pg.emit("vector", lambda e, ps5=ps5, pT_p=pT_p: e.tensor_copy(
                pT_p[:, :, :], ps5[:, :, :]),
                wait=wev, drain=True, inc=[(evv, 1)])
            st5["tpfree"][par] = (evv, evv.now)
            pvb = c.pbu if h % 2 == 0 else c.pbs
            def pvmm(e, h=h, pb=pb, ks=ks, pvb=pvb, pT_p=pT_p):
                r = None
                for ki in range(5):
                    r = e.matmul(pvb[pb:pb + 64, 0:P],
                                 Vt[:, ks // P + ki, h * 64:(h + 1) * 64],
                                 pT_p[:, ki, :], start=(ki == 0), stop=(ki == 4))
                return r
            wpv = [(evv, evv.now)]
            if st5["pvfree"][par]:
                wpv.append(st5["pvfree"][par])
            if qt == 0 and h < 2:
                wpv.append((evv, evv.now))
            pg.emit("tensor", pvmm, wait=wpv, inc=[(mm, 1)])
            st5["ptfree"][par] = (mm, mm.now)
            mmv = mm.now
            pg.emit("vector", lambda e, h=h, pb=pb, pvb=pvb: e.tensor_scalar_add(
                ao[pb:pb + 64, h // 2, :], pvb[pb:pb + 64, 0:P],
                vecs[pb:pb + 64, VI["bv"], h // 2:h // 2 + 1]),
                wait=[(mm, mmv)], drain=True, inc=[(evv, 1)])
            st5["pvfree"][par] = (evv, evv.now)

        for ct in range(CC):
            jy = None
            pg.emit("tensor", lambda e, ct=ct: _wdmm(e, c.pbs[:, 0:P], wpb, ao, ct),
                    wait=[(evv, evv.now), (fl, fl_t)], inc=[(mm, 1)])
            mmv = mm.now
            pg.emit("scalar", lambda e, ct=ct: e.activation(
                sf[:, :], Ftile[:, ct, :], AF.Silu),
                wait=[(fl, fl_t), (evv, evv.now)], drain=True, inc=[(evs, 1)])
            pg.emit("vector", lambda e, ct=ct: e.tensor_scalar_add(
                ss[:, :], c.pbs[:, 0:P], vecs[:, VI["bd"], ct:ct + 1]),
                wait=[(mm, mmv), (evs, evs.now)], drain=True)
            pg.emit("vector", lambda e: e.tensor_tensor(
                ss2[:, :], ss[:, :], sf[:, :], ALU.mult), drain=True)
            pg.emit("vector", lambda e, ct=ct: e.tensor_tensor(
                x1b[:, ct, :], ss2[:, :], xfm[:, ct, :], ALU.add),
                wait=([(odma, odma.now)] if ct == 0 else []),
                drain=True, inc=[(evv, 1)])
        pg.emit("sync", lambda e, q0=q0: e.dma_start(
            out=x1_hbm[:, :, q0:q0 + P].rearrange("cc p t -> p cc t"),
            in_=x1b[:, :, :]),
            wait=[(evv, evv.now), (odma, odma.now)], inc=[(odma, 16)])

    c.last_done = [(odma, odma.now), (evv, evv.now), (evs, evs.now), (mm, mm.now)]


def _wdmm(e, psum, wd, ao, ct):
    r = None
    for kc in range(CC):
        r = e.matmul(psum, wd[:, kc, ct * P:(ct + 1) * P], ao[:, kc, :],
                     start=(kc == 0), stop=(kc == CC - 1))
    return r


def emit_B1(c):
    nc, pg = c.nc, c.pg
    c.arena_reset()
    x1t = c.arena_alloc("b1_x1t", [P, CC, 512], F32)
    xat = c.arena_alloc("b1_xat", [P, CC, 512], F32)
    sq = c.arena_alloc("b1_sq", [P, 512], BF16)
    rows = c.arena_alloc("b1_rows", [1, 8, 512], F32)
    t1 = c.arena_alloc("b1_t1", [P, 512], F32)

    ld = pg.sem("b1_ld")
    mm = pg.sem("b1_mm")
    evv = pg.sem("b1_evv")
    evs = pg.sem("b1_evs")
    odma = pg.sem("b1_odma")

    x1_hbm = c.res["x1_hbm"].ap()
    xa_hbm = c.res["xa_hbm"].ap()
    xn_bf = c.res["xn_bf"]
    onef = c.res["ones_col_f"]
    oneb = c.res["ones_col_bf"]
    onerf = c.res["ones_row_f"]
    cval = c.res["cval"]
    prev = list(c.last_done)

    for tt in range(4):
        t0 = tt * 512
        w0 = list(prev) if tt == 0 else [(evv, evv.now)]
        pg.emit("sync", lambda e, t0=t0: e.dma_start(
            out=x1t[:, :, :],
            in_=x1_hbm[:, :, t0:t0 + 512].rearrange("cc p t -> p cc t")),
            wait=w0, inc=[(ld, 16)])
        ldv = ld.now
        def sxmm(e):
            r = None
            for cc in range(CC):
                r = e.matmul(c.pbs[0:1, :], onef[:, 0:1], x1t[:, cc, :],
                             start=(cc == 0), stop=(cc == CC - 1))
            return r
        pg.emit("tensor", sxmm, wait=[(ld, ldv), (evv, evv.now)], inc=[(mm, 1)])
        pg.emit("vector", lambda e: e.tensor_copy(rows[:, 0, :], c.pbs[0:1, :]),
                wait=[(mm, mm.now)], drain=True, inc=[(evv, 1)])
        for cc in range(CC):
            pg.emit("scalar", lambda e, cc=cc: e.activation(
                sq[:, :], x1t[:, cc, :], AF.Square),
                wait=[(ld, ldv), (mm, mm.now)],
                drain=True, inc=[(evs, 1)])
            pg.emit("tensor", lambda e, cc=cc: e.matmul(
                c.pbu[0:1, :], oneb[:, 0:1], sq[:, :],
                start=(cc == 0), stop=(cc == CC - 1)),
                wait=[(evs, evs.now)], inc=[(mm, 1)])
        pg.emit("vector", lambda e: e.tensor_copy(rows[:, 1, :], c.pbu[0:1, :]),
                wait=[(mm, mm.now)], drain=True, inc=[(evv, 1)])
        pg.emit("vector", lambda e: e.tensor_scalar_mul(rows[:, 2, :], rows[:, 0, :],
                                                        1.0 / C), drain=True)
        pg.emit("vector", lambda e: e.tensor_scalar_mul(rows[:, 3, :], rows[:, 2, :],
                                                        -1.0), drain=True)
        pg.emit("vector", lambda e: e.tensor_tensor(rows[:, 4, :], rows[:, 2, :],
                                                    rows[:, 2, :], ALU.mult), drain=True)
        pg.emit("vector", lambda e: e.scalar_tensor_tensor(
            rows[:, 5, :], rows[:, 1, :], 1.0 / C, rows[:, 4, :],
            ALU.mult, ALU.subtract), drain=True, inc=[(evv, 1)])
        pg.emit("scalar", lambda e: e.activation(rows[:, 6, :], rows[:, 5, :], AF.Sqrt,
                                                 bias=cval[0:1, CVAL["eps_ln"]:CVAL["eps_ln"] + 1]),
                wait=[(evv, evv.now)], drain=True, inc=[(evs, 1)])
        pg.emit("vector", lambda e: e.reciprocal(rows[:, 7, :], rows[:, 6, :]),
                wait=[(evs, evs.now)], drain=True, inc=[(evv, 1)])
        pg.emit("tensor", lambda e: e.matmul(c.pbu[:, :], onerf[0:1, :], rows[:, 3, :],
                                             start=True, stop=True),
                wait=[(evv, evv.now)], inc=[(mm, 1)])
        pg.emit("tensor", lambda e: e.matmul(c.pbs[:, :], onerf[0:1, :], rows[:, 7, :],
                                             start=True, stop=True),
                wait=[], inc=[(mm, 1)])
        mmv = mm.now
        for cc in range(CC):
            pg.emit("vector", lambda e, cc=cc: e.tensor_tensor(
                t1[:, :], x1t[:, cc, :], c.pbu[:, :], ALU.add),
                wait=[(mm, mmv)] + ([(odma, odma.now)] if cc == 0 else []),
                drain=True)
            pg.emit("vector", lambda e, cc=cc: e.tensor_tensor(
                xat[:, cc, :], t1[:, :], c.pbs[:, :], ALU.mult), drain=True)
            pg.emit("vector", lambda e, cc=cc, t0=t0: e.tensor_copy(
                xn_bf[:, cc, t0:t0 + 512], xat[:, cc, :]), drain=True, inc=[(evv, 1)])
        pg.emit("gpsimd", lambda e, t0=t0: e.dma_start(
            out=xa_hbm[:, :, t0:t0 + 512].rearrange("cc p t -> p cc t"),
            in_=xat[:, :, :]), wait=[(evv, evv.now)], inc=[(odma, 16)])

    c.last_done = [(odma, odma.now), (evv, evv.now)]


def _rms_rows(c, pg, src_tile, sq, rows, mm, evv, evs, ld_w, eps_key):
    oneb = c.res["ones_col_bf"]
    cval = c.res["cval"]
    for cc in range(CC):
        pg.emit("scalar", lambda e, cc=cc: e.activation(
            sq[:, :], src_tile[:, cc, :], AF.Square),
            wait=list(ld_w) + [(mm, mm.now)],
            drain=True, inc=[(evs, 1)])
        pg.emit("tensor", lambda e, cc=cc: e.matmul(
            c.pbu[0:1, :], oneb[:, 0:1], sq[:, :],
            start=(cc == 0), stop=(cc == CC - 1)),
            wait=[(evs, evs.now)], inc=[(mm, 1)])
    pg.emit("vector", lambda e: e.tensor_copy(rows[:, 0, :], c.pbu[0:1, :]),
            wait=[(mm, mm.now)], drain=True, inc=[(evv, 1)])
    pg.emit("vector", lambda e: e.tensor_scalar_mul(rows[:, 1, :], rows[:, 0, :],
                                                    1.0 / C), drain=True, inc=[(evv, 1)])
    pg.emit("scalar", lambda e: e.activation(
        rows[:, 2, :], rows[:, 1, :], AF.Sqrt,
        bias=cval[0:1, CVAL[eps_key]:CVAL[eps_key] + 1]),
        wait=[(evv, evv.now)], drain=True, inc=[(evs, 1)])
    pg.emit("vector", lambda e: e.reciprocal(rows[:, 3, :], rows[:, 2, :]),
            wait=[(evs, evs.now)], drain=True, inc=[(evv, 1)])


def emit_B2(c):
    nc, pg = c.nc, c.pg
    c.arena_reset()
    xat = c.arena_alloc("b2_xat", [P, CC, 512], F32)
    hb = c.arena_alloc("b2_h", [P, CC, 513], F32)
    hl = c.arena_alloc("b2_hl", [P, CC, 1], F32)
    rows = c.arena_alloc("b2_rows", [1, 4, 512], F32)
    rows_off = nc.lookup_mloc(rows).addr - c.arena_addr
    d = c.arena_alloc_at("b2_d", [P, 512], F32, rows_off)
    sq = c.arena_alloc_at("b2_sq", [P, 512], BF16, rows_off + 2048)
    xk = c.arena_alloc("b2_xk", [P, CC, 512], BF16)
    xv = c.arena_alloc("b2_xv", [P, CC, 512], BF16)
    xr = c.arena_alloc("b2_xr", [P, CC, 512], BF16)
    wb = c.arena_alloc("b2_wb", [P, CC, C], BF16)
    kf = c.arena_alloc("b2_kf", [P, CC, 512], F32)
    vf = c.arena_alloc("b2_vf", [P, CC, 512], F32)
    rcc = c.arena_alloc("b2_rcc", [P, 2, 512], F32)
    rwkv = c.arena_alloc_at("b2_rwkv", [P, CC, 512], BF16,
                            nc.lookup_mloc(xk).addr - c.arena_addr)
    abuf = c.arena_alloc("b2_ab", [P, 2, 513], F32)
    bbuf = c.arena_alloc("b2_bb", [P, 2, 513], F32)
    ekt = c.arena_alloc("b2_ekt", [P, 2, 512], F32)
    ek = c.arena_alloc("b2_ek", [P, 2, 512], F32)
    ekv = c.arena_alloc("b2_ekv", [P, 2, 512], F32)
    ast = c.arena_alloc("b2_ast", [P, CC], F32)
    bst = c.arena_alloc("b2_bst", [P, CC], F32)

    for nm, t in (("b2_kf", kf), ("b2_vf", vf), ("b2_rwkv", rwkv),
                  ("b2_xk", xk), ("b2_h", hb), ("b2_ast", ast), ("b2_bst", bst)):
        c.res[nm] = t

    ld = pg.sem("b2_ld")
    wdma = pg.sem("b2_wdma")
    mm = pg.sem("b2_mm")
    evv = pg.sem("b2_evv")
    evs = pg.sem("b2_evs")
    odma = pg.sem("b2_odma")

    xa_hbm = c.res["xa_hbm"].ap()
    vecs = c.res["vecs"]
    dvec = c.res["dvec"]
    cval = c.res["cval"]
    onerf = c.res["ones_row_f"]
    prev = list(c.last_done)

    pg.emit("gpsimd", lambda e: e.memset(ast[:, :], 0.0), wait=prev, drain=True,
            inc=[(odma, 1)])
    pg.emit("gpsimd", lambda e: e.memset(bst[:, :], 0.0), drain=True, inc=[(odma, 1)])
    pg.emit("gpsimd", lambda e: e.memset(hl[:, :, :], 0.0), drain=True, inc=[(odma, 1)])
    init_v = odma.now

    banks = Banks([c.pb01[:, 0:512], c.pb01[:, 512:1024], c.pb2, c.pb3])
    scst = [{"exp_ekt": 0, "rwkv": 0} for _ in range(2)]

    for ti in range(4):
        t0 = ti * 512
        w0 = [(odma, init_v)] if ti == 0 else [(evv, evv.now), (odma, odma.now)]
        pg.emit("sync", lambda e, t0=t0: e.dma_start(
            out=xat[:, :, :],
            in_=xa_hbm[:, :, t0:t0 + 512].rearrange("cc p t -> p cc t")),
            wait=w0 + list(prev if ti == 0 else []), inc=[(ld, 16)])
        ldv = ld.now
        _rms_rows(c, pg, xat, sq, rows, mm, evv, evs, [(ld, ldv)], "eps_rms")
        pg.emit("tensor", lambda e: e.matmul(c.pbs[:, :], onerf[0:1, :], rows[:, 3, :],
                                             start=True, stop=True),
                wait=[(evv, evv.now)], inc=[(mm, 1)])
        mmv = mm.now
        for cc in range(CC):
            pg.emit("vector", lambda e, cc=cc: e.scalar_tensor_tensor(
                hb[:, cc, 1:513], xat[:, cc, :], vecs[:, VI["ln1_w"], cc:cc + 1],
                c.pbs[:, :], ALU.mult, ALU.mult),
                wait=[(mm, mmv)], drain=True, inc=[(evv, 1)])
        pg.emit("vector", lambda e: e.tensor_copy(hb[:, :, 0:1], hl[:, :, :]),
                drain=True, inc=[(evv, 1)])
        for cc in range(CC):
            pg.emit("vector", lambda e, cc=cc: e.tensor_tensor(
                d[:, :], hb[:, cc, 1:513], hb[:, cc, 0:512], ALU.subtract),
                drain=True)
            for mix, dst in (("time_mix_k", xk), ("time_mix_v", xv), ("time_mix_r", xr)):
                pg.emit("vector", lambda e, cc=cc, mix=mix, dst=dst: e.scalar_tensor_tensor(
                    dst[:, cc, :], d[:, :], vecs[:, VI[mix], cc:cc + 1],
                    hb[:, cc, 0:512], ALU.mult, ALU.add), drain=True, inc=[(evv, 1)])
        mixes_done = evv.now
        pg.emit("vector", lambda e: e.tensor_copy(hl[:, :, :], hb[:, :, 512:513]),
                drain=True, inc=[(evv, 1)])

        for wname, xsrc, dst in (("L1_attn_W_key", xk, kf),
                                 ("L1_attn_W_value", xv, vf)):
            src_ap = c.inp[wname].ap().rearrange("(kc p) f -> p kc f", p=P)
            pg.emit("gpsimd", lambda e, src_ap=src_ap: e.dma_start(
                out=wb[:, :, :], in_=src_ap), wait=[(mm, mm.now)], inc=[(wdma, 16)])
            wdv = wdma.now
            for co in range(CC):
                jj, pp, ww = banks.acquire()
                pg.emit("tensor", lambda e, co=co, xsrc=xsrc, pp=pp: _kmm2(
                    e, pp, wb, xsrc, co),
                    wait=[(wdma, wdv), (evv, mixes_done)] + ww, inc=[(mm, 1)])
                mmv2 = mm.now
                pg.emit("vector", lambda e, dst=dst, co=co, pp=pp: e.tensor_copy(
                    dst[:, co, :], pp), wait=[(mm, mmv2)], drain=True, inc=[(evv, 1)])
                banks.release(jj, evv)
        src_ap = c.inp["L1_attn_W_recept"].ap().rearrange("(kc p) f -> p kc f", p=P)
        pg.emit("gpsimd", lambda e, src_ap=src_ap: e.dma_start(
            out=wb[:, :, :], in_=src_ap), wait=[(mm, mm.now)], inc=[(wdma, 16)])
        wd_rec = wdma.now

        for cp in range(4):
            prs = []
            for p2 in range(2):
                cc = 2 * cp + p2
                jj, pp, ww = banks.acquire()
                pg.emit("tensor", lambda e, cc=cc, pp=pp: _kmm2(e, pp, wb, xr, cc),
                        wait=[(wdma, wd_rec)] + ww, inc=[(mm, 1)])
                prs.append((jj, pp, mm.now))
                pg.emit("vector", lambda e, cc=cc, p2=p2: e.tensor_scalar(
                    ekt[:, p2, :], kf[:, cc, :],
                    vecs[:, VI["time_first"], cc:cc + 1], 20.0, ALU.add, ALU.min),
                    wait=[(evs, scst[p2]["exp_ekt"])], drain=(p2 == 0))
            for p2 in range(2):
                pg.emit("vector", lambda e, p2=p2: e.tensor_scalar_max(
                    ekt[:, p2, :], ekt[:, p2, :], -20.0),
                    drain=(p2 == 0), inc=[(evv, 1)])
                scst[p2]["clip"] = evv.now
            for p2 in range(2):
                cc = 2 * cp + p2
                pg.emit("vector", lambda e, cc=cc, p2=p2: e.tensor_scalar(
                    ek[:, p2, :], kf[:, cc, :], 20.0, -20.0, ALU.min, ALU.max),
                    drain=(p2 == 0), inc=[(evv, 1)])
                scst[p2]["ekclip"] = evv.now
            for p2 in range(2):
                pg.emit("scalar", lambda e, p2=p2: e.activation(
                    ekt[:, p2, :], ekt[:, p2, :], AF.Exp),
                    wait=[(evv, scst[p2]["clip"])], drain=True, inc=[(evs, 1)])
                scst[p2]["exp_ekt"] = evs.now
            for p2 in range(2):
                pg.emit("scalar", lambda e, p2=p2: e.activation(
                    ek[:, p2, :], ek[:, p2, :], AF.Exp),
                    wait=[(evv, scst[p2]["ekclip"])], drain=True, inc=[(evs, 1)])
                scst[p2]["exp_ek"] = evs.now
            for p2 in range(2):
                cc = 2 * cp + p2
                pg.emit("vector", lambda e, cc=cc, p2=p2: e.tensor_tensor(
                    ekv[:, p2, :], ek[:, p2, :], vf[:, cc, :], ALU.mult),
                    wait=[(evs, scst[p2]["exp_ek"])], drain=(p2 == 0))
            for p2 in range(2):
                cc = 2 * cp + p2
                pg.emit("vector", lambda e, cc=cc, p2=p2: e.tensor_copy(
                    abuf[:, p2, 0:1], ast[:, cc:cc + 1]), drain=(p2 == 0))
            for p2 in range(2):
                cc = 2 * cp + p2
                pg.emit("vector", lambda e, cc=cc, p2=p2: e.tensor_tensor_scan(
                    abuf[:, p2, 1:513],
                    dvec[:, DV["ew"], cc:cc + 1].broadcast_to([P, 512]),
                    ekv[:, p2, :], ast[:, cc:cc + 1], ALU.mult, ALU.add),
                    drain=(p2 == 0))
            for p2 in range(2):
                cc = 2 * cp + p2
                pg.emit("vector", lambda e, cc=cc, p2=p2: e.tensor_copy(
                    ast[:, cc:cc + 1], abuf[:, p2, 512:513]), drain=(p2 == 0))
            for p2 in range(2):
                cc = 2 * cp + p2
                pg.emit("vector", lambda e, cc=cc, p2=p2: e.tensor_copy(
                    bbuf[:, p2, 0:1], bst[:, cc:cc + 1]), drain=(p2 == 0))
            for p2 in range(2):
                cc = 2 * cp + p2
                pg.emit("vector", lambda e, cc=cc, p2=p2: e.tensor_tensor_scan(
                    bbuf[:, p2, 1:513],
                    dvec[:, DV["ew"], cc:cc + 1].broadcast_to([P, 512]),
                    ek[:, p2, :], bst[:, cc:cc + 1], ALU.mult, ALU.add),
                    drain=(p2 == 0))
            for p2 in range(2):
                cc = 2 * cp + p2
                pg.emit("vector", lambda e, cc=cc, p2=p2: e.tensor_copy(
                    bst[:, cc:cc + 1], bbuf[:, p2, 512:513]), drain=(p2 == 0))
            for p2 in range(2):
                cc = 2 * cp + p2
                pg.emit("vector", lambda e, cc=cc, p2=p2: e.tensor_tensor(
                    ekv[:, p2, :], ekt[:, p2, :], vf[:, cc, :], ALU.mult),
                    drain=(p2 == 0))
            for p2 in range(2):
                pg.emit("vector", lambda e, p2=p2: e.tensor_tensor(
                    ekv[:, p2, :], ekv[:, p2, :], abuf[:, p2, 0:512], ALU.add),
                    drain=(p2 == 0))
            for p2 in range(2):
                pg.emit("vector", lambda e, p2=p2: e.scalar_tensor_tensor(
                    ek[:, p2, :], ekt[:, p2, :], 1e-8, bbuf[:, p2, 0:512],
                    ALU.add, ALU.add), drain=(p2 == 0))
            for p2 in range(2):
                pg.emit("vector", lambda e, p2=p2: e.reciprocal(
                    abuf[:, p2, 0:512], ek[:, p2, :]), drain=(p2 == 0))
            for p2 in range(2):
                pg.emit("vector", lambda e, p2=p2: e.tensor_tensor(
                    ekv[:, p2, :], ekv[:, p2, :], abuf[:, p2, 0:512], ALU.mult),
                    drain=(p2 == 0))
            for p2 in range(2):
                jj, pp, mmv3 = prs[p2]
                pg.emit("scalar", lambda e, pp=pp, p2=p2: e.activation(
                    rcc[:, p2, :], pp, AF.Sigmoid),
                    wait=[(mm, mmv3), (evv, scst[p2]["rwkv"])], drain=True,
                    inc=[(evs, 1)])
                scst[p2]["sig"] = evs.now
                banks.release(jj, evs)
            for p2 in range(2):
                cc = 2 * cp + p2
                pg.emit("vector", lambda e, cc=cc, p2=p2: e.tensor_tensor(
                    rwkv[:, cc, :], ekv[:, p2, :], rcc[:, p2, :], ALU.mult),
                    wait=[(evs, scst[p2]["sig"])], drain=(p2 == 0),
                    inc=[(evv, 1)])
                scst[p2]["rwkv"] = evv.now
        scan_done = evv.now

        src_ap = c.inp["L1_attn_W_out"].ap().rearrange("(kc p) f -> p kc f", p=P)
        pg.emit("gpsimd", lambda e, src_ap=src_ap: e.dma_start(
            out=wb[:, :, :], in_=src_ap), wait=[(mm, mm.now)], inc=[(wdma, 16)])
        wdv = wdma.now
        for co in range(CC):
            jj, pp, ww = banks.acquire()
            pg.emit("tensor", lambda e, co=co, pp=pp: _kmm2(e, pp, wb, rwkv, co),
                    wait=[(wdma, wdv), (evv, scan_done)] + ww, inc=[(mm, 1)])
            mmv2 = mm.now
            pg.emit("vector", lambda e, co=co, pp=pp: e.tensor_tensor(
                xat[:, co, :], xat[:, co, :], pp, ALU.add),
                wait=[(mm, mmv2)], drain=True, inc=[(evv, 1)])
            banks.release(jj, evv)
        pg.emit("gpsimd", lambda e, t0=t0: e.dma_start(
            out=xa_hbm[:, :, t0:t0 + 512].rearrange("cc p t -> p cc t"),
            in_=xat[:, :, :]), wait=[(evv, evv.now), (odma, odma.now)],
            inc=[(odma, 16)])

    c.last_done = [(odma, odma.now), (evv, evv.now), (evs, evs.now), (mm, mm.now)]


def _kmm2(e, psum, wbuf, act, co):
    r = None
    for kc in range(CC):
        r = e.matmul(psum, wbuf[:, kc, co * P:(co + 1) * P], act[:, kc, :],
                     start=(kc == 0), stop=(kc == CC - 1))
    return r


def emit_B3(c):
    nc, pg = c.nc, c.pg
    c.arena_reset()
    xat = c.arena_alloc("b3_xat", [P, CC, 512], F32)
    hb = c.arena_alloc("b3_h", [P, CC, 513], F32)
    hl = c.arena_alloc("b3_hl", [P, CC, 1], F32)
    sq = c.arena_alloc("b3_sq", [P, 512], BF16)
    rows = c.arena_alloc("b3_rows", [1, 4, 512], F32)
    d = c.arena_alloc("b3_d", [P, 512], F32)
    xk2 = c.arena_alloc("b3_xk2", [P, CC, 512], BF16)
    xr2 = c.arena_alloc("b3_xr2", [P, CC, 512], BF16)
    wk2h = c.arena_alloc("b3_wk2", [P, CC, 2048], BF16)
    r2 = c.arena_alloc("b3_r2", [P, CC, 512], BF16)
    k2 = c.arena_alloc("b3_k2", [P, 32, 512], BF16)
    wv2 = [c.arena_alloc(f"b3_wv2{i}", [P, C], BF16) for i in range(2)]
    rl = c.arena_alloc("b3_rl", [P, 512], F32)
    t5 = c.arena_alloc("b3_t5", [P, 512], F32)
    wr2 = c.arena_alloc_at("b3_wr2", [P, CC, C], BF16,
                           nc.lookup_mloc(wk2h).addr - c.arena_addr)

    ld = pg.sem("b3_ld")
    wdma = pg.sem("b3_wdma")
    mm = pg.sem("b3_mm")
    evv = pg.sem("b3_evv")
    evs = pg.sem("b3_evs")
    odma = pg.sem("b3_odma")

    xa_hbm = c.res["xa_hbm"].ap()
    vecs = c.res["vecs"]
    onerf = c.res["ones_row_f"]
    prev = list(c.last_done)

    pg.emit("gpsimd", lambda e: e.memset(hl[:, :, :], 0.0), wait=prev, drain=True,
            inc=[(odma, 1)])
    init_v = odma.now

    banks = Banks([c.pb01[:, 0:512], c.pb01[:, 512:1024], c.pb2, c.pb3])
    scst = [{"exp_ekt": 0, "rwkv": 0} for _ in range(2)]

    for ti in range(4):
        t0 = ti * 512
        w0 = [(odma, init_v)] + list(prev) if ti == 0 else [(evv, evv.now), (odma, odma.now)]
        pg.emit("sync", lambda e, t0=t0: e.dma_start(
            out=xat[:, :, :],
            in_=xa_hbm[:, :, t0:t0 + 512].rearrange("cc p t -> p cc t")),
            wait=w0, inc=[(ld, 16)])
        ldv = ld.now
        _rms_rows(c, pg, xat, sq, rows, mm, evv, evs, [(ld, ldv)], "eps_rms")
        pg.emit("tensor", lambda e: e.matmul(c.pbs[:, :], onerf[0:1, :], rows[:, 3, :],
                                             start=True, stop=True),
                wait=[(evv, evv.now)], inc=[(mm, 1)])
        mmv = mm.now
        for cc in range(CC):
            pg.emit("vector", lambda e, cc=cc: e.scalar_tensor_tensor(
                hb[:, cc, 1:513], xat[:, cc, :], vecs[:, VI["ln2_w"], cc:cc + 1],
                c.pbs[:, :], ALU.mult, ALU.mult),
                wait=[(mm, mmv)], drain=True, inc=[(evv, 1)])
        pg.emit("vector", lambda e: e.tensor_copy(hb[:, :, 0:1], hl[:, :, :]),
                drain=True, inc=[(evv, 1)])
        for cc in range(CC):
            pg.emit("vector", lambda e, cc=cc: e.tensor_tensor(
                d[:, :], hb[:, cc, 1:513], hb[:, cc, 0:512], ALU.subtract),
                drain=True)
            for mix, dst in (("time_mix_k2", xk2), ("time_mix_r2", xr2)):
                pg.emit("vector", lambda e, cc=cc, mix=mix, dst=dst: e.scalar_tensor_tensor(
                    dst[:, cc, :], d[:, :], vecs[:, VI[mix], cc:cc + 1],
                    hb[:, cc, 0:512], ALU.mult, ALU.add), drain=True, inc=[(evv, 1)])
        mixes_done = evv.now
        pg.emit("vector", lambda e: e.tensor_copy(hl[:, :, :], hb[:, :, 512:513]),
                drain=True, inc=[(evv, 1)])

        src_ap = c.inp["L1_attn_W_recept2"].ap().rearrange("(kc p) f -> p kc f", p=P)
        pg.emit("gpsimd", lambda e, src_ap=src_ap: e.dma_start(
            out=wr2[:, :, :], in_=src_ap), wait=[(mm, mm.now)], inc=[(wdma, 16)])
        wdv = wdma.now
        for co in range(CC):
            jj, pp, ww = banks.acquire()
            pg.emit("tensor", lambda e, co=co, pp=pp: _kmm2(e, pp, wr2, xr2, co),
                    wait=[(wdma, wdv), (evv, mixes_done)] + ww, inc=[(mm, 1)])
            pg.emit("scalar", lambda e, co=co, pp=pp: e.activation(
                r2[:, co, :], pp, AF.Sigmoid),
                wait=[(mm, mm.now)], drain=True, inc=[(evs, 1)])
            banks.release(jj, evs)

        for fh in range(2):
            src_ap = c.inp["L1_attn_W_key2"].ap()[:, fh * 2048:(fh + 1) * 2048]
            src_ap = src_ap.rearrange("(kc p) f -> p kc f", p=P)
            pg.emit("gpsimd", lambda e, src_ap=src_ap: e.dma_start(
                out=wk2h[:, :, :], in_=src_ap), wait=[(mm, mm.now)], inc=[(wdma, 16)])
            wdv = wdma.now
            for f2 in range(FC):
                jj, pp, ww = banks.acquire()
                pg.emit("tensor", lambda e, f2=f2, pp=pp: _kmm2(e, pp, wk2h, xk2, f2),
                        wait=[(wdma, wdv)] + ww, inc=[(mm, 1)])
                mmv2 = mm.now
                pg.emit("scalar", lambda e, pp=pp: e.activation(rl[:, :], pp, AF.Relu),
                        wait=[(mm, mmv2), (evs, evs.now)], drain=True, inc=[(evs, 1)])
                banks.release(jj, evs)
                pg.emit("scalar", lambda e, fh=fh, f2=f2: e.activation(
                    k2[:, fh * FC + f2, :], rl[:, :], AF.Square),
                    drain=True, inc=[(evs, 1)])
        k2_done = evs.now

        chunk_hist = []
        entry_mm = mm.now
        for ch in range(2):
            pslots = [c.pb01[:, 0:512], c.pb01[:, 512:1024], c.pb2, c.pb3]
            for kc2 in range(32):
                gidx = len(chunk_hist)
                wvb = wv2[gidx % 2]
                src_ap = c.inp["L1_attn_W_value2"].ap()[kc2 * P:(kc2 + 1) * P, :]
                wwv = [(mm, chunk_hist[gidx - 2])] if gidx >= 2 else [(mm, entry_mm)]
                pg.emit("gpsimd", lambda e, wvb=wvb, src_ap=src_ap: e.dma_start(
                    out=wvb[:, :], in_=src_ap), wait=wwv, inc=[(wdma, 16)])
                wdv2 = wdma.now
                def y2mm(e, ch=ch, kc2=kc2, wvb=wvb, pslots=pslots):
                    r = None
                    for q in range(4):
                        ct = ch * 4 + q
                        r = e.matmul(pslots[q], wvb[:, ct * P:(ct + 1) * P],
                                     k2[:, kc2, :], start=(kc2 == 0), stop=(kc2 == 31))
                    return r
                pg.emit("tensor", y2mm, wait=[(wdma, wdv2), (evs, k2_done)] +
                        ([(evv, evv.now)] if kc2 == 0 else []), inc=[(mm, 1)])
                chunk_hist.append(mm.now)
            mmv3 = mm.now
            for q in range(4):
                ct = ch * 4 + q
                pg.emit("vector", lambda e, ct=ct, q=q, pslots=pslots: e.tensor_tensor(
                    t5[:, :], pslots[q], r2[:, ct, :], ALU.mult),
                    wait=[(mm, mmv3)], drain=True)
                pg.emit("vector", lambda e, ct=ct: e.tensor_tensor(
                    xat[:, ct, :], xat[:, ct, :], t5[:, :], ALU.add), drain=True,
                    inc=[(evv, 1)])
                banks.free[q] = (evv, evv.now)
            banks.i = 0
        pg.emit("gpsimd", lambda e, t0=t0: e.dma_start(
            out=xa_hbm[:, :, t0:t0 + 512].rearrange("cc p t -> p cc t"),
            in_=xat[:, :, :]), wait=[(evv, evv.now), (odma, odma.now)],
            inc=[(odma, 16)])

    c.last_done = [(odma, odma.now), (evv, evv.now), (evs, evs.now), (mm, mm.now)]


def emit_B4(c):
    nc, pg = c.nc, c.pg
    c.arena_reset()
    wgc = [c.arena_alloc(f"b4_wgc{i}", [P, C], BF16) for i in range(2)]
    wdc = c.arena_alloc("b4_wdc", [P, HID], BF16)

    ld = pg.sem("b4_ld")
    mm = pg.sem("b4_mm")
    evv = pg.sem("b4_evv")
    evs = pg.sem("b4_evs")
    odma = pg.sem("b4_odma")
    wdma = pg.sem("b4_wdma")

    idb = c.res["ident_bf"]
    cval = c.res["cval"]
    xn_bf = c.res["xn_bf"]
    vecs = c.res["vecs"]
    xa_hbm = c.res["xa_hbm"].ap()
    x1_hbm = c.res["x1_hbm"].ap()
    y_ap = c.res["y"].ap()
    wgT = c.res["wgT_hbm"].ap()
    wuT = c.res["wuT_hbm"].ap()
    wdT = c.res["wdT_hbm"].ap()
    prev = list(c.last_done)

    tbg = [c.arena_alloc(f"b4_tbg{i}", [P, HC, P], BF16) for i in range(2)]
    tp_hist = []
    grp = [0]
    grp_dma = []

    def tp_group(n_blocks, src_fn, dst_ap_fn, wait_first):
        gi = grp[0]
        grp[0] += 1
        buf = tbg[gi % 2]
        wgrp = list(wait_first)
        if gi >= 2:
            wgrp.append((odma, grp_dma[gi - 2]))
        for b in range(n_blocks):
            i = len(tp_hist)
            ps = c.psum_tp if i % 2 == 0 else c.psum_tp2
            w = list(wgrp) if b == 0 else []
            if i >= 2:
                w.append((evv, tp_hist[i - 2]))
            pg.emit("tensor", lambda e, ps=ps, b=b, src_fn=src_fn: e.transpose(
                ps[:, 0, :], src_fn(b), idb[:, :]), wait=w, inc=[(mm, 1)])
            pg.emit("vector", lambda e, ps=ps, b=b, buf=buf: e.tensor_copy(
                buf[:, b, :], ps[:, 0, :]), wait=[(mm, mm.now)], drain=True,
                inc=[(evv, 1)])
            tp_hist.append(evv.now)
        pg.emit("sync", lambda e, buf=buf, n_blocks=n_blocks, dst_ap_fn=dst_ap_fn:
                e.dma_start(out=dst_ap_fn(), in_=buf[:, 0:n_blocks, :]),
                wait=[(evv, evv.now)], inc=[(odma, 16)])
        grp_dma.append(odma.now)

    ld_hist = []
    for wi, (wname, dstT) in enumerate((("L1_ffn_w_gate", wgT), ("L1_ffn_w_up", wuT))):
        for hc in range(HC):
            gidx = wi * HC + hc
            wgb = wgc[gidx % 2]
            src_ap = c.inp[wname].ap()[hc * P:(hc + 1) * P, :]
            w0 = list(prev) if gidx == 0 else []
            if gidx >= 2:
                w0.append((mm, ld_hist[gidx - 2]))
            pg.emit("gpsimd", lambda e, wgb=wgb, src_ap=src_ap: e.dma_start(
                out=wgb[:, :], in_=src_ap), wait=w0, inc=[(ld, 16)])
            ldv = ld.now
            tp_group(CC, lambda b, wgb=wgb: wgb[:, b * P:(b + 1) * P],
                     lambda dstT=dstT, hc=hc: dstT[:, hc * P:(hc + 1) * P]
                     .rearrange("(dk p) h -> p dk h", p=P),
                     [(ld, ldv)])
            ld_hist.append(mm.now)
    for dk in range(CC):
        gidx = 2 * HC + dk
        src_ap = c.inp["L1_ffn_w_down"].ap()[dk * P:(dk + 1) * P, :]
        w0 = [(mm, ld_hist[gidx - 1])]
        pg.emit("gpsimd", lambda e, src_ap=src_ap: e.dma_start(
            out=wdc[:, :], in_=src_ap), wait=w0, inc=[(ld, 16)])
        ldv = ld.now
        tp_group(HC, lambda b: wdc[:, b * P:(b + 1) * P],
                 lambda dk=dk: wdT[:, dk * P:(dk + 1) * P]
                 .rearrange("(hk p) d -> p hk d", p=P),
                 [(ld, ldv)])
        ld_hist.append(mm.now)
    b4a_done = odma.now

    c.arena_reset()
    xat = c.arena_alloc("b4_xat", [P, CC, 512], F32)
    x1t = c.arena_alloc("b4_x1t", [P, CC, 512], F32)
    hdn = c.arena_alloc("b4_hdn", [P, HC, 512], BF16)
    wgTc = [c.arena_alloc(f"b4_wgT{i}", [P, CC, P], BF16) for i in range(2)]
    wuTc = [c.arena_alloc(f"b4_wuT{i}", [P, CC, P], BF16) for i in range(2)]
    wdTc = [c.arena_alloc(f"b4_wdT{i}", [P, HC, P], BF16) for i in range(2)]
    sgs = c.arena_alloc("b4_sgs", [P, 512], BF16)
    sd = c.arena_alloc("b4_sd", [P, 512], F32)
    t6 = c.arena_alloc("b4_t6", [P, 512], F32)
    ob = c.arena_alloc("b4_ob", [P, CC, 512], F32)
    otm = c.arena_alloc("b4_otm", [P, 4, C], F32)

    banks = Banks([c.pb01[:, 0:512], c.pb01[:, 512:1024], c.pb2, c.pb3])
    iv = CVAL["inv127"]
    c.b4_hist = []
    c.b4_entry = mm.now
    c.b4d_hist = []

    for ti in range(4):
        t0 = ti * 512
        w0 = [(odma, b4a_done)] if ti == 0 else [(evv, evv.now)]
        pg.emit("sync", lambda e, t0=t0: e.dma_start(
            out=xat[:, :, :],
            in_=xa_hbm[:, :, t0:t0 + 512].rearrange("cc p t -> p cc t")),
            wait=w0, inc=[(ld, 16)])
        pg.emit("sync", lambda e, t0=t0: e.dma_start(
            out=x1t[:, :, :],
            in_=x1_hbm[:, :, t0:t0 + 512].rearrange("cc p t -> p cc t")),
            wait=[], inc=[(ld, 16)])
        ldv = ld.now
        for hc in range(HC):
            wga = wgTc[hc % 2]
            wua = wuTc[hc % 2]
            gidx = ti * HC + hc
            if gidx >= 2:
                wwar = [(mm, c.b4_hist[gidx - 2])]
            else:
                wwar = [(mm, c.b4_entry)]
            pg.emit("gpsimd", lambda e, wga=wga, hc=hc: e.dma_start(
                out=wga[:, :, :],
                in_=wgT[:, hc * P:(hc + 1) * P].rearrange("(dk p) h -> p dk h", p=P)),
                wait=[(odma, b4a_done)] + wwar, inc=[(wdma, 16)])
            pg.emit("gpsimd", lambda e, wua=wua, hc=hc: e.dma_start(
                out=wua[:, :, :],
                in_=wuT[:, hc * P:(hc + 1) * P].rearrange("(dk p) h -> p dk h", p=P)),
                wait=[], inc=[(wdma, 16)])
            wdv = wdma.now
            jg, pG, wG = banks.acquire()
            ju, pU, wU = banks.acquire()
            def gatemm(e, wga=wga, t0=t0, pG=pG):
                r = None
                for kc in range(CC):
                    r = e.matmul(pG, wga[:, kc, :], xn_bf[:, kc, t0:t0 + 512],
                                 start=(kc == 0), stop=(kc == CC - 1))
                return r
            def upmm(e, wua=wua, t0=t0, pU=pU):
                r = None
                for kc in range(CC):
                    r = e.matmul(pU, wua[:, kc, :], xn_bf[:, kc, t0:t0 + 512],
                                 start=(kc == 0), stop=(kc == CC - 1))
                return r
            pg.emit("tensor", gatemm, wait=[(wdma, wdv), (ld, ldv)] + wG + wU,
                    inc=[(mm, 1)])
            pg.emit("tensor", upmm, wait=[], inc=[(mm, 1)])
            mmv = mm.now
            c.b4_hist.append(mm.now)
            pg.emit("scalar", lambda e, pG=pG: e.activation(
                sgs[:, :], pG, AF.Silu, scale=cval[:, iv:iv + 1]),
                wait=[(mm, mmv), (evv, evv.now)], drain=True, inc=[(evs, 1)])
            banks.release(jg, evs)
            pg.emit("vector", lambda e, hc=hc, pU=pU: e.scalar_tensor_tensor(
                hdn[:, hc, :], pU, cval[:, iv:iv + 1], sgs[:, :],
                ALU.mult, ALU.mult),
                wait=[(evs, evs.now)], drain=True, inc=[(evv, 1)])
            banks.release(ju, evv)
        hdn_done = evv.now

        for ct in range(CC):
            wda = wdTc[ct % 2]
            gidx2 = ti * CC + ct
            if gidx2 >= 2:
                wwar = [(mm, c.b4d_hist[gidx2 - 2])]
            else:
                wwar = [(mm, c.b4_entry)]
            pg.emit("gpsimd", lambda e, wda=wda, ct=ct: e.dma_start(
                out=wda[:, :, :],
                in_=wdT[:, ct * P:(ct + 1) * P].rearrange("(hk p) d -> p hk d", p=P)),
                wait=wwar, inc=[(wdma, 16)])
            wdv = wdma.now
            jd, pD, wD = banks.acquire()
            def downmm(e, wda=wda, pD=pD):
                r = None
                for hk in range(HC):
                    r = e.matmul(pD, wda[:, hk, :], hdn[:, hk, :],
                                 start=(hk == 0), stop=(hk == HC - 1))
                return r
            pg.emit("tensor", downmm, wait=[(wdma, wdv), (evv, hdn_done)] + wD,
                    inc=[(mm, 1)])
            mmv = mm.now
            c.b4d_hist.append(mm.now)
            pg.emit("scalar", lambda e, pD=pD: e.activation(
                sd[:, :], pD, AF.Silu, scale=cval[:, iv:iv + 1]),
                wait=[(mm, mmv), (evv, evv.now)], drain=True, inc=[(evs, 1)])
            banks.release(jd, evs)
            pg.emit("vector", lambda e, ct=ct: e.tensor_tensor(
                t6[:, :], sd[:, :], xat[:, ct, :], ALU.mult),
                wait=[(evs, evs.now)], drain=True)
            pg.emit("vector", lambda e, ct=ct: e.tensor_tensor(
                ob[:, ct, :], t6[:, :], x1t[:, ct, :], ALU.add), drain=True,
                inc=[(evv, 1)])
        ob_done = evv.now

        for j in range(4):
            for ct in range(CC):
                wtp = [(evv, evv.now)]
                if ct == 0:
                    wtp.append((odma, odma.now))
                pg.emit("tensor", lambda e, j=j, ct=ct: e.transpose(
                    c.pbu[:, 0:P], ob[:, ct, j * P:(j + 1) * P],
                    c.res["ident_f"][:, :]),
                    wait=wtp, inc=[(mm, 1)])
                pg.emit("vector", lambda e, j=j, ct=ct: e.tensor_copy(
                    otm[:, j, ct * P:(ct + 1) * P], c.pbu[:, 0:P]),
                    wait=[(mm, mm.now)], drain=True, inc=[(evv, 1)])
            pg.emit("sync", lambda e, j=j, t0=t0: e.dma_start(
                out=y_ap[t0 + j * P: t0 + (j + 1) * P, :], in_=otm[:, j, :]),
                wait=[(evv, evv.now)], inc=[(odma, 16)])

    c.last_done = [(odma, odma.now), (evv, evv.now), (evs, evs.now)]
    c.final_waits.append((odma, odma.now))



def kernel(**inputs):
    nc, c = build_program()
    in_maps = []
    x = np.ascontiguousarray(inputs["L0_attn_x"])
    for b in range(8):
        m = {"x": np.ascontiguousarray(x[b])}
        for k, v in inputs.items():
            if k == "L0_attn_x":
                continue
            m[k] = np.ascontiguousarray(v)
        in_maps.append(m)
    res = run_bass_kernel_spmd(nc, in_maps, core_ids=list(range(8)))
    return np.stack([res.results[b]["y"] for b in range(8)], axis=0)
